# version 07
# baseline (speedup 1.0000x reference)
import numpy as np
import ml_dtypes
from contextlib import ExitStack

import concourse.bass as bass
import concourse.mybir as mybir
from concourse.bass_utils import run_bass_kernel_spmd

F32 = mybir.dt.float32
BF16 = mybir.dt.bfloat16
AX = mybir.AxisListType
OP = mybir.AluOpType
AF = mybir.ActivationFunctionType

TAPS = None

MAGIC = float(3 * 2.0 ** 22)
PI = float(np.pi)
NEG = -1e30

T = 2048
NB = 2
NCH = T // 128
DM = 1024
DI = 2048
NH = 32
HP = 64
NS = 128
NA = 32
DIN = 4480
FI = 4096


class B:

    def tap(self, name, apv, bi=None, ci=None):
        if TAPS is None:
            return
        if bi is not None and (bi, ci) != (TAPS.get("bi", 0), TAPS.get("ci", 0)):
            return
        sh = list(apv.shape)
        dt = apv.dtype
        t = self.nc.dram_tensor("tap_" + name, sh, dt, kind="ExternalOutput").ap()

        @self.D
        def _(sync):
            for nm in ("sv", "ss", "st", "sg"):
                self.wait(sync, nm)
            self.inc(sync.dma_start(t, apv), "sd", 16)

    def __init__(self, nc, stack):
        self.nc = nc
        self.sem = {}
        self.cnt = {}
        for name in ("sd", "st", "sv", "ss", "sg"):
            self.sem[name] = stack.enter_context(nc.semaphore(name))
            self.cnt[name] = 0
        self.blk = None

    def inc(self, instr, name, k=1):
        instr.then_inc(self.sem[name], k)
        self.cnt[name] += k
        return self.cnt[name]

    def wait(self, eng, name, val=None):
        v = self.cnt[name] if val is None else val
        if v > 0:
            eng.wait_ge(self.sem[name], v)

    def T(self, f):
        self.blk.tensor(f)

    def V(self, f):
        self.blk.vector(f)

    def S(self, f):
        self.blk.scalar(f)

    def G(self, f):
        self.blk.gpsimd(f)

    def D(self, f):
        self.blk.sync(f)


def build(nc):
    stack = ExitStack()
    b = B(nc, stack)
    ap = lambda t: t.ap()

    d = {}
    d["u_fm"] = ap(nc.dram_tensor("u_fm", [NB, DM, T], F32, kind="ExternalInput"))
    d["u_tm"] = ap(nc.dram_tensor("u_tm", [NB, T, DM], F32, kind="ExternalInput"))
    d["w_inT"] = ap(nc.dram_tensor("w_inT", [DM, DIN], F32, kind="ExternalInput"))
    d["w_outT"] = ap(nc.dram_tensor("w_outT", [DI, DM], F32, kind="ExternalInput"))
    d["w_gT"] = ap(nc.dram_tensor("w_gT", [DM, FI], F32, kind="ExternalInput"))
    d["w_uT"] = ap(nc.dram_tensor("w_uT", [DM, FI], F32, kind="ExternalInput"))
    d["w_dT"] = ap(nc.dram_tensor("w_dT", [FI, DM], F32, kind="ExternalInput"))
    d["dtb_rep"] = ap(nc.dram_tensor("dtb_rep", [128, NH], F32, kind="ExternalInput"))
    d["bbias_row"] = ap(nc.dram_tensor("bbias_row", [1, NH * NS], F32, kind="ExternalInput"))
    d["cbias_row"] = ap(nc.dram_tensor("cbias_row", [1, NH * NS], F32, kind="ExternalInput"))
    d["bnw_rep"] = ap(nc.dram_tensor("bnw_rep", [128, NS], F32, kind="ExternalInput"))
    d["cnw_rep"] = ap(nc.dram_tensor("cnw_rep", [128, NS], F32, kind="ExternalInput"))
    d["d_rep"] = ap(nc.dram_tensor("d_rep", [128, NH], F32, kind="ExternalInput"))
    for cn in ("c_L", "c_shift", "c_mask", "c_ones", "c_e0", "c_idf32"):
        d[cn] = ap(nc.dram_tensor(cn, [128, 128], F32, kind="ExternalInput"))
    d["c_idbf"] = ap(nc.dram_tensor("c_idbf", [128, 128], BF16, kind="ExternalInput"))
    d["c_cols"] = ap(nc.dram_tensor("c_cols", [128, 4], F32, kind="ExternalInput"))
    d["c_colsbf"] = ap(nc.dram_tensor("c_colsbf", [128, 2], BF16, kind="ExternalInput"))
    d["out"] = ap(nc.dram_tensor("out", [NB, T, DM], F32, kind="ExternalOutput"))
    d["sc_xn"] = ap(nc.dram_tensor("sc_xn", [NB, NCH, 128, DM], BF16, kind="Internal"))
    d["sc_s"] = ap(nc.dram_tensor("sc_s", [NB, NCH, 128, DM], F32, kind="Internal"))
    d["sc_gu"] = ap(nc.dram_tensor("sc_gu", [NB, NCH, 128, FI], BF16, kind="Internal"))
    d["sc_wqg"] = ap(nc.dram_tensor("sc_wqg", [8, 128, FI], BF16, kind="Internal"))
    d["sc_wqu"] = ap(nc.dram_tensor("sc_wqu", [8, 128, FI], BF16, kind="Internal"))
    d["sc_wqd"] = ap(nc.dram_tensor("sc_wqd", [32, 128, DM], BF16, kind="Internal"))

    with stack:
        c = {}
        for cn in ("c_L", "c_shift", "c_mask", "c_ones", "c_e0", "c_idf32"):
            c[cn] = stack.enter_context(nc.sbuf_tensor("sb" + cn, [128, 128], F32))
        c["c_idbf"] = stack.enter_context(nc.sbuf_tensor("sbc_idbf", [128, 128], BF16))
        c["c_cols"] = stack.enter_context(nc.sbuf_tensor("sbc_cols", [128, 4], F32))
        c["c_colsbf"] = stack.enter_context(nc.sbuf_tensor("sbc_colsbf", [128, 2], BF16))
        c["dtb"] = stack.enter_context(nc.sbuf_tensor("dtb", [128, NH], F32))
        c["bnw"] = stack.enter_context(nc.sbuf_tensor("bnw", [128, NS], F32))
        c["cnw"] = stack.enter_context(nc.sbuf_tensor("cnw", [128, NS], F32))
        c["drep"] = stack.enter_context(nc.sbuf_tensor("drep", [128, NH], F32))
        c["wsc"] = stack.enter_context(nc.sbuf_tensor("wsc", [128, 8], F32))
        c["gumax"] = stack.enter_context(nc.sbuf_tensor("gumax", [128, NB * NCH], F32))
        PB = [stack.enter_context(nc.psum_tensor(f"pb{i}", [128, 512], F32))
              for i in range(8)]
        nc.const_aps.register(F32, 1.0, c["c_cols"][:, 0:1])
        nc.const_aps.register(F32, 1e-5, c["c_cols"][:, 2:3])
        nc.const_aps.register(F32, 0.0, c["c_cols"][:, 3:4])

        with nc.Block("init") as ib:
            @ib.sync
            def _(sync):
                for k in b.sem:
                    sync.sem_clear(b.sem[k])

        with ExitStack() as asc:
            a = {}

            def asb(n, sh, dt=F32):
                a[n] = asc.enter_context(nc.sbuf_tensor(n, sh, dt))
                return a[n]

            asb("winT", [128, 8, DIN], BF16)
            asb("woutT", [128, 16, DM], BF16)
            asb("bias_b", [128, NH * NS], BF16)
            asb("bias_c", [128, NH * NS], BF16)
            asb("Hst", [128, NH * HP], BF16)
            asb("carry", [1, NH * NA], F32)
            asb("acol", [128, 2], F32)

            with nc.Block("main_a") as blk:
                b.blk = blk
                with ExitStack() as pro:
                    a["wtmp"] = pro.enter_context(nc.sbuf_tensor("wtmp", [128, DIN], F32))
                    a["wqtmp"] = pro.enter_context(nc.sbuf_tensor("wqtmp", [128, FI], BF16))
                    a["row2"] = pro.enter_context(nc.sbuf_tensor("row2", [1, 2], F32))
                    _prologue(b, d, c, a, PB)
                del a["wtmp"], a["wqtmp"], a["row2"]
                asb("stat", [1, 256], F32)
                asb("ub32", [128, 8, 128], F32)
                asb("xnb", [128, 8, 128], BF16)
                asb("x_tm", [128, NH, HP], BF16)
                asb("xqh", [128, NH, HP], BF16)
                asb("xmaskE", [128, NH, HP], BF16)
                asb("zs", [128, DI], BF16)
                asb("sml", [128, 384], F32)
                asb("smt", [128, 128], F32)
                for n in ("dt_", "adt", "lam", "dtlam", "dt1ml", "wvec", "qw",
                          "alpha", "qh_e", "decay", "cs_tm", "dtlamD"):
                    asb(n, [128, NH], F32)
                asb("Gbuf", [128, NH, NA], F32)
                asb("sin_t", [128, NH, NA], BF16)
                asb("cos_t", [128, NH, NA], BF16)
                asb("bh_all", [128, 2, NH, NS], BF16)
                asb("ch_all", [128, NH, NS], BF16)
                asb("bh_sm", [128, 2, 8, 128], BF16)
                asb("ch_sm", [128, 2, 8, 128], BF16)
                asb("Ebuf", [128, 8, 128], F32)
                asb("dgbuf", [128, 8, 128], BF16)
                asb("PSb", [128, 8, 128], BF16)
                asb("ybar_g", [128, 512], BF16)
                asb("ybar_sm", [128, 4, 128], BF16)
                for bi in range(NB):
                    _batch_reset(b, a)
                    for ci in range(NCH):
                        _chunk_a(b, d, c, a, PB, bi, ci)

        with ExitStack() as bsc:
            p = {}

            def bsb(n, sh, dt=F32):
                p[n] = bsc.enter_context(nc.sbuf_tensor(n, sh, dt))
                return p[n]

            bsb("wgq", [128, 8, FI], BF16)
            bsb("wuq", [128, 8, FI], BF16)
            bsb("xnf", [128, 2, 8, 128], BF16)
            bsb("xq_fm", [128, 8, 128], BF16)
            bsb("m8", [128, 128], F32)
            bsb("mt", [128, 128], F32)
            bsb("srow", [1, 128], F32)
            bsb("scol", [128, 4], F32)
            bsb("gu_b", [128, FI], BF16)
            with nc.Block("main_b1") as blk:
                b.blk = blk

                @b.D
                def _(sync):
                    b.wait(sync, "sv")
                    b.wait(sync, "ss")
                    b.wait(sync, "sd")
                    for k in range(8):
                        b.inc(sync.dma_start(p["wgq"][:, k, :], d["sc_wqg"][k]), "sd", 16)
                        b.inc(sync.dma_start(p["wuq"][:, k, :], d["sc_wqu"][k]), "sd", 16)
                for bi in range(NB):
                    for ci in range(NCH):
                        _chunk_b1(b, d, c, p, PB, bi, ci)

        with ExitStack() as bsc2:
            q = {}

            def qsb(n, sh, dt=F32):
                q[n] = bsc2.enter_context(nc.sbuf_tensor(n, sh, dt))
                return q[n]

            qsb("wdq", [128, 32, DM], BF16)
            qsb("gu_tm", [128, 2, FI], BF16)
            qsb("guq_tm", [128, FI], BF16)
            qsb("qtmp", [128, FI], F32)
            qsb("guq_fm", [128, 32, 128], BF16)
            qsb("u_b", [128, 2, DM], F32)
            qsb("s_b", [128, 2, DM], F32)
            qsb("g_b", [128, DM], F32)
            qsb("o_b", [128, DM], F32)
            qsb("scol2", [128, 4], F32)
            with nc.Block("main_b2") as blk:
                b.blk = blk

                @b.D
                def _(sync):
                    b.wait(sync, "sv")
                    b.wait(sync, "ss")
                    b.wait(sync, "sd")
                    for k in range(32):
                        b.inc(sync.dma_start(q["wdq"][:, k, :], d["sc_wqd"][k]), "sd", 16)
                for bi in range(NB):
                    for ci in range(NCH):
                        _chunk_b2(b, d, c, q, PB, bi, ci)

            with nc.Block("fin") as blk:
                @blk.sync
                def _(sync):
                    b.wait(sync, "sd")
    return nc


def _prologue(b, d, c, a, PB):
    @b.D
    def _(sync):
        for dst, key in [(c["c_L"], "c_L"), (c["c_shift"], "c_shift"),
                         (c["c_mask"], "c_mask"), (c["c_ones"], "c_ones"),
                         (c["c_e0"], "c_e0"), (c["c_idf32"], "c_idf32"),
                         (c["c_idbf"], "c_idbf"), (c["c_cols"], "c_cols"),
                         (c["c_colsbf"], "c_colsbf"), (c["dtb"], "dtb_rep"),
                         (c["bnw"], "bnw_rep"), (c["cnw"], "cnw_rep"),
                         (c["drep"], "d_rep")]:
            b.inc(sync.dma_start(dst[:], d[key]), "sd", 16)
        b.inc(sync.dma_start(a["wtmp"][0:1, 0:NH * NS], d["bbias_row"]), "sd", 16)

    for key, dstname in [("cbias_row", "bias_b"), (None, "bias_c")]:
        for j in range(8):
            @b.T
            def _(tensor, j=j):
                b.wait(tensor, "sd")
                b.wait(tensor, "sv")
                tensor.matmul(PB[j % 2][:], c["c_ones"][0:1, :],
                              a["wtmp"][0:1, j * 512:(j + 1) * 512],
                              start=True, stop=True)
                b.inc(tensor.drain(), "st")

            @b.V
            def _(vector, dstname=dstname, j=j):
                b.wait(vector, "st")
                vector.tensor_copy(a[dstname][:, j * 512:(j + 1) * 512], PB[j % 2][:])
                b.inc(vector.drain(), "sv")

        if key is not None:
            @b.D
            def _(sync, key=key):
                b.wait(sync, "st")
                b.inc(sync.dma_start(a["wtmp"][0:1, 0:NH * NS], d[key]), "sd", 16)

    for k in range(8):
        @b.D
        def _(sync, k=k):
            b.wait(sync, "sv")
            b.inc(sync.dma_start(a["wtmp"][:], d["w_inT"][k * 128:(k + 1) * 128, :]),
                  "sd", 16)

        @b.V
        def _(vector, k=k):
            b.wait(vector, "sd")
            vector.tensor_copy(a["winT"][:, k, :], a["wtmp"][:])
            b.inc(vector.drain(), "sv")
    for k in range(16):
        @b.D
        def _(sync, k=k):
            b.wait(sync, "sv")
            b.inc(sync.dma_start(a["wtmp"][:, 0:DM],
                                 d["w_outT"][k * 128:(k + 1) * 128, :]), "sd", 16)

        @b.V
        def _(vector, k=k):
            b.wait(vector, "sd")
            vector.tensor_copy(a["woutT"][:, k, :], a["wtmp"][:, 0:DM])
            b.inc(vector.drain(), "sv")

    for wi, (dkey, skey, nkw, wid) in enumerate(
            [("w_gT", "sc_wqg", 8, FI), ("w_uT", "sc_wqu", 8, FI),
             ("w_dT", "sc_wqd", 32, DM)]):
        @b.V
        def _(vector):
            vector.memset(a["acol"][:, 0:1], 0.0)
            b.inc(vector.drain(), "sv")
        for k in range(nkw):
            @b.D
            def _(sync, k=k, dkey=dkey, wid=wid):
                b.wait(sync, "sv")
                b.inc(sync.dma_start(a["wtmp"][:, 0:wid],
                                     d[dkey][k * 128:(k + 1) * 128, :]), "sd", 16)

            @b.V
            def _(vector, wid=wid):
                b.wait(vector, "sd")
                vector.tensor_reduce(a["acol"][:, 1:2], a["wtmp"][:, 0:wid], AX.X,
                                     OP.add, apply_absolute_value=True)
                vector.drain()
                vector.tensor_tensor(a["acol"][:, 0:1], a["acol"][:, 0:1],
                                     a["acol"][:, 1:2], OP.add)
                b.inc(vector.drain(), "sv")

        @b.T
        def _(tensor):
            b.wait(tensor, "sv")
            tensor.matmul(PB[2][0:1, 0:1], c["c_cols"][:, 0:1], a["acol"][:, 0:1],
                          start=True, stop=True)
            b.inc(tensor.drain(), "st")

        @b.V
        def _(vector, nkw=nkw, wid=wid):
            b.wait(vector, "st")
            n_el = float(nkw * 128 * wid)
            vector.tensor_scalar(a["row2"][0:1, 0:1], PB[2][0:1, 0:1], 1.0 / n_el,
                                 1e-5, OP.mult, OP.max)
            vector.drain()
            vector.reciprocal(a["row2"][0:1, 1:2], a["row2"][0:1, 0:1])
            b.inc(vector.drain(), "sv")

        @b.T
        def _(tensor):
            b.wait(tensor, "sv")
            tensor.matmul(PB[2][:, 0:2], c["c_ones"][0:1, :], a["row2"][0:1, 0:2],
                          start=True, stop=True)
            b.inc(tensor.drain(), "st")

        @b.V
        def _(vector, wi=wi):
            b.wait(vector, "st")
            vector.tensor_copy(c["wsc"][:, 3 + wi:4 + wi], PB[2][:, 0:1])
            vector.tensor_copy(c["wsc"][:, wi:wi + 1], PB[2][:, 1:2])
            b.inc(vector.drain(), "sv")

        for k in range(nkw):
            @b.D
            def _(sync, k=k, dkey=dkey, wid=wid):
                b.wait(sync, "sv")
                b.inc(sync.dma_start(a["wtmp"][:, 0:wid],
                                     d[dkey][k * 128:(k + 1) * 128, :]), "sd", 16)

            @b.V
            def _(vector, wi=wi, wid=wid):
                b.wait(vector, "sd")
                vector.tensor_scalar(a["wtmp"][:, 0:wid], a["wtmp"][:, 0:wid],
                                     c["wsc"][:, wi:wi + 1], MAGIC, OP.mult, OP.add)
                vector.tensor_scalar(a["wtmp"][:, 0:wid], a["wtmp"][:, 0:wid],
                                     MAGIC, None, OP.subtract)
                vector.tensor_scalar(a["wqtmp"][:, 0:wid], a["wtmp"][:, 0:wid],
                                     -1.0, 1.0, OP.max, OP.min)
                b.inc(vector.drain(), "sv")

            @b.D
            def _(sync, skey=skey, k=k, wid=wid):
                b.wait(sync, "sv")
                b.inc(sync.dma_start(d[skey][k, :, :], a["wqtmp"][:, 0:wid]), "sd", 16)


def _batch_reset(b, a):
    @b.V
    def _(vector):
        b.wait(vector, "st")
        vector.memset(a["Hst"][:], 0.0)
        vector.memset(a["carry"][:], 0.0)
        b.inc(vector.drain(), "sv")


def _chunk_a(b, d, c, a, PB, bi, ci):
    cur = ci % 2
    prev = 1 - cur
    onesbf = c["c_colsbf"][:, 0:1]
    e127 = c["c_cols"][:, 1:2]
    bh_cur = a["bh_all"][:, cur]

    mk = {}

    def plan(ch, names, step=1):
        v = b.cnt[ch]
        for n in names:
            v += step
            mk[n] = v

    vnames = (["vLN1", "vLN2", "vXN"] + [f"vIP{s}" for s in range(4)] +
              ["vSM1", "vSM2", "vCS1", "vQW", "vXM", "vG", "vCAR0", "vRS0",
               "vRC0", "vCAR1", "vRS1", "vRC1", "vRMB", "vBB", "vRMC", "vCB"])
    vnames += ["vTRE0"]
    for g in range(4):
        vnames += [f"vE{g}", f"vPS{g}"]
        if g < 3:
            vnames += [f"vTRE{g + 1}"]
        if ci > 0:
            vnames += [f"vHB{g}"]
        vnames += [f"vYEV{g}", f"vYSM{g}"]
    vnames += ["vSEV"]
    plan("sv", vnames)
    plan("sg", [x for g in range(4) for x in (f"gCSB{g}", f"gDG{g}")])
    tnames = ["tLN", "tBC"] + [f"tIP{s}" for s in range(9)] + ["tCS", "tTH0", "tTH1"]
    tnames += ["tTR0", "tSC0", "tTR1"]
    for g in range(4):
        if ci > 0:
            tnames += [f"tBD{g}"]
        tnames += [f"tY{g}"]
        if g < 3:
            tnames += [f"tSC{g + 1}"]
        tnames += [f"tYT{g}"]
        if g < 2:
            tnames += [f"tTR{g + 2}"]
    tnames += ["tOP"]
    plan("st", tnames)
    snames = (["sLN0", "sLN"] + [f"sIP{s}" for s in range(4, 9)] +
              ["sSM", "sCS", "sQH", "sS0", "sC0", "sS1", "sC1", "sZS",
               "sRB", "sRC"] + [f"sE{g}" for g in range(4)])
    plan("ss", snames)
    plan("sd", ["dU", "dXN", "dS"], step=16)
    prev_mk = getattr(b, "prev_a_marks", {})

    def W(eng, ch, name):
        if name in mk:
            eng.wait_ge(b.sem[ch], mk[name])
        elif name in prev_mk:
            eng.wait_ge(b.sem[ch], prev_mk[name])

    @b.D
    def _(sync):
        b.wait(sync, "sd", mk["dU"] - 16)
        src = d["u_fm"][bi].rearrange("(k p) t -> p k t", p=128)[
            :, :, ci * 128:(ci + 1) * 128]
        sync.dma_start(a["ub32"][:], src).then_inc(b.sem["sd"], 16)
        sync.wait_ge(b.sem["sv"], mk["vXN"])
        sync.dma_start(d["sc_xn"][bi, ci],
                       a["xnb"][:].rearrange("p k t -> p (k t)")
                       ).then_inc(b.sem["sd"], 16)
        sync.wait_ge(b.sem["sv"], mk["vSEV"])
        sync.dma_start(d["sc_s"][bi, ci],
                       a["ub32"][:].rearrange("p k t -> p (k t)")
                       ).then_inc(b.sem["sd"], 16)
    b.cnt["sd"] = mk["dS"]

    @b.T
    def _(tensor):
        s_ = b.sem
        tensor.wait_ge(s_["sd"], mk["dU"])
        tensor.wait_ge(s_["ss"], mk["sLN0"])
        for k in range(8):
            tensor.matmul(PB[4][0:1, 0:128], onesbf, a["PSb"][:, k, :],
                          start=(k == 0), stop=(k == 7))
        for k in range(8):
            tensor.matmul(PB[4][0:1, 128:256], onesbf, a["xnb"][:, k, :],
                          start=(k == 0), stop=(k == 7))
        tensor.drain().then_inc(s_["st"], 1)
        tensor.wait_ge(s_["ss"], mk["sLN"])
        tensor.matmul(PB[5][:, 0:128], c["c_ones"][0:1, :], a["stat"][0:1, 0:128],
                      start=True, stop=True)
        tensor.matmul(PB[5][:, 128:256], c["c_ones"][0:1, :],
                      a["stat"][0:1, 128:256], start=True, stop=True)
        tensor.drain().then_inc(s_["st"], 1)
        tensor.wait_ge(s_["sv"], mk["vXN"])
        for sl in range(9):
            ncol = 512 if sl < 8 else 384
            if 2 <= sl < 6:
                tensor.wait_ge(s_["sv"], mk[f"vIP{sl - 2}"])
            elif sl >= 6:
                tensor.wait_ge(s_["ss"], mk[f"sIP{sl - 2}"])
            for k in range(8):
                tensor.matmul(PB[sl % 2][:, 0:ncol], a["xnb"][:, k, :],
                              a["winT"][:, k, sl * 512:sl * 512 + ncol],
                              start=(k == 0), stop=(k == 7))
            tensor.drain().then_inc(s_["st"], 1)
        tensor.wait_ge(s_["sv"], mk["vSM2"])
        tensor.matmul(PB[6][:, 0:32], c["c_L"][:], a["adt"][:], start=True, stop=True)
        tensor.matmul(PB[6][:, 32:64], c["c_shift"][:], a["dt1ml"][:], start=True,
                      stop=True)
        tensor.matmul(PB[6][:, 64:96], c["c_ones"][:], a["adt"][:], start=True,
                      stop=True)
        tensor.matmul(PB[6][:, 96:128], c["c_e0"][:], a["dt1ml"][:], start=True,
                      stop=True)
        tensor.drain().then_inc(s_["st"], 1)
        Gflat = a["Gbuf"][:].rearrange("p h q -> p (h q)")
        tensor.wait_ge(s_["sv"], mk["vG"])
        for j in range(2):
            tensor.matmul(PB[2 + j][:], c["c_ones"][0:1, :],
                          a["carry"][0:1, j * 512:(j + 1) * 512], start=True,
                          stop=False)
            tensor.matmul(PB[2 + j][:], c["c_L"][:], Gflat[:, j * 512:(j + 1) * 512],
                          start=False, stop=True)
        tensor.matmul(PB[7][0:1, :], c["c_cols"][:, 0:1], Gflat[:, 0:512],
                      start=True, stop=True)
        tensor.drain().then_inc(s_["st"], 1)
        tensor.wait_ge(s_["sv"], mk["vCAR0"])
        tensor.matmul(PB[7][0:1, :], c["c_cols"][:, 0:1], Gflat[:, 512:1024],
                      start=True, stop=True)
        tensor.drain().then_inc(s_["st"], 1)
        pbf6 = PB[6][:].bitcast(BF16)
        pbf7 = PB[7][:].bitcast(BF16)

        def t_tr(g):
            hg = g * 8
            if g == 0:
                tensor.wait_ge(s_["sv"], mk["vCB"])
            elif g >= 2:
                tensor.wait_ge(s_["sv"], mk[f"vYSM{g - 2}"])
            for i in range(8):
                tensor.transpose(pbf6[:, i * 128:(i + 1) * 128], bh_cur[:, hg + i, :],
                                 c["c_idbf"][:])
            for i in range(8):
                tensor.transpose(pbf7[:, i * 128:(i + 1) * 128],
                                 a["ch_all"][:, hg + i, :], c["c_idbf"][:])
            tensor.drain().then_inc(s_["st"], 1)

        def t_sc(g):
            tensor.wait_ge(s_["sv"], mk[f"vTRE{g}"])
            tensor.wait_ge(s_["sg"], mk[f"gCSB{g}"])
            if g > 0:
                tensor.wait_ge(s_["sv"], mk[f"vYEV{g - 1}"])
            for i in range(8):
                tensor.matmul(PB[2 + i // 4][:, (i % 4) * 128:(i % 4) * 128 + 128],
                              a["bh_sm"][:, g % 2, i, :], a["ch_sm"][:, g % 2, i, :],
                              start=True, stop=True)
            for i in range(8):
                tensor.transpose(PB[4 + i // 4][:, (i % 4) * 128:(i % 4) * 128 + 128],
                                 a["Ebuf"][:, i, :], c["c_idf32"][:])
            tensor.drain().then_inc(s_["st"], 1)

        def t_y(g):
            hg = g * 8
            if ci > 0:
                tensor.wait_ge(s_["sv"], mk[f"vPS{g}"])
                for i in range(8):
                    tensor.matmul(PB[2][:, i * 64:(i + 1) * 64],
                                  a["bh_all"][:, prev, hg + i, :],
                                  a["xmaskE"][:, hg + i, :], start=True, stop=True)
                tensor.drain().then_inc(s_["st"], 1)
                tensor.wait_ge(s_["sv"], mk[f"vHB{g}"])
            else:
                tensor.wait_ge(s_["sv"], mk[f"vPS{g}"])
            for i in range(8):
                tensor.matmul(PB[4][:, i * 64:(i + 1) * 64], a["PSb"][:, i, :],
                              a["x_tm"][:, hg + i, :], start=True, stop=True)
            for i in range(8):
                tensor.matmul(PB[5][:, i * 64:(i + 1) * 64], a["ch_sm"][:, g % 2, i, :],
                              a["Hst"][:, (hg + i) * HP:(hg + i + 1) * HP],
                              start=True, stop=True)
            for i in range(8):
                tensor.matmul(PB[2][:, i * 64:(i + 1) * 64], bh_cur[:, hg + i, :],
                              a["xqh"][:, hg + i, :], start=True, stop=True)
            tensor.drain().then_inc(s_["st"], 1)

        def t_yt(g):
            tensor.wait_ge(s_["sv"], mk[f"vYEV{g}"])
            for j in range(4):
                tensor.transpose(pbf6[:, j * 128:(j + 1) * 128],
                                 a["ybar_g"][:, j * 128:(j + 1) * 128], c["c_idbf"][:])
            tensor.drain().then_inc(s_["st"], 1)

        def t_op(g):
            tensor.wait_ge(s_["sv"], mk[f"vYSM{g}"])
            for j in range(4):
                kin = g * 4 + j
                for sx in range(2):
                    tensor.matmul(PB[sx][:, 0:512], a["ybar_sm"][:, j, :],
                                  a["woutT"][:, kin, sx * 512:(sx + 1) * 512],
                                  start=(kin == 0), stop=(kin == 15))

        t_tr(0)
        t_sc(0)
        t_tr(1)
        t_y(0)
        t_sc(1)
        t_yt(0)
        t_tr(2)
        t_y(1)
        t_op(0)
        t_sc(2)
        t_yt(1)
        t_tr(3)
        t_y(2)
        t_op(1)
        t_sc(3)
        t_yt(2)
        t_y(3)
        t_op(2)
        t_yt(3)
        t_op(3)
        tensor.drain().then_inc(s_["st"], 1)
    b.cnt["st"] = mk["tOP"]

    @b.V
    def _(vector):
        s_ = b.sem
        vector.wait_ge(s_["sd"], mk["dU"])
        W(vector, "st", "p_tY3")
        W(vector, "ss", "p_sE3")
        vector.drain().then_inc(s_["sv"], 1)
        vector.wait_ge(s_["st"], mk["tLN"])
        vector.tensor_scalar(a["stat"][0:1, 0:128], PB[4][0:1, 0:128], 1.0 / DM,
                             None, OP.mult)
        vector.tensor_scalar(a["stat"][0:1, 128:256], PB[4][0:1, 128:256], 1.0 / DM,
                             None, OP.mult)
        vector.drain()
        vector.tensor_tensor(a["smt"][0:1, 0:128], a["stat"][0:1, 0:128],
                             a["stat"][0:1, 0:128], OP.mult)
        vector.drain()
        vector.tensor_tensor(a["stat"][0:1, 128:256], a["stat"][0:1, 128:256],
                             a["smt"][0:1, 0:128], OP.subtract)
        vector.drain().then_inc(s_["sv"], 1)
        vector.wait_ge(s_["st"], mk["tBC"])
        mrep = PB[5][:, 0:128].unsqueeze(1).broadcast_to([128, 8, 128])
        rrep = PB[5][:, 128:256].unsqueeze(1).broadcast_to([128, 8, 128])
        vector.tensor_tensor(a["ub32"][:], a["ub32"][:], mrep, OP.subtract)
        vector.tensor_tensor(a["xnb"][:], a["ub32"][:], rrep, OP.mult)
        vector.drain().then_inc(s_["sv"], 1)
        for sl in range(4):
            vector.wait_ge(s_["st"], mk[f"tIP{sl}"])
            vector.tensor_copy(a["zs"][:, sl * 512:(sl + 1) * 512], PB[sl % 2][:])
            vector.drain().then_inc(s_["sv"], 1)
        vector.wait_ge(s_["ss"], mk["sIP8"])
        vector.tensor_tensor(a["dt_"][:], a["sml"][:, 256:288], c["dtb"][:], OP.add)
        vector.drain()
        vector.scalar_tensor_tensor(a["smt"][:, 0:32], a["dt_"][:], -1.0, a["dt_"][:],
                                    OP.mult, OP.min)
        vector.scalar_tensor_tensor(a["smt"][:, 32:64], a["sml"][:, 288:320],
                                    -1.0, a["sml"][:, 288:320], OP.mult, OP.min)
        vector.drain().then_inc(s_["sv"], 1)
        vector.wait_ge(s_["ss"], mk["sSM"])
        vector.tensor_scalar(a["smt"][:, 96:128], a["dt_"][:], 0.0, None, OP.max)
        vector.tensor_scalar(a["Gbuf"][:, 0, 0:32], a["sml"][:, 288:320], 0.0, None,
                             OP.max)
        vector.tensor_scalar(a["lam"][:], a["smt"][:, 64:96], 1.0, None, OP.add)
        vector.drain()
        vector.reciprocal(a["lam"][:], a["lam"][:])
        vector.drain()
        vector.tensor_tensor(a["dt_"][:], a["smt"][:, 96:128], a["smt"][:, 0:32],
                             OP.add)
        vector.tensor_tensor(a["adt"][:], a["Gbuf"][:, 0, 0:32], a["smt"][:, 32:64],
                             OP.add)
        vector.tensor_scalar(a["smt"][:, 96:128], a["lam"][:], -1.0, 1.0, OP.mult,
                             OP.add)
        vector.drain()
        vector.tensor_scalar(a["adt"][:], a["adt"][:], -1.0, -1e-4, OP.mult, OP.min)
        vector.tensor_tensor(a["dtlam"][:], a["dt_"][:], a["lam"][:], OP.mult)
        vector.tensor_tensor(a["dt1ml"][:], a["dt_"][:], a["smt"][:, 96:128],
                             OP.mult)
        vector.drain()
        vector.tensor_tensor(a["adt"][:], a["adt"][:], a["dt_"][:], OP.mult)
        vector.tensor_tensor(a["dtlamD"][:], a["dtlam"][:], c["drep"][:], OP.add)
        vector.drain().then_inc(s_["sv"], 1)
        vector.wait_ge(s_["st"], mk["tCS"])
        vector.tensor_copy(a["cs_tm"][:], PB[6][:, 0:32])
        vector.tensor_tensor(a["wvec"][:], a["dtlam"][:], PB[6][:, 32:64], OP.add)
        vector.drain()
        vector.tensor_scalar(a["wvec"][:], a["wvec"][:], 1e-30, None, OP.max)
        vector.drain().then_inc(s_["sv"], 1)
        vector.wait_ge(s_["ss"], mk["sCS"])
        vector.tensor_tensor(a["qw"][:], a["qw"][:], a["cs_tm"][:], OP.subtract)
        vector.drain()
        vector.tensor_tensor(a["smt"][:, 0:32], a["qw"][:], PB[6][:, 64:96], OP.add)
        vector.drain().then_inc(s_["sv"], 1)
        vector.wait_ge(s_["ss"], mk["sQH"])
        qrep = a["qh_e"][:].unsqueeze(2).broadcast_to([128, NH, HP])
        vector.tensor_tensor(a["xqh"][:], a["x_tm"][:], qrep, OP.mult)
        if ci > 0:
            drep0 = PB[6][:, 96:128].unsqueeze(2).broadcast_to([128, NH, HP])
            vector.tensor_tensor(a["xmaskE"][:], a["xmaskE"][:], drep0, OP.mult)
        vector.drain().then_inc(s_["sv"], 1)
        dtr = a["dt_"][:].unsqueeze(2).broadcast_to([128, NH, NA])
        angr = a["sml"][:, 352:384].unsqueeze(1).broadcast_to([128, NH, NA])
        vector.tensor_tensor(a["Gbuf"][:], dtr, angr, OP.mult)
        vector.drain().then_inc(s_["sv"], 1)
        Gflat = a["Gbuf"][:].rearrange("p h q -> p (h q)")
        for j in range(2):
            gsl = Gflat[:, j * 512:(j + 1) * 512]
            vector.wait_ge(s_["st"], mk["tTH0" if j == 0 else "tTH1"])
            vector.tensor_tensor(a["carry"][0:1, j * 512:(j + 1) * 512],
                                 a["carry"][0:1, j * 512:(j + 1) * 512],
                                 PB[7][0:1, :], OP.add)
            vector.drain().then_inc(s_["sv"], 1)
            vector.tensor_scalar(gsl, PB[2 + j][:], 1.0 / (2 * PI), MAGIC, OP.mult,
                                 OP.add)
            vector.tensor_scalar(gsl, gsl, MAGIC, 2 * PI, OP.subtract, OP.mult)
            vector.tensor_tensor(gsl, PB[2 + j][:], gsl, OP.subtract)
            vector.drain().then_inc(s_["sv"], 1)
            vector.wait_ge(s_["ss"], mk[f"sS{j}"])
            vector.tensor_scalar(gsl, PB[2 + j][:], 1.0 / (2 * PI), 0.25, OP.mult,
                                 OP.add)
            vector.tensor_scalar(gsl, gsl, MAGIC, MAGIC, OP.add, OP.subtract)
            vector.tensor_scalar(gsl, gsl, 2 * PI, PI / 2, OP.mult, OP.subtract)
            vector.tensor_tensor(gsl, PB[2 + j][:], gsl, OP.subtract)
            vector.drain().then_inc(s_["sv"], 1)
        for src0, nwkey, biaskey in [(0, "bnw", "bias_b"), (128, "cnw", "bias_c")]:
            vector.tensor_tensor(a["Ebuf"][:, 0, :], a["sml"][:, src0:src0 + 128],
                                 a["sml"][:, src0:src0 + 128], OP.mult)
            vector.drain()
            vector.tensor_reduce(a["acol"][:, 0:1], a["Ebuf"][:, 0, :], AX.X, OP.add)
            vector.drain().then_inc(s_["sv"], 1)
            vector.wait_ge(s_["ss"], mk["sRB" if src0 == 0 else "sRC"])
            if src0 == 0:
                W(vector, "st", "p_tOP")
                dst = a["bh_all"][:, cur]
            else:
                dst = a["ch_all"][:]
            vector.scalar_tensor_tensor(a["Ebuf"][:, 0, :],
                                        a["sml"][:, src0:src0 + 128],
                                        a["acol"][:, 0:1], c[nwkey][:], OP.mult,
                                        OP.mult)
            vector.drain()
            nrep = a["Ebuf"][:, 0, :].unsqueeze(1).broadcast_to([128, NH, NS])
            brep = a[biaskey][:].rearrange("p (h n) -> p h n", h=NH)
            vector.tensor_tensor(dst, nrep, brep, OP.add)
            vector.drain()
            rt = a["Ebuf"][:].rearrange("p g t -> p (g t)").bitcast(BF16)
            v1 = dst[:, :, 0:NA]
            v2 = dst[:, :, NA:2 * NA]
            sinr = a["sin_t"][:]
            cosr = a["cos_t"][:]
            rt0 = rt[:, 0:1024].rearrange("p (h q) -> p h q", h=NH)
            rt1 = rt[:, 1024:2048].rearrange("p (h q) -> p h q", h=NH)
            vector.tensor_tensor(rt0, v2, sinr, OP.mult)
            vector.tensor_tensor(v2, v2, cosr, OP.mult)
            vector.tensor_tensor(rt1, v1, sinr, OP.mult)
            vector.tensor_tensor(v2, rt1, v2, OP.add)
            vector.tensor_tensor(v1, v1, cosr, OP.mult)
            vector.tensor_tensor(v1, v1, rt0, OP.subtract)
            vector.drain().then_inc(s_["sv"], 1)
        def v_tre(g):
            vector.wait_ge(s_["st"], mk[f"tTR{g}"])
            vector.tensor_copy(a["bh_sm"][:, g % 2].rearrange("p i t -> p (i t)"),
                               PB[6][:].bitcast(BF16))
            vector.tensor_copy(a["ch_sm"][:, g % 2].rearrange("p i t -> p (i t)"),
                               PB[7][:].bitcast(BF16))
            vector.drain().then_inc(s_["sv"], 1)

        v_tre(0)
        for g in range(4):
            hg = g * 8
            vector.wait_ge(s_["st"], mk[f"tSC{g}"])
            for i in range(8):
                vector.scalar_tensor_tensor(
                    a["Ebuf"][:, i, :],
                    PB[4 + i // 4][:, (i % 4) * 128:(i % 4) * 128 + 128],
                    a["qw"][:, hg + i:hg + i + 1], c["c_mask"][:], OP.add, OP.add)
            vector.drain().then_inc(s_["sv"], 1)
            vector.wait_ge(s_["ss"], mk[f"sE{g}"])
            vector.wait_ge(s_["sg"], mk[f"gDG{g}"])
            pp = a["PSb"][:].rearrange("p i t -> p (i t)")
            dg = a["dgbuf"][:].rearrange("p i t -> p (i t)")
            vector.tensor_tensor(pp[:, 0:512], pp[:, 0:512], dg[:, 0:512], OP.add)
            vector.tensor_tensor(pp[:, 512:1024], pp[:, 512:1024], dg[:, 512:1024],
                                 OP.add)
            vector.drain()
            vector.tensor_tensor(pp[:, 0:512], PB[2][:], pp[:, 0:512], OP.mult)
            vector.tensor_tensor(pp[:, 512:1024], PB[3][:], pp[:, 512:1024], OP.mult)
            vector.drain().then_inc(s_["sv"], 1)
            if g < 3:
                v_tre(g + 1)
            if ci > 0:
                vector.wait_ge(s_["st"], mk[f"tBD{g}"])
                hsl0 = a["Hst"][:, hg * HP:(hg + 8) * HP]
                vector.tensor_tensor(hsl0, hsl0, PB[2][:], OP.add)
                vector.drain().then_inc(s_["sv"], 1)
            vector.wait_ge(s_["st"], mk[f"tY{g}"])
            arep = a["alpha"][:, hg:hg + 8].unsqueeze(2).broadcast_to([128, 8, HP])
            e13 = a["Gbuf"][:].rearrange("p h q -> p (h q)")[:, 0:512].rearrange(
                "p (i q) -> p i q", i=8)
            yb3 = a["ybar_g"][:].rearrange("p (i q) -> p i q", i=8)
            pb5_3 = PB[5][:].rearrange("p (i q) -> p i q", i=8)
            pb4_3 = PB[4][:].rearrange("p (i q) -> p i q", i=8)
            pb2_3 = PB[2][:].rearrange("p (i q) -> p i q", i=8)
            vector.tensor_tensor(e13, pb5_3, arep, OP.mult)
            vector.tensor_tensor(e13, e13, pb4_3, OP.add)
            drp = c["drep"][:, hg:hg + 8].unsqueeze(2).broadcast_to([128, 8, HP])
            xsl = a["x_tm"][:, hg:hg + 8, :]
            vector.tensor_tensor(yb3, xsl, drp, OP.mult)
            vector.tensor_tensor(e13, e13, yb3, OP.add)
            zs3 = a["zs"][:, hg * HP:(hg + 8) * HP].rearrange("p (i q) -> p i q", i=8)
            vector.tensor_tensor(yb3, e13, zs3, OP.mult)
            dcr = a["decay"][:, hg:hg + 8].unsqueeze(2).broadcast_to([128, 8, HP])
            hsl = a["Hst"][:, hg * HP:(hg + 8) * HP].rearrange("p (i q) -> p i q", i=8)
            vector.tensor_tensor(e13, hsl, dcr, OP.mult)
            vector.tensor_tensor(hsl, e13, pb2_3, OP.add)
            vector.drain().then_inc(s_["sv"], 1)
            vector.wait_ge(s_["st"], mk[f"tYT{g}"])
            vector.tensor_copy(a["ybar_sm"][:].rearrange("p j t -> p (j t)"),
                               PB[6][:].bitcast(BF16)[:, 0:512])
            vector.drain().then_inc(s_["sv"], 1)
        vector.wait_ge(s_["st"], mk["tOP"])
        s_ev = a["ub32"][:].rearrange("p k t -> p (k t)")
        vector.tensor_copy(s_ev[:, 0:512], PB[0][:])
        vector.tensor_copy(s_ev[:, 512:1024], PB[1][:])
        vector.tensor_tensor(
            a["xmaskE"][:].rearrange("p h q -> p (h q)"),
            a["x_tm"][:].rearrange("p h q -> p (h q)"),
            e127.broadcast_to([128, NH * HP]), OP.mult)
        vector.drain().then_inc(s_["sv"], 1)
    b.cnt["sv"] = mk["vSEV"]

    @b.G
    def _(gp):
        s_ = b.sem
        for g in range(4):
            hg = g * 8
            gp.wait_ge(s_["sv"], mk["vCB"] if g == 0 else mk[f"vPS{g - 1}"])
            if g > 0:
                gp.wait_ge(s_["ss"], mk[f"sE{g - 1}"])
            for i in range(8):
                gp.tensor_scalar(a["Ebuf"][:, i, :], c["c_ones"][:],
                                 a["cs_tm"][:, hg + i:hg + i + 1], None, OP.mult)
            gp.drain().then_inc(s_["sg"], 1)
            irep = c["c_idbf"][:].unsqueeze(1).broadcast_to([128, 8, 128])
            drep8 = a["dtlam"][:, hg:hg + 8].unsqueeze(2).broadcast_to([128, 8, 128])
            gp.tensor_tensor(a["dgbuf"][:], irep, drep8, OP.mult)
            gp.drain().then_inc(s_["sg"], 1)
    b.cnt["sg"] = mk["gDG3"]

    @b.S
    def _(scalar):
        s_ = b.sem
        scalar.wait_ge(s_["sv"], mk["vLN1"])
        scalar.copy(a["PSb"][:].rearrange("p k t -> p (k t)"),
                    a["ub32"][:].rearrange("p k t -> p (k t)"))
        scalar.activation(a["xnb"][:].rearrange("p k t -> p (k t)"),
                          a["ub32"][:].rearrange("p k t -> p (k t)"), AF.Square)
        scalar.drain().then_inc(s_["ss"], 1)
        scalar.wait_ge(s_["sv"], mk["vLN2"])
        scalar.activation(a["smt"][0:1, 0:128], a["stat"][0:1, 128:256], AF.Ln,
                          bias=1e-5)
        scalar.activation(a["stat"][0:1, 128:256], a["smt"][0:1, 0:128], AF.Exp,
                          scale=-0.5)
        scalar.drain().then_inc(s_["ss"], 1)
        for sl in range(4, 9):
            scalar.wait_ge(s_["st"], mk[f"tIP{sl}"])
            if sl < 8:
                scalar.copy(a["x_tm"][:].rearrange("p h q -> p (h q)")[
                    :, (sl - 4) * 512:(sl - 3) * 512], PB[sl % 2][:])
            else:
                scalar.copy(a["sml"][:], PB[sl % 2][:, 0:384])
            scalar.drain().then_inc(s_["ss"], 1)
        scalar.wait_ge(s_["sv"], mk["vSM1"])
        scalar.activation(a["smt"][:, 64:128], a["smt"][:, 0:64], AF.Exp)
        scalar.activation(a["smt"][:, 0:64], a["smt"][:, 64:128], AF.Ln, bias=1.0)
        scalar.activation(a["smt"][:, 64:96], a["sml"][:, 320:352], AF.Exp,
                          scale=-1.0)
        scalar.drain().then_inc(s_["ss"], 1)
        scalar.wait_ge(s_["sv"], mk["vCS1"])
        scalar.activation(a["qw"][:], a["wvec"][:], AF.Ln)
        scalar.activation(a["alpha"][:], a["cs_tm"][:], AF.Exp)
        scalar.activation(a["decay"][:], PB[6][:, 64:96], AF.Exp)
        scalar.drain().then_inc(s_["ss"], 1)
        scalar.wait_ge(s_["sv"], mk["vQW"])
        scalar.activation(a["qh_e"][:], a["smt"][:, 0:32], AF.Exp)
        scalar.drain().then_inc(s_["ss"], 1)
        for j in range(2):
            gsl = a["Gbuf"][:].rearrange("p h q -> p (h q)")[:, j * 512:(j + 1) * 512]
            scalar.wait_ge(s_["sv"], mk[f"vRS{j}"])
            scalar.activation(
                a["sin_t"][:].rearrange("p h q -> p (h q)")[:, j * 512:(j + 1) * 512],
                gsl, AF.Sin)
            scalar.drain().then_inc(s_["ss"], 1)
            scalar.wait_ge(s_["sv"], mk[f"vRC{j}"])
            scalar.activation(
                a["cos_t"][:].rearrange("p h q -> p (h q)")[:, j * 512:(j + 1) * 512],
                gsl, AF.Sin)
            scalar.drain().then_inc(s_["ss"], 1)
        scalar.wait_ge(s_["sv"], mk["vIP3"])
        for jz in range(4):
            scalar.activation(a["zs"][:, jz * 512:(jz + 1) * 512],
                              a["zs"][:, jz * 512:(jz + 1) * 512], AF.Silu)
        scalar.drain().then_inc(s_["ss"], 1)
        for nm, wm in [("sRB", "vRMB"), ("sRC", "vRMC")]:
            scalar.wait_ge(s_["sv"], mk[wm])
            scalar.activation(a["acol"][:, 1:2], a["acol"][:, 0:1], AF.Ln,
                              scale=1.0 / NS, bias=1e-5)
            scalar.activation(a["acol"][:, 0:1], a["acol"][:, 1:2], AF.Exp,
                              scale=-0.5)
            scalar.drain().then_inc(s_["ss"], 1)
        for g in range(4):
            scalar.wait_ge(s_["sv"], mk[f"vE{g}"])
            ee = a["Ebuf"][:].rearrange("p i t -> p (i t)")
            pp = a["PSb"][:].rearrange("p i t -> p (i t)")
            scalar.activation(pp[:, 0:512], ee[:, 0:512], AF.Exp)
            scalar.activation(pp[:, 512:1024], ee[:, 512:1024], AF.Exp)
            scalar.drain().then_inc(s_["ss"], 1)
    b.cnt["ss"] = mk["sE3"]

    b.prev_a_marks = {"p_tY3": mk["tY3"], "p_sE3": mk["sE3"], "p_tOP": mk["tOP"]}

    b.tap("xnb", a["xnb"][:].rearrange("p k t -> p (k t)"), bi, ci)
    b.tap("xtm", a["x_tm"][:].rearrange("p h q -> p (h q)"), bi, ci)
    b.tap("zs", a["zs"][:], bi, ci)
    b.tap("sml", a["sml"][:], bi, ci)
    for nm in ("dt_", "adt", "lam", "wvec", "qw", "alpha", "qh_e", "decay",
               "cs_tm", "dtlam", "dt1ml"):
        b.tap(nm, a[nm][:], bi, ci)
    b.tap("sin", a["sin_t"][:].rearrange("p h q -> p (h q)"), bi, ci)
    b.tap("cos", a["cos_t"][:].rearrange("p h q -> p (h q)"), bi, ci)
    b.tap("bh", a["bh_all"][:, cur].rearrange("p h q -> p (h q)"), bi, ci)
    b.tap("ch", a["ch_all"][:].rearrange("p h q -> p (h q)"), bi, ci)
    b.tap("hst", a["Hst"][:], bi, ci)
    b.tap("sev", a["ub32"][:].rearrange("p k t -> p (k t)"), bi, ci)
    b.tap("carry", a["carry"][:], bi, ci)


def _chunk_b1(b, d, c, p, PB, bi, ci):
    mk = {}

    def plan(ch, names, step=1):
        v = b.cnt[ch]
        for n in names:
            v += step
            mk[n] = v

    plan("sv", ["vM8", "vSC", "vSR", "vXQ"] + [f"vGU{sl}" for sl in range(8)]
         + ["vGM"])
    plan("st", ["tM8", "tSR", "tSB"] + [x for sl in range(8)
                                        for x in (f"tG{sl}", f"tU{sl}")])
    plan("ss", [f"sG{sl}" for sl in range(8)])
    plan("sd", ["dXNF", "dGU"], step=16)

    prev_dGU = mk["dXNF"] - 16
    buf = (bi * NCH + ci) % 2
    prev_vXQ = getattr(b, "prev_b1_vXQ", 0)
    xnf = p["xnf"][:, buf]

    @b.D
    def _(sync):
        b.wait(sync, "sd", prev_dGU)
        sync.wait_ge(b.sem["sv"], prev_vXQ)
        sync.dma_start(xnf.rearrange("p k t -> p (k t)"),
                       d["sc_xn"][bi, ci]).then_inc(b.sem["sd"], 16)
        sync.wait_ge(b.sem["sv"], mk["vGU7"])
        sync.dma_start(d["sc_gu"][bi, ci], p["gu_b"][:]).then_inc(b.sem["sd"], 16)
    b.cnt["sd"] = mk["dGU"]
    b.prev_b1_vXQ = mk["vXQ"]

    @b.T
    def _(tensor):
        s_ = b.sem
        tensor.wait_ge(s_["sv"], mk["vM8"])
        tensor.transpose(PB[4][:, 0:128], p["m8"][:], c["c_idf32"][:])
        tensor.drain().then_inc(s_["st"], 1)
        tensor.wait_ge(s_["sv"], mk["vSC"])
        tensor.transpose(PB[4][0:1, 128:256], p["scol"][:, 1:2], c["c_idf32"][:])
        tensor.drain().then_inc(s_["st"], 1)
        tensor.wait_ge(s_["sv"], mk["vSR"])
        tensor.matmul(PB[5][:, 0:128], c["c_ones"][0:1, :], p["srow"][0:1, :],
                      start=True, stop=True)
        tensor.drain().then_inc(s_["st"], 1)
        tensor.wait_ge(s_["sv"], mk["vXQ"])
        for sl in range(8):
            if sl >= 2:
                tensor.wait_ge(s_["ss"], mk[f"sG{sl - 2}"])
            for k in range(8):
                tensor.matmul(PB[sl % 2][:], p["xq_fm"][:, k, :],
                              p["wgq"][:, k, sl * 512:(sl + 1) * 512],
                              start=(k == 0), stop=(k == 7))
            tensor.drain().then_inc(s_["st"], 1)
            if sl >= 2:
                tensor.wait_ge(s_["sv"], mk[f"vGU{sl - 2}"])
            for k in range(8):
                tensor.matmul(PB[2 + sl % 2][:], p["xq_fm"][:, k, :],
                              p["wuq"][:, k, sl * 512:(sl + 1) * 512],
                              start=(k == 0), stop=(k == 7))
            tensor.drain().then_inc(s_["st"], 1)
    b.cnt["st"] = mk["tU7"]
    prev_tU7 = getattr(b, "prev_b1_tU7", 0)
    b.prev_b1_tU7 = mk["tU7"]

    @b.V
    def _(vector):
        s_ = b.sem
        vector.wait_ge(s_["sd"], mk["dXNF"])
        vector.scalar_tensor_tensor(p["m8"][:], xnf[:, 0, :], -1.0,
                                    xnf[:, 0, :], OP.mult, OP.max)
        for k in range(1, 8):
            vector.scalar_tensor_tensor(p["mt"][:], xnf[:, k, :], -1.0,
                                        xnf[:, k, :], OP.mult, OP.max)
            vector.tensor_tensor(p["m8"][:], p["m8"][:], p["mt"][:], OP.max)
        vector.drain().then_inc(s_["sv"], 1)
        vector.wait_ge(s_["st"], mk["tM8"])
        vector.tensor_reduce(p["scol"][:, 0:1], PB[4][:, 0:128], AX.X, OP.max)
        vector.drain()
        vector.tensor_scalar(p["scol"][:, 0:1], p["scol"][:, 0:1], 1e-5, None,
                             OP.max)
        vector.drain()
        vector.reciprocal(p["scol"][:, 1:2], p["scol"][:, 0:1])
        vector.tensor_scalar(p["scol"][:, 2:3], p["scol"][:, 0:1],
                             c["wsc"][:, 3:4], 1.0 / 127.0, OP.mult, OP.mult)
        vector.tensor_scalar(p["scol"][:, 3:4], p["scol"][:, 0:1],
                             c["wsc"][:, 4:5], 1.0 / 127.0, OP.mult, OP.mult)
        vector.drain()
        vector.tensor_scalar(p["scol"][:, 1:2], p["scol"][:, 1:2], 127.0, None,
                             OP.mult)
        vector.drain().then_inc(s_["sv"], 1)
        vector.wait_ge(s_["st"], mk["tSR"])
        vector.tensor_copy(p["srow"][0:1, :], PB[4][0:1, 128:256])
        vector.drain().then_inc(s_["sv"], 1)
        vector.wait_ge(s_["st"], mk["tSB"])
        vector.wait_ge(s_["st"], prev_tU7)
        for k in range(8):
            vector.tensor_tensor(p["mt"][:], xnf[:, k, :], PB[5][:, 0:128],
                                 OP.mult)
            vector.tensor_scalar(p["xq_fm"][:, k, :], p["mt"][:], MAGIC, MAGIC,
                                 OP.add, OP.subtract)
        vector.drain().then_inc(s_["sv"], 1)
        for sl in range(8):
            vector.wait_ge(s_["st"], mk[f"tU{sl}"])
            vector.wait_ge(s_["ss"], mk[f"sG{sl}"])
            vector.scalar_tensor_tensor(
                p["gu_b"][:, sl * 512:(sl + 1) * 512], PB[2 + sl % 2][:],
                p["scol"][:, 3:4], p["gu_b"][:, sl * 512:(sl + 1) * 512],
                OP.mult, OP.mult)
            vector.drain().then_inc(s_["sv"], 1)
        vector.tensor_reduce(c["gumax"][:, bi * NCH + ci:bi * NCH + ci + 1],
                             p["gu_b"][:], AX.X, OP.max, apply_absolute_value=True)
        vector.drain().then_inc(s_["sv"], 1)
    b.cnt["sv"] = mk["vGM"]

    @b.S
    def _(scalar):
        s_ = b.sem
        scalar.wait_ge(s_["sd"], prev_dGU)
        for sl in range(8):
            scalar.wait_ge(s_["st"], mk[f"tG{sl}"])
            scalar.activation(p["gu_b"][:, sl * 512:(sl + 1) * 512],
                              PB[sl % 2][:], AF.Silu, scale=p["scol"][:, 2:3])
            scalar.drain().then_inc(s_["ss"], 1)
    b.cnt["ss"] = mk["sG7"]

    b.tap("xqfm", p["xq_fm"][:].rearrange("p k t -> p (k t)"), bi, ci)
    b.tap("gu", p["gu_b"][:], bi, ci)
    b.tap("b1scol", p["scol"][:], bi, ci)


def _chunk_b2(b, d, c, q, PB, bi, ci):
    mk = {}

    def plan(ch, names, step=1):
        v = b.cnt[ch]
        for n in names:
            v += step
            mk[n] = v

    plan("sv", ["vGQ"] + [f"vTQ{g4}" for g4 in range(4)] + ["vOB"])
    plan("st", [f"tTQ{g4}" for g4 in range(4)] + ["tD0", "tD1"])
    plan("ss", ["sD0", "sD1"])
    mk["dIN"] = b.cnt["sd"] + 48
    mk["dOUT"] = b.cnt["sd"] + 64

    buf = (bi * NCH + ci) % 2
    prev_vGQ = getattr(b, "prev_b2_vGQ", 0)
    prev_vOB = getattr(b, "prev_b2_vOB", 0)
    gu_tm = q["gu_tm"][:, buf]
    u_b = q["u_b"][:, buf]
    s_b = q["s_b"][:, buf]

    @b.D
    def _(sync):
        b.wait(sync, "sd", mk["dIN"] - 48)
        sync.wait_ge(b.sem["sv"], prev_vGQ)
        sync.dma_start(gu_tm, d["sc_gu"][bi, ci]).then_inc(b.sem["sd"], 16)
        sync.dma_start(u_b, d["u_tm"][bi, ci * 128:(ci + 1) * 128, :]
                       ).then_inc(b.sem["sd"], 16)
        sync.dma_start(s_b, d["sc_s"][bi, ci]).then_inc(b.sem["sd"], 16)
        sync.wait_ge(b.sem["sv"], mk["vOB"])
        sync.dma_start(d["out"][bi, ci * 128:(ci + 1) * 128, :], q["o_b"][:]
                       ).then_inc(b.sem["sd"], 16)
    b.cnt["sd"] = mk["dOUT"]
    b.prev_b2_vGQ = mk["vGQ"]
    b.prev_b2_vOB = mk["vOB"]

    @b.T
    def _(tensor):
        s_ = b.sem
        tensor.wait_ge(s_["sv"], mk["vGQ"])
        for g4 in range(4):
            if g4 >= 2:
                tensor.wait_ge(s_["sv"], mk[f"vTQ{g4 - 2}"])
            pbf = PB[4 + g4 % 2][:].bitcast(BF16)
            for i in range(8):
                tensor.transpose(
                    pbf[:, i * 128:(i + 1) * 128],
                    q["guq_tm"][:, (g4 * 8 + i) * 128:(g4 * 8 + i + 1) * 128],
                    c["c_idbf"][:])
            tensor.drain().then_inc(s_["st"], 1)
        tensor.wait_ge(s_["sv"], mk["vTQ3"])
        for sl in range(2):
            for k in range(32):
                tensor.matmul(PB[sl][:], q["guq_fm"][:, k, :],
                              q["wdq"][:, k, sl * 512:(sl + 1) * 512],
                              start=(k == 0), stop=(k == 31))
            tensor.drain().then_inc(s_["st"], 1)
    b.cnt["st"] = mk["tD1"]

    @b.V
    def _(vector):
        s_ = b.sem
        vector.wait_ge(s_["sd"], mk["dIN"])
        vector.tensor_scalar(q["scol2"][:, 0:1],
                             c["gumax"][:, bi * NCH + ci:bi * NCH + ci + 1],
                             1e-5, None, OP.max)
        vector.drain()
        vector.reciprocal(q["scol2"][:, 1:2], q["scol2"][:, 0:1])
        vector.tensor_scalar(q["scol2"][:, 2:3], q["scol2"][:, 0:1],
                             c["wsc"][:, 5:6], 1.0 / 127.0, OP.mult, OP.mult)
        vector.drain()
        vector.tensor_scalar(q["scol2"][:, 1:2], q["scol2"][:, 1:2], 127.0, None,
                             OP.mult)
        vector.drain()
        vector.tensor_tensor(q["qtmp"][:], gu_tm,
                             q["scol2"][:, 1:2].broadcast_to([128, FI]), OP.mult)
        vector.tensor_scalar(q["qtmp"][:], q["qtmp"][:], MAGIC, MAGIC, OP.add,
                             OP.subtract)
        vector.tensor_copy(q["guq_tm"][:], q["qtmp"][:])
        vector.drain().then_inc(s_["sv"], 1)
        for g4 in range(4):
            vector.wait_ge(s_["st"], mk[f"tTQ{g4}"])
            vector.tensor_copy(
                q["guq_fm"][:].rearrange("p k t -> p (k t)")[
                    :, g4 * 1024:(g4 + 1) * 1024], PB[4 + g4 % 2][:].bitcast(BF16))
            vector.drain().then_inc(s_["sv"], 1)
        vector.wait_ge(s_["ss"], mk["sD1"])
        b.wait(vector, "sd", mk["dOUT"] - 64)
        vector.tensor_tensor(q["o_b"][:], s_b, q["g_b"][:], OP.mult)
        vector.drain()
        vector.tensor_tensor(q["o_b"][:], q["o_b"][:], u_b, OP.add)
        vector.drain().then_inc(s_["sv"], 1)
    b.cnt["sv"] = mk["vOB"]

    @b.S
    def _(scalar):
        s_ = b.sem
        for sl in range(2):
            scalar.wait_ge(s_["st"], mk[f"tD{sl}"])
            scalar.activation(q["g_b"][:, sl * 512:(sl + 1) * 512], PB[sl][:],
                              AF.Silu, scale=q["scol2"][:, 2:3])
            scalar.drain().then_inc(s_["ss"], 1)
    b.cnt["ss"] = mk["sD1"]

    b.tap("guq", q["guq_tm"][:], bi, ci)
    b.tap("b2scol", q["scol2"][:], bi, ci)
    b.tap("gutm", gu_tm, bi, ci)
    b.tap("gb", q["g_b"][:], bi, ci)
    b.tap("ob", q["o_b"][:], bi, ci)


def _host_inputs(inputs, core):
    f = np.float32
    u = np.asarray(inputs["L0_attn_u"], f)[core * NB:(core + 1) * NB]
    ins = {
        "u_tm": np.ascontiguousarray(u),
        "u_fm": np.ascontiguousarray(u.transpose(0, 2, 1)),
        "w_inT": np.ascontiguousarray(np.asarray(inputs["L0_attn_W_in"], f).T),
        "w_outT": np.ascontiguousarray(np.asarray(inputs["L0_attn_W_out"], f).T),
        "w_gT": np.ascontiguousarray(np.asarray(inputs["L0_ffn_w_gate"], f).T),
        "w_uT": np.ascontiguousarray(np.asarray(inputs["L0_ffn_w_up"], f).T),
        "w_dT": np.ascontiguousarray(np.asarray(inputs["L0_ffn_w_down"], f).T),
        "dtb_rep": np.ascontiguousarray(
            np.broadcast_to(np.asarray(inputs["L0_attn_dt_bias"], f), (128, NH))),
        "bbias_row": np.asarray(inputs["L0_attn_B_bias"], f).reshape(1, NH * NS),
        "cbias_row": np.asarray(inputs["L0_attn_C_bias"], f).reshape(1, NH * NS),
        "bnw_rep": np.ascontiguousarray(
            np.broadcast_to(np.asarray(inputs["L0_attn_B_norm_w"], f), (128, NS))),
        "cnw_rep": np.ascontiguousarray(
            np.broadcast_to(np.asarray(inputs["L0_attn_C_norm_w"], f), (128, NS))),
        "d_rep": np.ascontiguousarray(
            np.broadcast_to(np.asarray(inputs["L0_attn_D_param"], f), (128, NH))),
    }
    return ins


def _host_consts():
    f = np.float32
    idx = np.arange(128)
    L = (idx[:, None] <= idx[None, :]).astype(f)
    shift = (idx[:, None] == idx[None, :] + 1).astype(f)
    mask = np.where(idx[None, :] > idx[:, None], 0.0, NEG).astype(f)
    ones = np.ones((128, 128), f)
    e0 = np.zeros((128, 128), f)
    e0[0, :] = 1.0
    ident = np.eye(128, dtype=f)
    cols = np.zeros((128, 4), f)
    cols[:, 0] = 1.0
    cols[127, 1] = 1.0
    cols[:, 2] = 1e-5
    colsbf = np.zeros((128, 2), ml_dtypes.bfloat16)
    colsbf[:, 0] = 1.0
    colsbf[127, 1] = 1.0
    return {
        "c_L": L, "c_shift": shift, "c_mask": mask, "c_ones": ones, "c_e0": e0,
        "c_idf32": ident, "c_idbf": ident.astype(ml_dtypes.bfloat16),
        "c_cols": cols, "c_colsbf": colsbf,
    }


_CACHE = {}


def kernel(**inputs):
    if "nc" not in _CACHE:
        nc = bass.Bass("TRN2", target_bir_lowering=False, debug=False)
        build(nc)
        _CACHE["nc"] = nc
    nc = _CACHE["nc"]
    consts = _host_consts()
    in_maps = []
    for core in range(8):
        m = _host_inputs(inputs, core)
        m.update(consts)
        in_maps.append(m)
    res = run_bass_kernel_spmd(nc, in_maps, core_ids=list(range(8)))
    out = np.concatenate([np.asarray(res.results[cc]["out"]) for cc in range(8)],
                         axis=0)
    return out.astype(np.float32)


# version 08
# speedup vs baseline: 1.0965x; 1.0965x over previous
import numpy as np
import ml_dtypes
from contextlib import ExitStack

import concourse.bass as bass
import concourse.mybir as mybir
from concourse.bass_utils import run_bass_kernel_spmd

F32 = mybir.dt.float32
BF16 = mybir.dt.bfloat16
AX = mybir.AxisListType
OP = mybir.AluOpType
AF = mybir.ActivationFunctionType

TAPS = None

MAGIC = float(3 * 2.0 ** 22)
PI = float(np.pi)
NEG = -1e30

T = 2048
NB = 2
NCH = T // 128
DM = 1024
DI = 2048
NH = 32
HP = 64
NS = 128
NA = 32
DIN = 4480
FI = 4096


class B:

    def tap(self, name, apv, bi=None, ci=None):
        if TAPS is None:
            return
        if bi is not None and (bi, ci) != (TAPS.get("bi", 0), TAPS.get("ci", 0)):
            return
        sh = list(apv.shape)
        dt = apv.dtype
        t = self.nc.dram_tensor("tap_" + name, sh, dt, kind="ExternalOutput").ap()

        @self.D
        def _(sync):
            for nm in ("sv", "ss", "st", "sg"):
                self.wait(sync, nm)
            self.inc(sync.dma_start(t, apv), "sd", 16)

    def __init__(self, nc, stack):
        self.nc = nc
        self.sem = {}
        self.cnt = {}
        for name in ("sd", "st", "sv", "ss", "sg"):
            self.sem[name] = stack.enter_context(nc.semaphore(name))
            self.cnt[name] = 0
        self.blk = None

    def inc(self, instr, name, k=1):
        instr.then_inc(self.sem[name], k)
        self.cnt[name] += k
        return self.cnt[name]

    def wait(self, eng, name, val=None):
        v = self.cnt[name] if val is None else val
        if v > 0:
            eng.wait_ge(self.sem[name], v)

    def T(self, f):
        self.blk.tensor(f)

    def V(self, f):
        self.blk.vector(f)

    def S(self, f):
        self.blk.scalar(f)

    def G(self, f):
        self.blk.gpsimd(f)

    def D(self, f):
        self.blk.sync(f)


def build(nc):
    stack = ExitStack()
    b = B(nc, stack)
    ap = lambda t: t.ap()

    d = {}
    d["u_fm"] = ap(nc.dram_tensor("u_fm", [NB, DM, T], F32, kind="ExternalInput"))
    d["u_tm"] = ap(nc.dram_tensor("u_tm", [NB, T, DM], F32, kind="ExternalInput"))
    d["w_inT"] = ap(nc.dram_tensor("w_inT", [DM, DIN], F32, kind="ExternalInput"))
    d["w_outT"] = ap(nc.dram_tensor("w_outT", [DI, DM], F32, kind="ExternalInput"))
    d["w_gT"] = ap(nc.dram_tensor("w_gT", [DM, FI], F32, kind="ExternalInput"))
    d["w_uT"] = ap(nc.dram_tensor("w_uT", [DM, FI], F32, kind="ExternalInput"))
    d["w_dT"] = ap(nc.dram_tensor("w_dT", [FI, DM], F32, kind="ExternalInput"))
    d["dtb_rep"] = ap(nc.dram_tensor("dtb_rep", [128, NH], F32, kind="ExternalInput"))
    d["bbias_row"] = ap(nc.dram_tensor("bbias_row", [1, NH * NS], F32, kind="ExternalInput"))
    d["cbias_row"] = ap(nc.dram_tensor("cbias_row", [1, NH * NS], F32, kind="ExternalInput"))
    d["bnw_rep"] = ap(nc.dram_tensor("bnw_rep", [128, NS], F32, kind="ExternalInput"))
    d["cnw_rep"] = ap(nc.dram_tensor("cnw_rep", [128, NS], F32, kind="ExternalInput"))
    d["d_rep"] = ap(nc.dram_tensor("d_rep", [128, NH], F32, kind="ExternalInput"))
    for cn in ("c_L", "c_shift", "c_mask", "c_ones", "c_e0", "c_idf32"):
        d[cn] = ap(nc.dram_tensor(cn, [128, 128], F32, kind="ExternalInput"))
    d["c_idbf"] = ap(nc.dram_tensor("c_idbf", [128, 128], BF16, kind="ExternalInput"))
    d["c_cols"] = ap(nc.dram_tensor("c_cols", [128, 4], F32, kind="ExternalInput"))
    d["c_colsbf"] = ap(nc.dram_tensor("c_colsbf", [128, 2], BF16, kind="ExternalInput"))
    d["out"] = ap(nc.dram_tensor("out", [NB, T, DM], F32, kind="ExternalOutput"))
    d["sc_xn"] = ap(nc.dram_tensor("sc_xn", [NB, NCH, 128, DM], BF16, kind="Internal"))
    d["sc_s"] = ap(nc.dram_tensor("sc_s", [NB, NCH, 128, DM], F32, kind="Internal"))
    d["sc_gu"] = ap(nc.dram_tensor("sc_gu", [NB, NCH, 128, FI], BF16, kind="Internal"))
    d["sc_wqg"] = ap(nc.dram_tensor("sc_wqg", [8, 128, FI], BF16, kind="Internal"))
    d["sc_wqu"] = ap(nc.dram_tensor("sc_wqu", [8, 128, FI], BF16, kind="Internal"))
    d["sc_wqd"] = ap(nc.dram_tensor("sc_wqd", [32, 128, DM], BF16, kind="Internal"))

    with stack:
        c = {}
        for cn in ("c_L", "c_shift", "c_mask", "c_ones", "c_e0", "c_idf32"):
            c[cn] = stack.enter_context(nc.sbuf_tensor("sb" + cn, [128, 128], F32))
        c["c_idbf"] = stack.enter_context(nc.sbuf_tensor("sbc_idbf", [128, 128], BF16))
        c["c_cols"] = stack.enter_context(nc.sbuf_tensor("sbc_cols", [128, 4], F32))
        c["c_colsbf"] = stack.enter_context(nc.sbuf_tensor("sbc_colsbf", [128, 2], BF16))
        c["dtb"] = stack.enter_context(nc.sbuf_tensor("dtb", [128, NH], F32))
        c["bnw"] = stack.enter_context(nc.sbuf_tensor("bnw", [128, NS], F32))
        c["cnw"] = stack.enter_context(nc.sbuf_tensor("cnw", [128, NS], F32))
        c["drep"] = stack.enter_context(nc.sbuf_tensor("drep", [128, NH], F32))
        c["wsc"] = stack.enter_context(nc.sbuf_tensor("wsc", [128, 8], F32))
        c["gumax"] = stack.enter_context(nc.sbuf_tensor("gumax", [128, NB * NCH], F32))
        PB = [stack.enter_context(nc.psum_tensor(f"pb{i}", [128, 512], F32))
              for i in range(8)]
        nc.const_aps.register(F32, 1.0, c["c_cols"][:, 0:1])
        nc.const_aps.register(F32, 1e-5, c["c_cols"][:, 2:3])
        nc.const_aps.register(F32, 0.0, c["c_cols"][:, 3:4])

        with nc.Block("init") as ib:
            @ib.sync
            def _(sync):
                for k in b.sem:
                    sync.sem_clear(b.sem[k])

        with ExitStack() as asc:
            a = {}

            def asb(n, sh, dt=F32):
                a[n] = asc.enter_context(nc.sbuf_tensor(n, sh, dt))
                return a[n]

            asb("winT", [128, 8, DIN], BF16)
            asb("woutT", [128, 16, DM], BF16)
            asb("bias_b", [128, NH * NS], BF16)
            asb("bias_c", [128, NH * NS], BF16)
            asb("Hst", [128, NH * HP], BF16)
            asb("carry", [1, NH * NA], F32)
            asb("acol", [128, 2], F32)

            with nc.Block("main_a") as blk:
                b.blk = blk
                with ExitStack() as pro:
                    a["wtmp"] = pro.enter_context(nc.sbuf_tensor("wtmp", [128, DIN], F32))
                    a["wqtmp"] = pro.enter_context(nc.sbuf_tensor("wqtmp", [128, FI], BF16))
                    a["row2"] = pro.enter_context(nc.sbuf_tensor("row2", [1, 2], F32))
                    _prologue(b, d, c, a, PB)
                del a["wtmp"], a["wqtmp"], a["row2"]
                asb("stat", [1, 256], F32)
                asb("ub32", [128, 8, 128], F32)
                asb("xnb", [128, 8, 128], BF16)
                asb("x_tm", [128, NH, HP], BF16)
                asb("xqh", [128, NH, HP], BF16)
                asb("xmaskE", [128, NH, HP], BF16)
                asb("zs", [128, DI], BF16)
                asb("sml", [128, 384], F32)
                asb("smt", [128, 128], F32)
                for n in ("dt_", "adt", "lam", "dtlam", "dt1ml", "wvec", "qw",
                          "alpha", "qh_e", "decay", "cs_tm", "dtlamD"):
                    asb(n, [128, NH], F32)
                asb("Gbuf", [128, NH, NA], F32)
                asb("sin_t", [128, NH, NA], BF16)
                asb("cos_t", [128, NH, NA], BF16)
                asb("bh_all", [128, 2, NH, NS], BF16)
                asb("ch_all", [128, NH, NS], BF16)
                asb("bh_sm", [128, 2, 8, 128], BF16)
                asb("ch_sm", [128, 2, 8, 128], BF16)
                asb("Ebuf", [128, 8, 128], F32)
                asb("dgbuf", [128, 8, 128], BF16)
                asb("PSb", [128, 8, 128], BF16)
                asb("ybar_g", [128, 512], BF16)
                asb("ybar_sm", [128, 4, 128], BF16)
                for bi in range(NB):
                    _batch_reset(b, a)
                    for ci in range(NCH):
                        _chunk_a(b, d, c, a, PB, bi, ci)

        with ExitStack() as bsc:
            p = {}

            def bsb(n, sh, dt=F32):
                p[n] = bsc.enter_context(nc.sbuf_tensor(n, sh, dt))
                return p[n]

            bsb("wgq", [128, 8, FI], BF16)
            bsb("wuq", [128, 8, FI], BF16)
            bsb("xnf", [128, 2, 8, 128], BF16)
            bsb("xq_fm", [128, 8, 128], BF16)
            bsb("m8", [128, 128], F32)
            bsb("mt", [128, 128], F32)
            bsb("srow", [1, 128], F32)
            bsb("scol", [128, 4], F32)
            bsb("gu_b", [128, FI], BF16)
            with nc.Block("main_b1") as blk:
                b.blk = blk

                @b.D
                def _(sync):
                    b.wait(sync, "sv")
                    b.wait(sync, "ss")
                    b.wait(sync, "sd")
                    for k in range(8):
                        b.inc(sync.dma_start(p["wgq"][:, k, :], d["sc_wqg"][k]), "sd", 16)
                        b.inc(sync.dma_start(p["wuq"][:, k, :], d["sc_wqu"][k]), "sd", 16)
                for bi in range(NB):
                    for ci in range(NCH):
                        _chunk_b1(b, d, c, p, PB, bi, ci)

        with ExitStack() as bsc2:
            q = {}

            def qsb(n, sh, dt=F32):
                q[n] = bsc2.enter_context(nc.sbuf_tensor(n, sh, dt))
                return q[n]

            qsb("wdq", [128, 32, DM], BF16)
            qsb("gu_tm", [128, 2, FI], BF16)
            qsb("guq_tm", [128, FI], BF16)
            qsb("qtmp", [128, FI], F32)
            qsb("guq_fm", [128, 32, 128], BF16)
            qsb("u_b", [128, 2, DM], F32)
            qsb("s_b", [128, 2, DM], F32)
            qsb("g_b", [128, DM], F32)
            qsb("o_b", [128, DM], F32)
            qsb("scol2", [128, 4], F32)
            with nc.Block("main_b2") as blk:
                b.blk = blk

                @b.D
                def _(sync):
                    b.wait(sync, "sv")
                    b.wait(sync, "ss")
                    b.wait(sync, "sd")
                    for k in range(32):
                        b.inc(sync.dma_start(q["wdq"][:, k, :], d["sc_wqd"][k]), "sd", 16)
                for bi in range(NB):
                    for ci in range(NCH):
                        _chunk_b2(b, d, c, q, PB, bi, ci)

            with nc.Block("fin") as blk:
                @blk.sync
                def _(sync):
                    b.wait(sync, "sd")
    return nc


def _prologue(b, d, c, a, PB):
    @b.D
    def _(sync):
        for dst, key in [(c["c_L"], "c_L"), (c["c_shift"], "c_shift"),
                         (c["c_mask"], "c_mask"), (c["c_ones"], "c_ones"),
                         (c["c_e0"], "c_e0"), (c["c_idf32"], "c_idf32"),
                         (c["c_idbf"], "c_idbf"), (c["c_cols"], "c_cols"),
                         (c["c_colsbf"], "c_colsbf"), (c["dtb"], "dtb_rep"),
                         (c["bnw"], "bnw_rep"), (c["cnw"], "cnw_rep"),
                         (c["drep"], "d_rep")]:
            b.inc(sync.dma_start(dst[:], d[key]), "sd", 16)
        b.inc(sync.dma_start(a["wtmp"][0:1, 0:NH * NS], d["bbias_row"]), "sd", 16)

    for key, dstname in [("cbias_row", "bias_b"), (None, "bias_c")]:
        for j in range(8):
            @b.T
            def _(tensor, j=j):
                b.wait(tensor, "sd")
                b.wait(tensor, "sv")
                tensor.matmul(PB[j % 2][:], c["c_ones"][0:1, :],
                              a["wtmp"][0:1, j * 512:(j + 1) * 512],
                              start=True, stop=True)
                b.inc(tensor.drain(), "st")

            @b.V
            def _(vector, dstname=dstname, j=j):
                b.wait(vector, "st")
                vector.tensor_copy(a[dstname][:, j * 512:(j + 1) * 512], PB[j % 2][:])
                b.inc(vector.drain(), "sv")

        if key is not None:
            @b.D
            def _(sync, key=key):
                b.wait(sync, "st")
                b.inc(sync.dma_start(a["wtmp"][0:1, 0:NH * NS], d[key]), "sd", 16)

    for k in range(8):
        @b.D
        def _(sync, k=k):
            b.wait(sync, "sv")
            b.inc(sync.dma_start(a["wtmp"][:], d["w_inT"][k * 128:(k + 1) * 128, :]),
                  "sd", 16)

        @b.V
        def _(vector, k=k):
            b.wait(vector, "sd")
            vector.tensor_copy(a["winT"][:, k, :], a["wtmp"][:])
            b.inc(vector.drain(), "sv")
    for k in range(16):
        @b.D
        def _(sync, k=k):
            b.wait(sync, "sv")
            b.inc(sync.dma_start(a["wtmp"][:, 0:DM],
                                 d["w_outT"][k * 128:(k + 1) * 128, :]), "sd", 16)

        @b.V
        def _(vector, k=k):
            b.wait(vector, "sd")
            vector.tensor_copy(a["woutT"][:, k, :], a["wtmp"][:, 0:DM])
            b.inc(vector.drain(), "sv")

    for wi, (dkey, skey, nkw, wid) in enumerate(
            [("w_gT", "sc_wqg", 8, FI), ("w_uT", "sc_wqu", 8, FI),
             ("w_dT", "sc_wqd", 32, DM)]):
        @b.V
        def _(vector):
            vector.memset(a["acol"][:, 0:1], 0.0)
            b.inc(vector.drain(), "sv")
        for k in range(nkw):
            @b.D
            def _(sync, k=k, dkey=dkey, wid=wid):
                b.wait(sync, "sv")
                b.inc(sync.dma_start(a["wtmp"][:, 0:wid],
                                     d[dkey][k * 128:(k + 1) * 128, :]), "sd", 16)

            @b.V
            def _(vector, wid=wid):
                b.wait(vector, "sd")
                vector.tensor_reduce(a["acol"][:, 1:2], a["wtmp"][:, 0:wid], AX.X,
                                     OP.add, apply_absolute_value=True)
                vector.drain()
                vector.tensor_tensor(a["acol"][:, 0:1], a["acol"][:, 0:1],
                                     a["acol"][:, 1:2], OP.add)
                b.inc(vector.drain(), "sv")

        @b.T
        def _(tensor):
            b.wait(tensor, "sv")
            tensor.matmul(PB[2][0:1, 0:1], c["c_cols"][:, 0:1], a["acol"][:, 0:1],
                          start=True, stop=True)
            b.inc(tensor.drain(), "st")

        @b.V
        def _(vector, nkw=nkw, wid=wid):
            b.wait(vector, "st")
            n_el = float(nkw * 128 * wid)
            vector.tensor_scalar(a["row2"][0:1, 0:1], PB[2][0:1, 0:1], 1.0 / n_el,
                                 1e-5, OP.mult, OP.max)
            vector.drain()
            vector.reciprocal(a["row2"][0:1, 1:2], a["row2"][0:1, 0:1])
            b.inc(vector.drain(), "sv")

        @b.T
        def _(tensor):
            b.wait(tensor, "sv")
            tensor.matmul(PB[2][:, 0:2], c["c_ones"][0:1, :], a["row2"][0:1, 0:2],
                          start=True, stop=True)
            b.inc(tensor.drain(), "st")

        @b.V
        def _(vector, wi=wi):
            b.wait(vector, "st")
            vector.tensor_copy(c["wsc"][:, 3 + wi:4 + wi], PB[2][:, 0:1])
            vector.tensor_copy(c["wsc"][:, wi:wi + 1], PB[2][:, 1:2])
            b.inc(vector.drain(), "sv")

        for k in range(nkw):
            @b.D
            def _(sync, k=k, dkey=dkey, wid=wid):
                b.wait(sync, "sv")
                b.inc(sync.dma_start(a["wtmp"][:, 0:wid],
                                     d[dkey][k * 128:(k + 1) * 128, :]), "sd", 16)

            @b.V
            def _(vector, wi=wi, wid=wid):
                b.wait(vector, "sd")
                vector.tensor_scalar(a["wtmp"][:, 0:wid], a["wtmp"][:, 0:wid],
                                     c["wsc"][:, wi:wi + 1], MAGIC, OP.mult, OP.add)
                vector.tensor_scalar(a["wtmp"][:, 0:wid], a["wtmp"][:, 0:wid],
                                     MAGIC, None, OP.subtract)
                vector.tensor_scalar(a["wqtmp"][:, 0:wid], a["wtmp"][:, 0:wid],
                                     -1.0, 1.0, OP.max, OP.min)
                b.inc(vector.drain(), "sv")

            @b.D
            def _(sync, skey=skey, k=k, wid=wid):
                b.wait(sync, "sv")
                b.inc(sync.dma_start(d[skey][k, :, :], a["wqtmp"][:, 0:wid]), "sd", 16)


def _batch_reset(b, a):
    @b.V
    def _(vector):
        b.wait(vector, "st")
        vector.memset(a["Hst"][:], 0.0)
        vector.memset(a["carry"][:], 0.0)
        b.inc(vector.drain(), "sv")


def _chunk_a(b, d, c, a, PB, bi, ci):
    cur = ci % 2
    prev = 1 - cur
    onesbf = c["c_colsbf"][:, 0:1]
    e127 = c["c_cols"][:, 1:2]
    bh_cur = a["bh_all"][:, cur]

    mk = {}

    def plan(ch, names, step=1):
        v = b.cnt[ch]
        for n in names:
            v += step
            mk[n] = v

    vnames = (["vLN1", "vLN2", "vXN"] + [f"vIP{s}" for s in range(4)] +
              ["vSM1", "vSM2", "vCS1", "vQW", "vXM", "vG", "vCAR0", "vRS0",
               "vRC0", "vCAR1", "vRS1", "vRC1", "vRMB", "vBB", "vRMC", "vCB"])
    vnames += ["vCSB0", "vTRE0"]
    for g in range(4):
        vnames += [f"vE{g}", f"vPS{g}"]
        if g < 3:
            vnames += [f"vCSB{g + 1}", f"vTRE{g + 1}"]
        if ci > 0:
            vnames += [f"vHB{g}"]
        vnames += [f"vYEV{g}", f"vYSM{g}"]
    vnames += ["vSEV"]
    plan("sv", vnames)

    tnames = ["tLN", "tBC"] + [f"tIP{s}" for s in range(9)] + ["tCS", "tTH0", "tTH1"]
    tnames += ["tTR0", "tSC0", "tTR1"]
    for g in range(4):
        if ci > 0:
            tnames += [f"tBD{g}"]
        tnames += [f"tY{g}"]
        if g < 3:
            tnames += [f"tSC{g + 1}"]
        tnames += [f"tYT{g}"]
        if g < 2:
            tnames += [f"tTR{g + 2}"]
    tnames += ["tOP"]
    plan("st", tnames)
    snames = (["sLN0", "sLN"] + [f"sIP{s}" for s in range(4, 9)] +
              ["sSM", "sCS", "sQH", "sS0", "sC0", "sS1", "sC1", "sZS",
               "sRB", "sRC"] + [f"sE{g}" for g in range(4)])
    plan("ss", snames)
    plan("sd", ["dU", "dXN", "dS"], step=16)
    prev_mk = getattr(b, "prev_a_marks", {})

    def W(eng, ch, name):
        if name in mk:
            eng.wait_ge(b.sem[ch], mk[name])
        elif name in prev_mk:
            eng.wait_ge(b.sem[ch], prev_mk[name])

    @b.D
    def _(sync):
        b.wait(sync, "sd", mk["dU"] - 16)
        src = d["u_fm"][bi].rearrange("(k p) t -> p k t", p=128)[
            :, :, ci * 128:(ci + 1) * 128]
        sync.dma_start(a["ub32"][:], src).then_inc(b.sem["sd"], 16)
        sync.wait_ge(b.sem["sv"], mk["vXN"])
        sync.dma_start(d["sc_xn"][bi, ci],
                       a["xnb"][:].rearrange("p k t -> p (k t)")
                       ).then_inc(b.sem["sd"], 16)
        sync.wait_ge(b.sem["sv"], mk["vSEV"])
        sync.dma_start(d["sc_s"][bi, ci],
                       a["ub32"][:].rearrange("p k t -> p (k t)")
                       ).then_inc(b.sem["sd"], 16)
    b.cnt["sd"] = mk["dS"]

    @b.T
    def _(tensor):
        s_ = b.sem
        tensor.wait_ge(s_["sd"], mk["dU"])
        tensor.wait_ge(s_["ss"], mk["sLN0"])
        for k in range(8):
            tensor.matmul(PB[4][0:1, 0:128], onesbf, a["PSb"][:, k, :],
                          start=(k == 0), stop=(k == 7))
        for k in range(8):
            tensor.matmul(PB[4][0:1, 128:256], onesbf, a["xnb"][:, k, :],
                          start=(k == 0), stop=(k == 7))
        tensor.drain().then_inc(s_["st"], 1)
        tensor.wait_ge(s_["ss"], mk["sLN"])
        tensor.matmul(PB[5][:, 0:128], c["c_ones"][0:1, :], a["stat"][0:1, 0:128],
                      start=True, stop=True)
        tensor.matmul(PB[5][:, 128:256], c["c_ones"][0:1, :],
                      a["stat"][0:1, 128:256], start=True, stop=True)
        tensor.drain().then_inc(s_["st"], 1)
        tensor.wait_ge(s_["sv"], mk["vXN"])
        for sl in range(9):
            ncol = 512 if sl < 8 else 384
            if 2 <= sl < 6:
                tensor.wait_ge(s_["sv"], mk[f"vIP{sl - 2}"])
            elif sl >= 6:
                tensor.wait_ge(s_["ss"], mk[f"sIP{sl - 2}"])
            for k in range(8):
                tensor.matmul(PB[sl % 2][:, 0:ncol], a["xnb"][:, k, :],
                              a["winT"][:, k, sl * 512:sl * 512 + ncol],
                              start=(k == 0), stop=(k == 7))
            tensor.drain().then_inc(s_["st"], 1)
        tensor.wait_ge(s_["sv"], mk["vSM2"])
        tensor.matmul(PB[6][:, 0:32], c["c_L"][:], a["adt"][:], start=True, stop=True)
        tensor.matmul(PB[6][:, 32:64], c["c_shift"][:], a["dt1ml"][:], start=True,
                      stop=True)
        tensor.matmul(PB[6][:, 64:96], c["c_ones"][:], a["adt"][:], start=True,
                      stop=True)
        tensor.matmul(PB[6][:, 96:128], c["c_e0"][:], a["dt1ml"][:], start=True,
                      stop=True)
        tensor.drain().then_inc(s_["st"], 1)
        Gflat = a["Gbuf"][:].rearrange("p h q -> p (h q)")
        tensor.wait_ge(s_["sv"], mk["vG"])
        for j in range(2):
            tensor.matmul(PB[2 + j][:], c["c_ones"][0:1, :],
                          a["carry"][0:1, j * 512:(j + 1) * 512], start=True,
                          stop=False)
            tensor.matmul(PB[2 + j][:], c["c_L"][:], Gflat[:, j * 512:(j + 1) * 512],
                          start=False, stop=True)
        tensor.matmul(PB[7][0:1, :], c["c_cols"][:, 0:1], Gflat[:, 0:512],
                      start=True, stop=True)
        tensor.drain().then_inc(s_["st"], 1)
        tensor.wait_ge(s_["sv"], mk["vCAR0"])
        tensor.matmul(PB[7][0:1, :], c["c_cols"][:, 0:1], Gflat[:, 512:1024],
                      start=True, stop=True)
        tensor.drain().then_inc(s_["st"], 1)
        pbf6 = PB[6][:].bitcast(BF16)
        pbf7 = PB[7][:].bitcast(BF16)

        def t_tr(g):
            hg = g * 8
            if g == 0:
                tensor.wait_ge(s_["sv"], mk["vCB"])
            elif g >= 2:
                tensor.wait_ge(s_["sv"], mk[f"vYSM{g - 2}"])
            for i in range(8):
                tensor.transpose(pbf6[:, i * 128:(i + 1) * 128], bh_cur[:, hg + i, :],
                                 c["c_idbf"][:])
            for i in range(8):
                tensor.transpose(pbf7[:, i * 128:(i + 1) * 128],
                                 a["ch_all"][:, hg + i, :], c["c_idbf"][:])
            tensor.drain().then_inc(s_["st"], 1)

        def t_sc(g):
            tensor.wait_ge(s_["sv"], mk[f"vTRE{g}"])
            if g > 0:
                tensor.wait_ge(s_["sv"], mk[f"vYEV{g - 1}"])
            for i in range(8):
                tensor.matmul(PB[2 + i // 4][:, (i % 4) * 128:(i % 4) * 128 + 128],
                              a["bh_sm"][:, g % 2, i, :], a["ch_sm"][:, g % 2, i, :],
                              start=True, stop=True)
            for i in range(8):
                tensor.transpose(PB[4 + i // 4][:, (i % 4) * 128:(i % 4) * 128 + 128],
                                 a["Ebuf"][:, i, :], c["c_idf32"][:])
            tensor.drain().then_inc(s_["st"], 1)

        def t_y(g):
            hg = g * 8
            if ci > 0:
                tensor.wait_ge(s_["sv"], mk[f"vPS{g}"])
                for i in range(8):
                    tensor.matmul(PB[2][:, i * 64:(i + 1) * 64],
                                  a["bh_all"][:, prev, hg + i, :],
                                  a["xmaskE"][:, hg + i, :], start=True, stop=True)
                tensor.drain().then_inc(s_["st"], 1)
                tensor.wait_ge(s_["sv"], mk[f"vHB{g}"])
            else:
                tensor.wait_ge(s_["sv"], mk[f"vPS{g}"])
            for i in range(8):
                tensor.matmul(PB[4][:, i * 64:(i + 1) * 64], a["PSb"][:, i, :],
                              a["x_tm"][:, hg + i, :], start=True, stop=True)
            for i in range(8):
                tensor.matmul(PB[5][:, i * 64:(i + 1) * 64], a["ch_sm"][:, g % 2, i, :],
                              a["Hst"][:, (hg + i) * HP:(hg + i + 1) * HP],
                              start=True, stop=True)
            for i in range(8):
                tensor.matmul(PB[2][:, i * 64:(i + 1) * 64], bh_cur[:, hg + i, :],
                              a["xqh"][:, hg + i, :], start=True, stop=True)
            tensor.drain().then_inc(s_["st"], 1)

        def t_yt(g):
            tensor.wait_ge(s_["sv"], mk[f"vYEV{g}"])
            for j in range(4):
                tensor.transpose(pbf6[:, j * 128:(j + 1) * 128],
                                 a["ybar_g"][:, j * 128:(j + 1) * 128], c["c_idbf"][:])
            tensor.drain().then_inc(s_["st"], 1)

        def t_op(g):
            tensor.wait_ge(s_["sv"], mk[f"vYSM{g}"])
            for j in range(4):
                kin = g * 4 + j
                for sx in range(2):
                    tensor.matmul(PB[sx][:, 0:512], a["ybar_sm"][:, j, :],
                                  a["woutT"][:, kin, sx * 512:(sx + 1) * 512],
                                  start=(kin == 0), stop=(kin == 15))

        t_tr(0)
        t_sc(0)
        t_tr(1)
        t_y(0)
        t_sc(1)
        t_yt(0)
        t_tr(2)
        t_y(1)
        t_op(0)
        t_sc(2)
        t_yt(1)
        t_tr(3)
        t_y(2)
        t_op(1)
        t_sc(3)
        t_yt(2)
        t_y(3)
        t_op(2)
        t_yt(3)
        t_op(3)
        tensor.drain().then_inc(s_["st"], 1)
    b.cnt["st"] = mk["tOP"]

    @b.V
    def _(vector):
        s_ = b.sem
        vector.wait_ge(s_["sd"], mk["dU"])
        W(vector, "st", "p_tY3")
        W(vector, "ss", "p_sE3")
        vector.drain().then_inc(s_["sv"], 1)
        vector.wait_ge(s_["st"], mk["tLN"])
        vector.tensor_scalar(a["stat"][0:1, 0:128], PB[4][0:1, 0:128], 1.0 / DM,
                             None, OP.mult)
        vector.tensor_scalar(a["stat"][0:1, 128:256], PB[4][0:1, 128:256], 1.0 / DM,
                             None, OP.mult)
        vector.drain()
        vector.tensor_tensor(a["smt"][0:1, 0:128], a["stat"][0:1, 0:128],
                             a["stat"][0:1, 0:128], OP.mult)
        vector.drain()
        vector.tensor_tensor(a["stat"][0:1, 128:256], a["stat"][0:1, 128:256],
                             a["smt"][0:1, 0:128], OP.subtract)
        vector.drain().then_inc(s_["sv"], 1)
        vector.wait_ge(s_["st"], mk["tBC"])
        mrep = PB[5][:, 0:128].unsqueeze(1).broadcast_to([128, 8, 128])
        rrep = PB[5][:, 128:256].unsqueeze(1).broadcast_to([128, 8, 128])
        vector.tensor_tensor(a["ub32"][:], a["ub32"][:], mrep, OP.subtract)
        vector.tensor_tensor(a["xnb"][:], a["ub32"][:], rrep, OP.mult)
        vector.drain().then_inc(s_["sv"], 1)
        for sl in range(4):
            vector.wait_ge(s_["st"], mk[f"tIP{sl}"])
            vector.tensor_copy(a["zs"][:, sl * 512:(sl + 1) * 512], PB[sl % 2][:])
            vector.drain().then_inc(s_["sv"], 1)
        vector.wait_ge(s_["ss"], mk["sIP8"])
        vector.tensor_tensor(a["dt_"][:], a["sml"][:, 256:288], c["dtb"][:], OP.add)
        vector.drain()
        vector.scalar_tensor_tensor(a["smt"][:, 0:32], a["dt_"][:], -1.0, a["dt_"][:],
                                    OP.mult, OP.min)
        vector.scalar_tensor_tensor(a["smt"][:, 32:64], a["sml"][:, 288:320],
                                    -1.0, a["sml"][:, 288:320], OP.mult, OP.min)
        vector.drain().then_inc(s_["sv"], 1)
        vector.wait_ge(s_["ss"], mk["sSM"])
        vector.tensor_scalar(a["smt"][:, 96:128], a["dt_"][:], 0.0, None, OP.max)
        vector.tensor_scalar(a["Gbuf"][:, 0, 0:32], a["sml"][:, 288:320], 0.0, None,
                             OP.max)
        vector.tensor_scalar(a["lam"][:], a["smt"][:, 64:96], 1.0, None, OP.add)
        vector.drain()
        vector.reciprocal(a["lam"][:], a["lam"][:])
        vector.drain()
        vector.tensor_tensor(a["dt_"][:], a["smt"][:, 96:128], a["smt"][:, 0:32],
                             OP.add)
        vector.tensor_tensor(a["adt"][:], a["Gbuf"][:, 0, 0:32], a["smt"][:, 32:64],
                             OP.add)
        vector.tensor_scalar(a["smt"][:, 96:128], a["lam"][:], -1.0, 1.0, OP.mult,
                             OP.add)
        vector.drain()
        vector.tensor_scalar(a["adt"][:], a["adt"][:], -1.0, -1e-4, OP.mult, OP.min)
        vector.tensor_tensor(a["dtlam"][:], a["dt_"][:], a["lam"][:], OP.mult)
        vector.tensor_tensor(a["dt1ml"][:], a["dt_"][:], a["smt"][:, 96:128],
                             OP.mult)
        vector.drain()
        vector.tensor_tensor(a["adt"][:], a["adt"][:], a["dt_"][:], OP.mult)
        vector.tensor_tensor(a["dtlamD"][:], a["dtlam"][:], c["drep"][:], OP.add)
        vector.drain().then_inc(s_["sv"], 1)
        vector.wait_ge(s_["st"], mk["tCS"])
        vector.tensor_copy(a["cs_tm"][:], PB[6][:, 0:32])
        vector.tensor_tensor(a["wvec"][:], a["dtlam"][:], PB[6][:, 32:64], OP.add)
        vector.drain()
        vector.tensor_scalar(a["wvec"][:], a["wvec"][:], 1e-30, None, OP.max)
        vector.drain().then_inc(s_["sv"], 1)
        vector.wait_ge(s_["ss"], mk["sCS"])
        vector.tensor_tensor(a["qw"][:], a["qw"][:], a["cs_tm"][:], OP.subtract)
        vector.drain()
        vector.tensor_tensor(a["smt"][:, 0:32], a["qw"][:], PB[6][:, 64:96], OP.add)
        vector.drain().then_inc(s_["sv"], 1)
        vector.wait_ge(s_["ss"], mk["sQH"])
        qrep = a["qh_e"][:].unsqueeze(2).broadcast_to([128, NH, HP])
        vector.tensor_tensor(a["xqh"][:], a["x_tm"][:], qrep, OP.mult)
        if ci > 0:
            drep0 = PB[6][:, 96:128].unsqueeze(2).broadcast_to([128, NH, HP])
            vector.tensor_tensor(a["xmaskE"][:], a["xmaskE"][:], drep0, OP.mult)
        vector.drain().then_inc(s_["sv"], 1)
        dtr = a["dt_"][:].unsqueeze(2).broadcast_to([128, NH, NA])
        angr = a["sml"][:, 352:384].unsqueeze(1).broadcast_to([128, NH, NA])
        vector.tensor_tensor(a["Gbuf"][:], dtr, angr, OP.mult)
        vector.drain().then_inc(s_["sv"], 1)
        Gflat = a["Gbuf"][:].rearrange("p h q -> p (h q)")
        for j in range(2):
            gsl = Gflat[:, j * 512:(j + 1) * 512]
            vector.wait_ge(s_["st"], mk["tTH0" if j == 0 else "tTH1"])
            vector.tensor_tensor(a["carry"][0:1, j * 512:(j + 1) * 512],
                                 a["carry"][0:1, j * 512:(j + 1) * 512],
                                 PB[7][0:1, :], OP.add)
            vector.drain().then_inc(s_["sv"], 1)
            vector.tensor_scalar(gsl, PB[2 + j][:], 1.0 / (2 * PI), MAGIC, OP.mult,
                                 OP.add)
            vector.tensor_scalar(gsl, gsl, MAGIC, 2 * PI, OP.subtract, OP.mult)
            vector.tensor_tensor(gsl, PB[2 + j][:], gsl, OP.subtract)
            vector.drain().then_inc(s_["sv"], 1)
            vector.wait_ge(s_["ss"], mk[f"sS{j}"])
            vector.tensor_scalar(gsl, PB[2 + j][:], 1.0 / (2 * PI), 0.25, OP.mult,
                                 OP.add)
            vector.tensor_scalar(gsl, gsl, MAGIC, MAGIC, OP.add, OP.subtract)
            vector.tensor_scalar(gsl, gsl, 2 * PI, PI / 2, OP.mult, OP.subtract)
            vector.tensor_tensor(gsl, PB[2 + j][:], gsl, OP.subtract)
            vector.drain().then_inc(s_["sv"], 1)
        for src0, nwkey, biaskey in [(0, "bnw", "bias_b"), (128, "cnw", "bias_c")]:
            vector.tensor_tensor(a["Ebuf"][:, 0, :], a["sml"][:, src0:src0 + 128],
                                 a["sml"][:, src0:src0 + 128], OP.mult)
            vector.drain()
            vector.tensor_reduce(a["acol"][:, 0:1], a["Ebuf"][:, 0, :], AX.X, OP.add)
            vector.drain().then_inc(s_["sv"], 1)
            vector.wait_ge(s_["ss"], mk["sRB" if src0 == 0 else "sRC"])
            if src0 == 0:
                W(vector, "st", "p_tOP")
                dst = a["bh_all"][:, cur]
            else:
                dst = a["ch_all"][:]
            vector.scalar_tensor_tensor(a["Ebuf"][:, 0, :],
                                        a["sml"][:, src0:src0 + 128],
                                        a["acol"][:, 0:1], c[nwkey][:], OP.mult,
                                        OP.mult)
            vector.drain()
            nrep = a["Ebuf"][:, 0, :].unsqueeze(1).broadcast_to([128, NH, NS])
            brep = a[biaskey][:].rearrange("p (h n) -> p h n", h=NH)
            vector.tensor_tensor(dst, nrep, brep, OP.add)
            vector.drain()
            rt = a["Ebuf"][:].rearrange("p g t -> p (g t)").bitcast(BF16)
            v1 = dst[:, :, 0:NA]
            v2 = dst[:, :, NA:2 * NA]
            sinr = a["sin_t"][:]
            cosr = a["cos_t"][:]
            rt0 = rt[:, 0:1024].rearrange("p (h q) -> p h q", h=NH)
            rt1 = rt[:, 1024:2048].rearrange("p (h q) -> p h q", h=NH)
            vector.tensor_tensor(rt0, v2, sinr, OP.mult)
            vector.tensor_tensor(v2, v2, cosr, OP.mult)
            vector.tensor_tensor(rt1, v1, sinr, OP.mult)
            vector.tensor_tensor(v2, rt1, v2, OP.add)
            vector.tensor_tensor(v1, v1, cosr, OP.mult)
            vector.tensor_tensor(v1, v1, rt0, OP.subtract)
            vector.drain().then_inc(s_["sv"], 1)
        def v_csb(g):
            hg = g * 8
            for i in range(8):
                vector.tensor_scalar(a["Ebuf"][:, i, :], c["c_ones"][:],
                                     a["cs_tm"][:, hg + i:hg + i + 1], None, OP.mult)
            vector.drain().then_inc(s_["sv"], 1)

        def v_tre(g):
            vector.wait_ge(s_["st"], mk[f"tTR{g}"])
            vector.tensor_copy(a["bh_sm"][:, g % 2].rearrange("p i t -> p (i t)"),
                               PB[6][:].bitcast(BF16))
            vector.tensor_copy(a["ch_sm"][:, g % 2].rearrange("p i t -> p (i t)"),
                               PB[7][:].bitcast(BF16))
            vector.drain().then_inc(s_["sv"], 1)

        v_csb(0)
        v_tre(0)
        for g in range(4):
            hg = g * 8
            vector.wait_ge(s_["st"], mk[f"tSC{g}"])
            for i in range(8):
                vector.scalar_tensor_tensor(
                    a["Ebuf"][:, i, :],
                    PB[4 + i // 4][:, (i % 4) * 128:(i % 4) * 128 + 128],
                    a["qw"][:, hg + i:hg + i + 1], c["c_mask"][:], OP.add, OP.add)
            vector.drain().then_inc(s_["sv"], 1)
            vector.wait_ge(s_["ss"], mk[f"sE{g}"])
            irep = c["c_idbf"][:].unsqueeze(1).broadcast_to([128, 8, 128])
            drep8 = a["dtlam"][:, hg:hg + 8].unsqueeze(2).broadcast_to([128, 8, 128])
            vector.tensor_tensor(a["dgbuf"][:], irep, drep8, OP.mult)
            vector.drain()
            pp = a["PSb"][:].rearrange("p i t -> p (i t)")
            dg = a["dgbuf"][:].rearrange("p i t -> p (i t)")
            vector.tensor_tensor(pp[:, 0:512], pp[:, 0:512], dg[:, 0:512], OP.add)
            vector.tensor_tensor(pp[:, 512:1024], pp[:, 512:1024], dg[:, 512:1024],
                                 OP.add)
            vector.drain()
            vector.tensor_tensor(pp[:, 0:512], PB[2][:], pp[:, 0:512], OP.mult)
            vector.tensor_tensor(pp[:, 512:1024], PB[3][:], pp[:, 512:1024], OP.mult)
            vector.drain().then_inc(s_["sv"], 1)
            if g < 3:
                v_csb(g + 1)
                v_tre(g + 1)
            if ci > 0:
                vector.wait_ge(s_["st"], mk[f"tBD{g}"])
                hsl0 = a["Hst"][:, hg * HP:(hg + 8) * HP]
                vector.tensor_tensor(hsl0, hsl0, PB[2][:], OP.add)
                vector.drain().then_inc(s_["sv"], 1)
            vector.wait_ge(s_["st"], mk[f"tY{g}"])
            arep = a["alpha"][:, hg:hg + 8].unsqueeze(2).broadcast_to([128, 8, HP])
            e13 = a["Gbuf"][:].rearrange("p h q -> p (h q)")[:, 0:512].rearrange(
                "p (i q) -> p i q", i=8)
            yb3 = a["ybar_g"][:].rearrange("p (i q) -> p i q", i=8)
            pb5_3 = PB[5][:].rearrange("p (i q) -> p i q", i=8)
            pb4_3 = PB[4][:].rearrange("p (i q) -> p i q", i=8)
            pb2_3 = PB[2][:].rearrange("p (i q) -> p i q", i=8)
            vector.tensor_tensor(e13, pb5_3, arep, OP.mult)
            vector.tensor_tensor(e13, e13, pb4_3, OP.add)
            drp = c["drep"][:, hg:hg + 8].unsqueeze(2).broadcast_to([128, 8, HP])
            xsl = a["x_tm"][:, hg:hg + 8, :]
            vector.tensor_tensor(yb3, xsl, drp, OP.mult)
            vector.tensor_tensor(e13, e13, yb3, OP.add)
            zs3 = a["zs"][:, hg * HP:(hg + 8) * HP].rearrange("p (i q) -> p i q", i=8)
            vector.tensor_tensor(yb3, e13, zs3, OP.mult)
            dcr = a["decay"][:, hg:hg + 8].unsqueeze(2).broadcast_to([128, 8, HP])
            hsl = a["Hst"][:, hg * HP:(hg + 8) * HP].rearrange("p (i q) -> p i q", i=8)
            vector.tensor_tensor(e13, hsl, dcr, OP.mult)
            vector.tensor_tensor(hsl, e13, pb2_3, OP.add)
            vector.drain().then_inc(s_["sv"], 1)
            vector.wait_ge(s_["st"], mk[f"tYT{g}"])
            vector.tensor_copy(a["ybar_sm"][:].rearrange("p j t -> p (j t)"),
                               PB[6][:].bitcast(BF16)[:, 0:512])
            vector.drain().then_inc(s_["sv"], 1)
        vector.wait_ge(s_["st"], mk["tOP"])
        s_ev = a["ub32"][:].rearrange("p k t -> p (k t)")
        vector.tensor_copy(s_ev[:, 0:512], PB[0][:])
        vector.tensor_copy(s_ev[:, 512:1024], PB[1][:])
        vector.tensor_tensor(
            a["xmaskE"][:].rearrange("p h q -> p (h q)"),
            a["x_tm"][:].rearrange("p h q -> p (h q)"),
            e127.broadcast_to([128, NH * HP]), OP.mult)
        vector.drain().then_inc(s_["sv"], 1)
    b.cnt["sv"] = mk["vSEV"]

    @b.S
    def _(scalar):
        s_ = b.sem
        scalar.wait_ge(s_["sv"], mk["vLN1"])
        scalar.copy(a["PSb"][:].rearrange("p k t -> p (k t)"),
                    a["ub32"][:].rearrange("p k t -> p (k t)"))
        scalar.activation(a["xnb"][:].rearrange("p k t -> p (k t)"),
                          a["ub32"][:].rearrange("p k t -> p (k t)"), AF.Square)
        scalar.drain().then_inc(s_["ss"], 1)
        scalar.wait_ge(s_["sv"], mk["vLN2"])
        scalar.activation(a["smt"][0:1, 0:128], a["stat"][0:1, 128:256], AF.Ln,
                          bias=1e-5)
        scalar.activation(a["stat"][0:1, 128:256], a["smt"][0:1, 0:128], AF.Exp,
                          scale=-0.5)
        scalar.drain().then_inc(s_["ss"], 1)
        for sl in range(4, 9):
            scalar.wait_ge(s_["st"], mk[f"tIP{sl}"])
            if sl < 8:
                scalar.copy(a["x_tm"][:].rearrange("p h q -> p (h q)")[
                    :, (sl - 4) * 512:(sl - 3) * 512], PB[sl % 2][:])
            else:
                scalar.copy(a["sml"][:], PB[sl % 2][:, 0:384])
            scalar.drain().then_inc(s_["ss"], 1)
        scalar.wait_ge(s_["sv"], mk["vSM1"])
        scalar.activation(a["smt"][:, 64:128], a["smt"][:, 0:64], AF.Exp)
        scalar.activation(a["smt"][:, 0:64], a["smt"][:, 64:128], AF.Ln, bias=1.0)
        scalar.activation(a["smt"][:, 64:96], a["sml"][:, 320:352], AF.Exp,
                          scale=-1.0)
        scalar.drain().then_inc(s_["ss"], 1)
        scalar.wait_ge(s_["sv"], mk["vCS1"])
        scalar.activation(a["qw"][:], a["wvec"][:], AF.Ln)
        scalar.activation(a["alpha"][:], a["cs_tm"][:], AF.Exp)
        scalar.activation(a["decay"][:], PB[6][:, 64:96], AF.Exp)
        scalar.drain().then_inc(s_["ss"], 1)
        scalar.wait_ge(s_["sv"], mk["vQW"])
        scalar.activation(a["qh_e"][:], a["smt"][:, 0:32], AF.Exp)
        scalar.drain().then_inc(s_["ss"], 1)
        for j in range(2):
            gsl = a["Gbuf"][:].rearrange("p h q -> p (h q)")[:, j * 512:(j + 1) * 512]
            scalar.wait_ge(s_["sv"], mk[f"vRS{j}"])
            scalar.activation(
                a["sin_t"][:].rearrange("p h q -> p (h q)")[:, j * 512:(j + 1) * 512],
                gsl, AF.Sin)
            scalar.drain().then_inc(s_["ss"], 1)
            scalar.wait_ge(s_["sv"], mk[f"vRC{j}"])
            scalar.activation(
                a["cos_t"][:].rearrange("p h q -> p (h q)")[:, j * 512:(j + 1) * 512],
                gsl, AF.Sin)
            scalar.drain().then_inc(s_["ss"], 1)
        scalar.wait_ge(s_["sv"], mk["vIP3"])
        for jz in range(4):
            scalar.activation(a["zs"][:, jz * 512:(jz + 1) * 512],
                              a["zs"][:, jz * 512:(jz + 1) * 512], AF.Silu)
        scalar.drain().then_inc(s_["ss"], 1)
        for nm, wm in [("sRB", "vRMB"), ("sRC", "vRMC")]:
            scalar.wait_ge(s_["sv"], mk[wm])
            scalar.activation(a["acol"][:, 1:2], a["acol"][:, 0:1], AF.Ln,
                              scale=1.0 / NS, bias=1e-5)
            scalar.activation(a["acol"][:, 0:1], a["acol"][:, 1:2], AF.Exp,
                              scale=-0.5)
            scalar.drain().then_inc(s_["ss"], 1)
        for g in range(4):
            scalar.wait_ge(s_["sv"], mk[f"vE{g}"])
            ee = a["Ebuf"][:].rearrange("p i t -> p (i t)")
            pp = a["PSb"][:].rearrange("p i t -> p (i t)")
            scalar.activation(pp[:, 0:512], ee[:, 0:512], AF.Exp)
            scalar.activation(pp[:, 512:1024], ee[:, 512:1024], AF.Exp)
            scalar.drain().then_inc(s_["ss"], 1)
    b.cnt["ss"] = mk["sE3"]

    b.prev_a_marks = {"p_tY3": mk["tY3"], "p_sE3": mk["sE3"], "p_tOP": mk["tOP"]}

    b.tap("xnb", a["xnb"][:].rearrange("p k t -> p (k t)"), bi, ci)
    b.tap("xtm", a["x_tm"][:].rearrange("p h q -> p (h q)"), bi, ci)
    b.tap("zs", a["zs"][:], bi, ci)
    b.tap("sml", a["sml"][:], bi, ci)
    for nm in ("dt_", "adt", "lam", "wvec", "qw", "alpha", "qh_e", "decay",
               "cs_tm", "dtlam", "dt1ml"):
        b.tap(nm, a[nm][:], bi, ci)
    b.tap("sin", a["sin_t"][:].rearrange("p h q -> p (h q)"), bi, ci)
    b.tap("cos", a["cos_t"][:].rearrange("p h q -> p (h q)"), bi, ci)
    b.tap("bh", a["bh_all"][:, cur].rearrange("p h q -> p (h q)"), bi, ci)
    b.tap("ch", a["ch_all"][:].rearrange("p h q -> p (h q)"), bi, ci)
    b.tap("hst", a["Hst"][:], bi, ci)
    b.tap("sev", a["ub32"][:].rearrange("p k t -> p (k t)"), bi, ci)
    b.tap("carry", a["carry"][:], bi, ci)


def _chunk_b1(b, d, c, p, PB, bi, ci):
    mk = {}

    def plan(ch, names, step=1):
        v = b.cnt[ch]
        for n in names:
            v += step
            mk[n] = v

    plan("sv", ["vM8", "vSC", "vSR", "vXQ"] + [f"vGU{sl}" for sl in range(8)]
         + ["vGM"])
    plan("st", ["tM8", "tSR", "tSB"] + [x for sl in range(8)
                                        for x in (f"tG{sl}", f"tU{sl}")])
    plan("ss", [f"sG{sl}" for sl in range(8)])
    plan("sd", ["dXNF", "dGU"], step=16)

    prev_dGU = mk["dXNF"] - 16
    buf = (bi * NCH + ci) % 2
    prev_vXQ = getattr(b, "prev_b1_vXQ", 0)
    xnf = p["xnf"][:, buf]

    @b.D
    def _(sync):
        b.wait(sync, "sd", prev_dGU)
        sync.wait_ge(b.sem["sv"], prev_vXQ)
        sync.dma_start(xnf.rearrange("p k t -> p (k t)"),
                       d["sc_xn"][bi, ci]).then_inc(b.sem["sd"], 16)
        sync.wait_ge(b.sem["sv"], mk["vGU7"])
        sync.dma_start(d["sc_gu"][bi, ci], p["gu_b"][:]).then_inc(b.sem["sd"], 16)
    b.cnt["sd"] = mk["dGU"]
    b.prev_b1_vXQ = mk["vXQ"]

    @b.T
    def _(tensor):
        s_ = b.sem
        tensor.wait_ge(s_["sv"], mk["vM8"])
        tensor.transpose(PB[4][:, 0:128], p["m8"][:], c["c_idf32"][:])
        tensor.drain().then_inc(s_["st"], 1)
        tensor.wait_ge(s_["sv"], mk["vSC"])
        tensor.transpose(PB[4][0:1, 128:256], p["scol"][:, 1:2], c["c_idf32"][:])
        tensor.drain().then_inc(s_["st"], 1)
        tensor.wait_ge(s_["sv"], mk["vSR"])
        tensor.matmul(PB[5][:, 0:128], c["c_ones"][0:1, :], p["srow"][0:1, :],
                      start=True, stop=True)
        tensor.drain().then_inc(s_["st"], 1)
        tensor.wait_ge(s_["sv"], mk["vXQ"])
        for sl in range(8):
            if sl >= 2:
                tensor.wait_ge(s_["ss"], mk[f"sG{sl - 2}"])
            for k in range(8):
                tensor.matmul(PB[sl % 2][:], p["xq_fm"][:, k, :],
                              p["wgq"][:, k, sl * 512:(sl + 1) * 512],
                              start=(k == 0), stop=(k == 7))
            tensor.drain().then_inc(s_["st"], 1)
            if sl >= 2:
                tensor.wait_ge(s_["sv"], mk[f"vGU{sl - 2}"])
            for k in range(8):
                tensor.matmul(PB[2 + sl % 2][:], p["xq_fm"][:, k, :],
                              p["wuq"][:, k, sl * 512:(sl + 1) * 512],
                              start=(k == 0), stop=(k == 7))
            tensor.drain().then_inc(s_["st"], 1)
    b.cnt["st"] = mk["tU7"]
    prev_tU7 = getattr(b, "prev_b1_tU7", 0)
    b.prev_b1_tU7 = mk["tU7"]

    @b.V
    def _(vector):
        s_ = b.sem
        vector.wait_ge(s_["sd"], mk["dXNF"])
        vector.scalar_tensor_tensor(p["m8"][:], xnf[:, 0, :], -1.0,
                                    xnf[:, 0, :], OP.mult, OP.max)
        for k in range(1, 8):
            vector.scalar_tensor_tensor(p["mt"][:], xnf[:, k, :], -1.0,
                                        xnf[:, k, :], OP.mult, OP.max)
            vector.tensor_tensor(p["m8"][:], p["m8"][:], p["mt"][:], OP.max)
        vector.drain().then_inc(s_["sv"], 1)
        vector.wait_ge(s_["st"], mk["tM8"])
        vector.tensor_reduce(p["scol"][:, 0:1], PB[4][:, 0:128], AX.X, OP.max)
        vector.drain()
        vector.tensor_scalar(p["scol"][:, 0:1], p["scol"][:, 0:1], 1e-5, None,
                             OP.max)
        vector.drain()
        vector.reciprocal(p["scol"][:, 1:2], p["scol"][:, 0:1])
        vector.tensor_scalar(p["scol"][:, 2:3], p["scol"][:, 0:1],
                             c["wsc"][:, 3:4], 1.0 / 127.0, OP.mult, OP.mult)
        vector.tensor_scalar(p["scol"][:, 3:4], p["scol"][:, 0:1],
                             c["wsc"][:, 4:5], 1.0 / 127.0, OP.mult, OP.mult)
        vector.drain()
        vector.tensor_scalar(p["scol"][:, 1:2], p["scol"][:, 1:2], 127.0, None,
                             OP.mult)
        vector.drain().then_inc(s_["sv"], 1)
        vector.wait_ge(s_["st"], mk["tSR"])
        vector.tensor_copy(p["srow"][0:1, :], PB[4][0:1, 128:256])
        vector.drain().then_inc(s_["sv"], 1)
        vector.wait_ge(s_["st"], mk["tSB"])
        vector.wait_ge(s_["st"], prev_tU7)
        for k in range(8):
            vector.tensor_tensor(p["mt"][:], xnf[:, k, :], PB[5][:, 0:128],
                                 OP.mult)
            vector.tensor_scalar(p["xq_fm"][:, k, :], p["mt"][:], MAGIC, MAGIC,
                                 OP.add, OP.subtract)
        vector.drain().then_inc(s_["sv"], 1)
        for sl in range(8):
            vector.wait_ge(s_["st"], mk[f"tU{sl}"])
            vector.wait_ge(s_["ss"], mk[f"sG{sl}"])
            vector.scalar_tensor_tensor(
                p["gu_b"][:, sl * 512:(sl + 1) * 512], PB[2 + sl % 2][:],
                p["scol"][:, 3:4], p["gu_b"][:, sl * 512:(sl + 1) * 512],
                OP.mult, OP.mult)
            vector.drain().then_inc(s_["sv"], 1)
        vector.tensor_reduce(c["gumax"][:, bi * NCH + ci:bi * NCH + ci + 1],
                             p["gu_b"][:], AX.X, OP.max, apply_absolute_value=True)
        vector.drain().then_inc(s_["sv"], 1)
    b.cnt["sv"] = mk["vGM"]

    @b.S
    def _(scalar):
        s_ = b.sem
        scalar.wait_ge(s_["sd"], prev_dGU)
        for sl in range(8):
            scalar.wait_ge(s_["st"], mk[f"tG{sl}"])
            scalar.activation(p["gu_b"][:, sl * 512:(sl + 1) * 512],
                              PB[sl % 2][:], AF.Silu, scale=p["scol"][:, 2:3])
            scalar.drain().then_inc(s_["ss"], 1)
    b.cnt["ss"] = mk["sG7"]

    b.tap("xqfm", p["xq_fm"][:].rearrange("p k t -> p (k t)"), bi, ci)
    b.tap("gu", p["gu_b"][:], bi, ci)
    b.tap("b1scol", p["scol"][:], bi, ci)


def _chunk_b2(b, d, c, q, PB, bi, ci):
    mk = {}

    def plan(ch, names, step=1):
        v = b.cnt[ch]
        for n in names:
            v += step
            mk[n] = v

    plan("sv", ["vGQ"] + [f"vTQ{g4}" for g4 in range(4)] + ["vOB"])
    plan("st", [f"tTQ{g4}" for g4 in range(4)] + ["tD0", "tD1"])
    plan("ss", ["sD0", "sD1"])
    mk["dIN"] = b.cnt["sd"] + 48
    mk["dOUT"] = b.cnt["sd"] + 64

    buf = (bi * NCH + ci) % 2
    prev_vGQ = getattr(b, "prev_b2_vGQ", 0)
    prev_vOB = getattr(b, "prev_b2_vOB", 0)
    gu_tm = q["gu_tm"][:, buf]
    u_b = q["u_b"][:, buf]
    s_b = q["s_b"][:, buf]

    @b.D
    def _(sync):
        b.wait(sync, "sd", mk["dIN"] - 48)
        sync.wait_ge(b.sem["sv"], prev_vGQ)
        sync.dma_start(gu_tm, d["sc_gu"][bi, ci]).then_inc(b.sem["sd"], 16)
        sync.dma_start(u_b, d["u_tm"][bi, ci * 128:(ci + 1) * 128, :]
                       ).then_inc(b.sem["sd"], 16)
        sync.dma_start(s_b, d["sc_s"][bi, ci]).then_inc(b.sem["sd"], 16)
        sync.wait_ge(b.sem["sv"], mk["vOB"])
        sync.dma_start(d["out"][bi, ci * 128:(ci + 1) * 128, :], q["o_b"][:]
                       ).then_inc(b.sem["sd"], 16)
    b.cnt["sd"] = mk["dOUT"]
    b.prev_b2_vGQ = mk["vGQ"]
    b.prev_b2_vOB = mk["vOB"]

    @b.T
    def _(tensor):
        s_ = b.sem
        tensor.wait_ge(s_["sv"], mk["vGQ"])
        for g4 in range(4):
            if g4 >= 2:
                tensor.wait_ge(s_["sv"], mk[f"vTQ{g4 - 2}"])
            pbf = PB[4 + g4 % 2][:].bitcast(BF16)
            for i in range(8):
                tensor.transpose(
                    pbf[:, i * 128:(i + 1) * 128],
                    q["guq_tm"][:, (g4 * 8 + i) * 128:(g4 * 8 + i + 1) * 128],
                    c["c_idbf"][:])
            tensor.drain().then_inc(s_["st"], 1)
        tensor.wait_ge(s_["sv"], mk["vTQ3"])
        for sl in range(2):
            for k in range(32):
                tensor.matmul(PB[sl][:], q["guq_fm"][:, k, :],
                              q["wdq"][:, k, sl * 512:(sl + 1) * 512],
                              start=(k == 0), stop=(k == 31))
            tensor.drain().then_inc(s_["st"], 1)
    b.cnt["st"] = mk["tD1"]

    @b.V
    def _(vector):
        s_ = b.sem
        vector.wait_ge(s_["sd"], mk["dIN"])
        vector.tensor_scalar(q["scol2"][:, 0:1],
                             c["gumax"][:, bi * NCH + ci:bi * NCH + ci + 1],
                             1e-5, None, OP.max)
        vector.drain()
        vector.reciprocal(q["scol2"][:, 1:2], q["scol2"][:, 0:1])
        vector.tensor_scalar(q["scol2"][:, 2:3], q["scol2"][:, 0:1],
                             c["wsc"][:, 5:6], 1.0 / 127.0, OP.mult, OP.mult)
        vector.drain()
        vector.tensor_scalar(q["scol2"][:, 1:2], q["scol2"][:, 1:2], 127.0, None,
                             OP.mult)
        vector.drain()
        vector.tensor_tensor(q["qtmp"][:], gu_tm,
                             q["scol2"][:, 1:2].broadcast_to([128, FI]), OP.mult)
        vector.tensor_scalar(q["qtmp"][:], q["qtmp"][:], MAGIC, MAGIC, OP.add,
                             OP.subtract)
        vector.tensor_copy(q["guq_tm"][:], q["qtmp"][:])
        vector.drain().then_inc(s_["sv"], 1)
        for g4 in range(4):
            vector.wait_ge(s_["st"], mk[f"tTQ{g4}"])
            vector.tensor_copy(
                q["guq_fm"][:].rearrange("p k t -> p (k t)")[
                    :, g4 * 1024:(g4 + 1) * 1024], PB[4 + g4 % 2][:].bitcast(BF16))
            vector.drain().then_inc(s_["sv"], 1)
        vector.wait_ge(s_["ss"], mk["sD1"])
        b.wait(vector, "sd", mk["dOUT"] - 64)
        vector.tensor_tensor(q["o_b"][:], s_b, q["g_b"][:], OP.mult)
        vector.drain()
        vector.tensor_tensor(q["o_b"][:], q["o_b"][:], u_b, OP.add)
        vector.drain().then_inc(s_["sv"], 1)
    b.cnt["sv"] = mk["vOB"]

    @b.S
    def _(scalar):
        s_ = b.sem
        for sl in range(2):
            scalar.wait_ge(s_["st"], mk[f"tD{sl}"])
            scalar.activation(q["g_b"][:, sl * 512:(sl + 1) * 512], PB[sl][:],
                              AF.Silu, scale=q["scol2"][:, 2:3])
            scalar.drain().then_inc(s_["ss"], 1)
    b.cnt["ss"] = mk["sD1"]

    b.tap("guq", q["guq_tm"][:], bi, ci)
    b.tap("b2scol", q["scol2"][:], bi, ci)
    b.tap("gutm", gu_tm, bi, ci)
    b.tap("gb", q["g_b"][:], bi, ci)
    b.tap("ob", q["o_b"][:], bi, ci)


def _host_inputs(inputs, core):
    f = np.float32
    u = np.asarray(inputs["L0_attn_u"], f)[core * NB:(core + 1) * NB]
    ins = {
        "u_tm": np.ascontiguousarray(u),
        "u_fm": np.ascontiguousarray(u.transpose(0, 2, 1)),
        "w_inT": np.ascontiguousarray(np.asarray(inputs["L0_attn_W_in"], f).T),
        "w_outT": np.ascontiguousarray(np.asarray(inputs["L0_attn_W_out"], f).T),
        "w_gT": np.ascontiguousarray(np.asarray(inputs["L0_ffn_w_gate"], f).T),
        "w_uT": np.ascontiguousarray(np.asarray(inputs["L0_ffn_w_up"], f).T),
        "w_dT": np.ascontiguousarray(np.asarray(inputs["L0_ffn_w_down"], f).T),
        "dtb_rep": np.ascontiguousarray(
            np.broadcast_to(np.asarray(inputs["L0_attn_dt_bias"], f), (128, NH))),
        "bbias_row": np.asarray(inputs["L0_attn_B_bias"], f).reshape(1, NH * NS),
        "cbias_row": np.asarray(inputs["L0_attn_C_bias"], f).reshape(1, NH * NS),
        "bnw_rep": np.ascontiguousarray(
            np.broadcast_to(np.asarray(inputs["L0_attn_B_norm_w"], f), (128, NS))),
        "cnw_rep": np.ascontiguousarray(
            np.broadcast_to(np.asarray(inputs["L0_attn_C_norm_w"], f), (128, NS))),
        "d_rep": np.ascontiguousarray(
            np.broadcast_to(np.asarray(inputs["L0_attn_D_param"], f), (128, NH))),
    }
    return ins


def _host_consts():
    f = np.float32
    idx = np.arange(128)
    L = (idx[:, None] <= idx[None, :]).astype(f)
    shift = (idx[:, None] == idx[None, :] + 1).astype(f)
    mask = np.where(idx[None, :] > idx[:, None], 0.0, NEG).astype(f)
    ones = np.ones((128, 128), f)
    e0 = np.zeros((128, 128), f)
    e0[0, :] = 1.0
    ident = np.eye(128, dtype=f)
    cols = np.zeros((128, 4), f)
    cols[:, 0] = 1.0
    cols[127, 1] = 1.0
    cols[:, 2] = 1e-5
    colsbf = np.zeros((128, 2), ml_dtypes.bfloat16)
    colsbf[:, 0] = 1.0
    colsbf[127, 1] = 1.0
    return {
        "c_L": L, "c_shift": shift, "c_mask": mask, "c_ones": ones, "c_e0": e0,
        "c_idf32": ident, "c_idbf": ident.astype(ml_dtypes.bfloat16),
        "c_cols": cols, "c_colsbf": colsbf,
    }


_CACHE = {}


def kernel(**inputs):
    if "nc" not in _CACHE:
        nc = bass.Bass("TRN2", target_bir_lowering=False, debug=False)
        build(nc)
        _CACHE["nc"] = nc
    nc = _CACHE["nc"]
    consts = _host_consts()
    in_maps = []
    for core in range(8):
        m = _host_inputs(inputs, core)
        m.update(consts)
        in_maps.append(m)
    res = run_bass_kernel_spmd(nc, in_maps, core_ids=list(range(8)))
    out = np.concatenate([np.asarray(res.results[cc]["out"]) for cc in range(8)],
                         axis=0)
    return out.astype(np.float32)


# version 11
# speedup vs baseline: 1.1515x; 1.0501x over previous
import numpy as np
import ml_dtypes
from contextlib import ExitStack

import concourse.bass as bass
import concourse.mybir as mybir
from concourse.bass_utils import run_bass_kernel_spmd

F32 = mybir.dt.float32
BF16 = mybir.dt.bfloat16
AX = mybir.AxisListType
OP = mybir.AluOpType
AF = mybir.ActivationFunctionType

TAPS = None
DBG_CUT = None

MAGIC = float(3 * 2.0 ** 22)
PI = float(np.pi)
NEG = -1e30

T = 2048
NB = 2
NCH = T // 128
DM = 1024
DI = 2048
NH = 32
HP = 64
NS = 128
NA = 32
DIN = 4480
FI = 4096


class B:

    def tap(self, name, apv, bi=None, ci=None):
        if TAPS is None:
            return
        if bi is not None and (bi, ci) != (TAPS.get("bi", 0), TAPS.get("ci", 0)):
            return
        sh = list(apv.shape)
        dt = apv.dtype
        t = self.nc.dram_tensor("tap_" + name, sh, dt, kind="ExternalOutput").ap()

        @self.D
        def _(sync):
            for nm in ("sv", "ss", "st", "sg"):
                self.wait(sync, nm)
            self.inc(sync.dma_start(t, apv), "sd", 16)

    def __init__(self, nc, stack):
        self.nc = nc
        self.sem = {}
        self.cnt = {}
        for name in ("sd", "st", "sv", "ss", "sg"):
            self.sem[name] = stack.enter_context(nc.semaphore(name))
            self.cnt[name] = 0
        self.blk = None

    def inc(self, instr, name, k=1):
        instr.then_inc(self.sem[name], k)
        self.cnt[name] += k
        return self.cnt[name]

    def wait(self, eng, name, val=None):
        v = self.cnt[name] if val is None else val
        if v > 0:
            eng.wait_ge(self.sem[name], v)

    def T(self, f):
        self.blk.tensor(f)

    def V(self, f):
        self.blk.vector(f)

    def S(self, f):
        self.blk.scalar(f)

    def G(self, f):
        self.blk.gpsimd(f)

    def D(self, f):
        self.blk.sync(f)


def build(nc):
    stack = ExitStack()
    b = B(nc, stack)
    ap = lambda t: t.ap()

    d = {}
    d["u_fm"] = ap(nc.dram_tensor("u_fm", [NB, DM, T], F32, kind="ExternalInput"))
    d["u_tm"] = ap(nc.dram_tensor("u_tm", [NB, T, DM], F32, kind="ExternalInput"))
    d["w_inT"] = ap(nc.dram_tensor("w_inT", [DM, DIN], F32, kind="ExternalInput"))
    d["w_outT"] = ap(nc.dram_tensor("w_outT", [DI, DM], F32, kind="ExternalInput"))
    d["w_gT"] = ap(nc.dram_tensor("w_gT", [DM, FI], F32, kind="ExternalInput"))
    d["w_uT"] = ap(nc.dram_tensor("w_uT", [DM, FI], F32, kind="ExternalInput"))
    d["w_dT"] = ap(nc.dram_tensor("w_dT", [FI, DM], F32, kind="ExternalInput"))
    d["dtb_rep"] = ap(nc.dram_tensor("dtb_rep", [128, NH], F32, kind="ExternalInput"))
    d["bbias_row"] = ap(nc.dram_tensor("bbias_row", [1, NH * NS], F32, kind="ExternalInput"))
    d["cbias_row"] = ap(nc.dram_tensor("cbias_row", [1, NH * NS], F32, kind="ExternalInput"))
    d["bnw_rep"] = ap(nc.dram_tensor("bnw_rep", [128, NS], F32, kind="ExternalInput"))
    d["cnw_rep"] = ap(nc.dram_tensor("cnw_rep", [128, NS], F32, kind="ExternalInput"))
    d["d_rep"] = ap(nc.dram_tensor("d_rep", [128, NH], F32, kind="ExternalInput"))
    for cn in ("c_L", "c_shift", "c_mask", "c_maskT", "c_ones", "c_e0", "c_idf32"):
        d[cn] = ap(nc.dram_tensor(cn, [128, 128], F32, kind="ExternalInput"))
    d["c_idbf"] = ap(nc.dram_tensor("c_idbf", [128, 128], BF16, kind="ExternalInput"))
    d["c_cols"] = ap(nc.dram_tensor("c_cols", [128, 4], F32, kind="ExternalInput"))
    d["c_colsbf"] = ap(nc.dram_tensor("c_colsbf", [128, 2], BF16, kind="ExternalInput"))
    d["out"] = ap(nc.dram_tensor("out", [NB, T, DM], F32, kind="ExternalOutput"))
    d["sc_xn"] = ap(nc.dram_tensor("sc_xn", [NB, NCH, 128, DM], BF16, kind="Internal"))
    d["sc_s"] = ap(nc.dram_tensor("sc_s", [NB, NCH, 128, DM], F32, kind="Internal"))
    d["sc_gu"] = ap(nc.dram_tensor("sc_gu", [NB, NCH, 128, FI], BF16, kind="Internal"))
    d["sc_wqg"] = ap(nc.dram_tensor("sc_wqg", [8, 128, FI], BF16, kind="Internal"))
    d["sc_wqu"] = ap(nc.dram_tensor("sc_wqu", [8, 128, FI], BF16, kind="Internal"))
    d["sc_wqd"] = ap(nc.dram_tensor("sc_wqd", [32, 128, DM], BF16, kind="Internal"))

    with stack:
        c = {}
        for cn in ("c_L", "c_shift", "c_mask", "c_maskT", "c_ones", "c_e0",
                   "c_idf32"):
            c[cn] = stack.enter_context(nc.sbuf_tensor("sb" + cn, [128, 128], F32))
        c["c_idbf"] = stack.enter_context(nc.sbuf_tensor("sbc_idbf", [128, 128], BF16))
        c["c_cols"] = stack.enter_context(nc.sbuf_tensor("sbc_cols", [128, 4], F32))
        c["c_colsbf"] = stack.enter_context(nc.sbuf_tensor("sbc_colsbf", [128, 2], BF16))
        c["dtb"] = stack.enter_context(nc.sbuf_tensor("dtb", [128, NH], F32))
        c["bnw"] = stack.enter_context(nc.sbuf_tensor("bnw", [128, NS], F32))
        c["cnw"] = stack.enter_context(nc.sbuf_tensor("cnw", [128, NS], F32))
        c["drep"] = stack.enter_context(nc.sbuf_tensor("drep", [128, NH], F32))
        c["drepbf"] = stack.enter_context(nc.sbuf_tensor("drepbf", [128, NH], BF16))
        c["wsc"] = stack.enter_context(nc.sbuf_tensor("wsc", [128, 8], F32))
        c["gumax"] = stack.enter_context(nc.sbuf_tensor("gumax", [128, NB * NCH], F32))
        PB = [stack.enter_context(nc.psum_tensor(f"pb{i}", [128, 512], F32))
              for i in range(8)]
        nc.const_aps.register(F32, 1.0, c["c_cols"][:, 0:1])
        nc.const_aps.register(F32, 1e-5, c["c_cols"][:, 2:3])
        nc.const_aps.register(F32, 0.0, c["c_cols"][:, 3:4])

        with nc.Block("init") as ib:
            @ib.sync
            def _(sync):
                for k in b.sem:
                    sync.sem_clear(b.sem[k])

        with ExitStack() as asc:
            a = {}

            def asb(n, sh, dt=F32):
                a[n] = asc.enter_context(nc.sbuf_tensor(n, sh, dt))
                return a[n]

            asb("winT", [128, 8, DIN], BF16)
            asb("woutT", [128, 16, DM], BF16)
            asb("bias_b", [128, NH * NS], BF16)
            asb("bias_c", [128, NH * NS], BF16)
            asb("Hst", [128, NH * HP], BF16)
            asb("carry", [1, NH * NA], F32)
            asb("acol", [128, 2], F32)

            with nc.Block("main_a") as blk:
                b.blk = blk
                with ExitStack() as pro:
                    a["wtmp"] = pro.enter_context(nc.sbuf_tensor("wtmp", [128, DIN], F32))
                    a["wqtmp"] = pro.enter_context(nc.sbuf_tensor("wqtmp", [128, FI], BF16))
                    a["row2"] = pro.enter_context(nc.sbuf_tensor("row2", [1, 2], F32))
                    _prologue(b, d, c, a, PB)
                del a["wtmp"], a["wqtmp"], a["row2"]
                asb("ub32", [128, 8, 128], F32)
                asb("xnb", [128, 8, 128], BF16)
                asb("x_tm", [128, NH, HP], BF16)
                asb("xqh", [128, NH, HP], BF16)
                asb("xmaskE", [128, NH, HP], BF16)
                asb("zs", [128, DI], BF16)
                asb("sml", [128, 384], F32)
                asb("smt", [128, 128], F32)
                for n in ("dt_", "adt", "lam", "dtlam", "dt1ml", "wvec", "qw",
                          "cs_tm", "alpha"):
                    asb(n, [128, NH], F32)
                for n in ("qh_e", "decay", "d10bf"):
                    asb(n, [128, NH], BF16)
                asb("Gbuf", [128, NH, NA], F32)
                asb("sin_t", [128, NH, NA], BF16)
                asb("cos_t", [128, NH, NA], BF16)
                asb("bh_all", [128, 2, NH, NS], BF16)
                asb("ch_all", [128, NH, NS], BF16)
                asb("bh_sm", [128, 2, 8, 128], BF16)
                asb("ch_sm", [128, 2, 8, 128], BF16)
                asb("Ebuf", [128, 8, 128], F32)
                asb("dgbuf", [128, 8, 128], BF16)
                asb("eint", [128, 8, HP], BF16)
                asb("PSb", [128, 8, 128], BF16)
                asb("ybar_g", [128, 512], BF16)
                asb("ybar_sm", [128, 4, 128], BF16)
                for bi in range(NB):
                    _batch_reset(b, a)
                    for ci in range(NCH):
                        _chunk_a(b, d, c, a, PB, bi, ci)

        with ExitStack() as bsc:
            p = {}

            def bsb(n, sh, dt=F32):
                p[n] = bsc.enter_context(nc.sbuf_tensor(n, sh, dt))
                return p[n]

            bsb("wgq", [128, 8, FI], BF16)
            bsb("wuq", [128, 8, FI], BF16)
            bsb("xnf", [128, 2, 8, 128], BF16)
            bsb("xq_fm", [128, 8, 128], BF16)
            bsb("m8", [128, 128], F32)
            bsb("mt", [128, 128], F32)
            bsb("srow", [1, 128], F32)
            bsb("scol", [128, 4], F32)
            bsb("gu_b", [128, FI], BF16)
            with nc.Block("main_b1") as blk:
                b.blk = blk

                @b.D
                def _(sync):
                    b.wait(sync, "sv")
                    b.wait(sync, "ss")
                    b.wait(sync, "sd")
                    for k in range(8):
                        b.inc(sync.dma_start(p["wgq"][:, k, :], d["sc_wqg"][k]), "sd", 16)
                        b.inc(sync.dma_start(p["wuq"][:, k, :], d["sc_wqu"][k]), "sd", 16)
                for bi in range(NB):
                    for ci in range(NCH):
                        _chunk_b1(b, d, c, p, PB, bi, ci)

        with ExitStack() as bsc2:
            q = {}

            def qsb(n, sh, dt=F32):
                q[n] = bsc2.enter_context(nc.sbuf_tensor(n, sh, dt))
                return q[n]

            qsb("wdq", [128, 32, DM], BF16)
            qsb("gu_tm", [128, 2, FI], BF16)
            qsb("guq_tm", [128, FI], BF16)
            qsb("qtmp", [128, FI], F32)
            qsb("guq_fm", [128, 32, 128], BF16)
            qsb("u_b", [128, 2, DM], F32)
            qsb("s_b", [128, 2, DM], F32)
            qsb("g_b", [128, DM], F32)
            qsb("o_b", [128, DM], F32)
            qsb("scol2", [128, 4], F32)
            with nc.Block("main_b2") as blk:
                b.blk = blk

                @b.D
                def _(sync):
                    b.wait(sync, "sv")
                    b.wait(sync, "ss")
                    b.wait(sync, "sd")
                    for k in range(32):
                        b.inc(sync.dma_start(q["wdq"][:, k, :], d["sc_wqd"][k]), "sd", 16)
                for bi in range(NB):
                    for ci in range(NCH):
                        _chunk_b2(b, d, c, q, PB, bi, ci)

            with nc.Block("fin") as blk:
                @blk.sync
                def _(sync):
                    b.wait(sync, "sd")
    return nc


def _prologue(b, d, c, a, PB):
    @b.D
    def _(sync):
        for dst, key in [(c["c_L"], "c_L"), (c["c_shift"], "c_shift"),
                         (c["c_mask"], "c_mask"), (c["c_maskT"], "c_maskT"),
                         (c["c_ones"], "c_ones"),
                         (c["c_e0"], "c_e0"), (c["c_idf32"], "c_idf32"),
                         (c["c_idbf"], "c_idbf"), (c["c_cols"], "c_cols"),
                         (c["c_colsbf"], "c_colsbf"), (c["dtb"], "dtb_rep"),
                         (c["bnw"], "bnw_rep"), (c["cnw"], "cnw_rep"),
                         (c["drep"], "d_rep")]:
            b.inc(sync.dma_start(dst[:], d[key]), "sd", 16)
        b.inc(sync.dma_start(a["wtmp"][0:1, 0:NH * NS], d["bbias_row"]), "sd", 16)

    @b.V
    def _(vector):
        b.wait(vector, "sd")
        vector.tensor_copy(c["drepbf"][:], c["drep"][:])
        b.inc(vector.drain(), "sv")

    for key, dstname in [("cbias_row", "bias_b"), (None, "bias_c")]:
        for j in range(8):
            @b.T
            def _(tensor, j=j):
                b.wait(tensor, "sd")
                b.wait(tensor, "sv")
                tensor.matmul(PB[j % 2][:], c["c_ones"][0:1, :],
                              a["wtmp"][0:1, j * 512:(j + 1) * 512],
                              start=True, stop=True)
                b.inc(tensor.drain(), "st")

            @b.V
            def _(vector, dstname=dstname, j=j):
                b.wait(vector, "st")
                vector.tensor_copy(a[dstname][:, j * 512:(j + 1) * 512], PB[j % 2][:])
                b.inc(vector.drain(), "sv")

        if key is not None:
            @b.D
            def _(sync, key=key):
                b.wait(sync, "st")
                b.inc(sync.dma_start(a["wtmp"][0:1, 0:NH * NS], d[key]), "sd", 16)

    for k in range(8):
        @b.D
        def _(sync, k=k):
            b.wait(sync, "sv")
            b.inc(sync.dma_start(a["wtmp"][:], d["w_inT"][k * 128:(k + 1) * 128, :]),
                  "sd", 16)

        @b.V
        def _(vector, k=k):
            b.wait(vector, "sd")
            vector.tensor_copy(a["winT"][:, k, :], a["wtmp"][:])
            b.inc(vector.drain(), "sv")
    for k in range(16):
        @b.D
        def _(sync, k=k):
            b.wait(sync, "sv")
            b.inc(sync.dma_start(a["wtmp"][:, 0:DM],
                                 d["w_outT"][k * 128:(k + 1) * 128, :]), "sd", 16)

        @b.V
        def _(vector, k=k):
            b.wait(vector, "sd")
            vector.tensor_copy(a["woutT"][:, k, :], a["wtmp"][:, 0:DM])
            b.inc(vector.drain(), "sv")

    for wi, (dkey, skey, nkw, wid) in enumerate(
            [("w_gT", "sc_wqg", 8, FI), ("w_uT", "sc_wqu", 8, FI),
             ("w_dT", "sc_wqd", 32, DM)]):
        @b.V
        def _(vector):
            vector.memset(a["acol"][:, 0:1], 0.0)
            b.inc(vector.drain(), "sv")
        for k in range(nkw):
            @b.D
            def _(sync, k=k, dkey=dkey, wid=wid):
                b.wait(sync, "sv")
                b.inc(sync.dma_start(a["wtmp"][:, 0:wid],
                                     d[dkey][k * 128:(k + 1) * 128, :]), "sd", 16)

            @b.V
            def _(vector, wid=wid):
                b.wait(vector, "sd")
                vector.tensor_reduce(a["acol"][:, 1:2], a["wtmp"][:, 0:wid], AX.X,
                                     OP.add, apply_absolute_value=True)
                vector.drain()
                vector.tensor_tensor(a["acol"][:, 0:1], a["acol"][:, 0:1],
                                     a["acol"][:, 1:2], OP.add)
                b.inc(vector.drain(), "sv")

        @b.T
        def _(tensor):
            b.wait(tensor, "sv")
            tensor.matmul(PB[2][0:1, 0:1], c["c_cols"][:, 0:1], a["acol"][:, 0:1],
                          start=True, stop=True)
            b.inc(tensor.drain(), "st")

        @b.V
        def _(vector, nkw=nkw, wid=wid):
            b.wait(vector, "st")
            n_el = float(nkw * 128 * wid)
            vector.tensor_scalar(a["row2"][0:1, 0:1], PB[2][0:1, 0:1], 1.0 / n_el,
                                 1e-5, OP.mult, OP.max)
            vector.drain()
            vector.reciprocal(a["row2"][0:1, 1:2], a["row2"][0:1, 0:1])
            b.inc(vector.drain(), "sv")

        @b.T
        def _(tensor):
            b.wait(tensor, "sv")
            tensor.matmul(PB[2][:, 0:2], c["c_ones"][0:1, :], a["row2"][0:1, 0:2],
                          start=True, stop=True)
            b.inc(tensor.drain(), "st")

        @b.V
        def _(vector, wi=wi):
            b.wait(vector, "st")
            vector.tensor_copy(c["wsc"][:, 3 + wi:4 + wi], PB[2][:, 0:1])
            vector.tensor_copy(c["wsc"][:, wi:wi + 1], PB[2][:, 1:2])
            b.inc(vector.drain(), "sv")

        for k in range(nkw):
            @b.D
            def _(sync, k=k, dkey=dkey, wid=wid):
                b.wait(sync, "sv")
                b.inc(sync.dma_start(a["wtmp"][:, 0:wid],
                                     d[dkey][k * 128:(k + 1) * 128, :]), "sd", 16)

            @b.V
            def _(vector, wi=wi, wid=wid):
                b.wait(vector, "sd")
                vector.tensor_scalar(a["wtmp"][:, 0:wid], a["wtmp"][:, 0:wid],
                                     c["wsc"][:, wi:wi + 1], MAGIC, OP.mult, OP.add)
                vector.tensor_scalar(a["wtmp"][:, 0:wid], a["wtmp"][:, 0:wid],
                                     MAGIC, None, OP.subtract)
                vector.tensor_scalar(a["wqtmp"][:, 0:wid], a["wtmp"][:, 0:wid],
                                     -1.0, 1.0, OP.max, OP.min)
                b.inc(vector.drain(), "sv")

            @b.D
            def _(sync, skey=skey, k=k, wid=wid):
                b.wait(sync, "sv")
                b.inc(sync.dma_start(d[skey][k, :, :], a["wqtmp"][:, 0:wid]), "sd", 16)


def _batch_reset(b, a):
    @b.V
    def _(vector):
        b.wait(vector, "st")
        vector.memset(a["Hst"][:], 0.0)
        vector.memset(a["carry"][:], 0.0)
        b.inc(vector.drain(), "sv")


def _chunk_a(b, d, c, a, PB, bi, ci):
    cur = ci % 2
    prev = 1 - cur
    onesbf = c["c_colsbf"][:, 0:1]
    e127bf = c["c_colsbf"][:, 1:2]
    bh_cur = a["bh_all"][:, cur]
    pos2sl = [8, 0, 1, 2, 3, 4, 5, 6, 7]

    def pair(g):
        return (2, 3) if g % 2 == 0 else (4, 5)

    def opair(g):
        return (4, 5) if g % 2 == 0 else (2, 3)

    mk = {}

    def plan(ch, names, step=1):
        v = b.cnt[ch]
        for n in names:
            v += step
            mk[n] = v

    vnames = ["vLN1", "vLN2", "vXN", "vSM1", "vIP0", "vIP1", "vSM2", "vCS1",
              "vIP2", "vIP3", "vQW", "vG", "vCAR0", "vRS0", "vRC0", "vCAR1",
              "vRS1", "vRC1", "vXM", "vRMB", "vBB", "vRMC", "vCB"]
    for g in range(4):
        vnames += [f"vPS{g}"]
        if ci > 0:
            vnames += [f"vHB{g}"]
        vnames += [f"vYE1{g}", f"vYEV{g}"]
    vnames += ["vSEV"]
    plan("sv", vnames)

    tnames = (["tLN", "tBC"] + [f"tIP{p}" for p in range(5)] +
              ["tCS", "tIP5", "tIP6", "tIP7", "tIP8", "tTH0", "tTH1"])
    tnames += ["tTR0", "tSCs0", "tSCb0", "tTR1"]
    for g in range(4):
        if ci > 0:
            tnames += [f"tBD{g}"]
        tnames += [f"tY{g}"]
        if g < 3:
            tnames += [f"tSCs{g + 1}"]
        tnames += [f"tYB{g}"]
        if g < 3:
            tnames += [f"tSCb{g + 1}"]
        tnames += [f"tYT{g}"]
        if g < 2:
            tnames += [f"tTR{g + 2}"]
    tnames += ["tOP"]
    plan("st", tnames)

    snames = ["sLN0", "sLN", "sBC", "sIP8", "sSM", "sCS", "sQH", "sIP4", "sIP5",
              "sS0", "sC0", "sIP6", "sIP7", "sS1", "sC1", "sZS", "sRB", "sRC",
              "sCSB0", "sDG0", "sTRE0"]
    for g in range(4):
        snames += [f"sE{g}"]
        if g >= 1:
            snames += [f"sYSM{g - 1}"]
        snames += [f"sYSC{g}"]
        if g < 3:
            snames += [f"sCSB{g + 1}", f"sTRE{g + 1}", f"sDG{g + 1}"]
    snames += ["sYSM3"]
    plan("ss", snames)
    plan("sd", ["dU", "dXN", "dS"], step=16)
    prev_mk = getattr(b, "prev_a_marks", {})

    def W(eng, ch, name):
        if name in mk:
            eng.wait_ge(b.sem[ch], mk[name])
        elif name in prev_mk:
            eng.wait_ge(b.sem[ch], prev_mk[name])

    @b.D
    def _(sync):
        b.wait(sync, "sd", mk["dU"] - 16)
        src = d["u_fm"][bi].rearrange("(k p) t -> p k t", p=128)[
            :, :, ci * 128:(ci + 1) * 128]
        sync.dma_start(a["ub32"][:], src).then_inc(b.sem["sd"], 16)
        sync.wait_ge(b.sem["sv"], mk["vXN"])
        sync.dma_start(d["sc_xn"][bi, ci],
                       a["xnb"][:].rearrange("p k t -> p (k t)")
                       ).then_inc(b.sem["sd"], 16)
        sync.wait_ge(b.sem["sv"], mk["vSEV"])
        sync.dma_start(d["sc_s"][bi, ci],
                       a["ub32"][:].rearrange("p k t -> p (k t)")
                       ).then_inc(b.sem["sd"], 16)
    b.cnt["sd"] = mk["dS"]

    @b.T
    def _(tensor):
        s_ = b.sem
        tensor.wait_ge(s_["sd"], mk["dU"])
        tensor.wait_ge(s_["ss"], mk["sLN0"])
        for k in range(8):
            tensor.matmul(PB[4][0:1, 0:128], onesbf, a["PSb"][:, k, :],
                          start=(k == 0), stop=(k == 7))
        for k in range(8):
            tensor.matmul(PB[4][0:1, 128:256], onesbf, a["xnb"][:, k, :],
                          start=(k == 0), stop=(k == 7))
        tensor.drain().then_inc(s_["st"], 1)
        tensor.wait_ge(s_["ss"], mk["sLN"])
        tensor.matmul(PB[5][:, 0:128], c["c_ones"][0:1, :], a["Gbuf"][:].rearrange("p h q -> p (h q)")[0:1, 512:640],
                      start=True, stop=True)
        tensor.matmul(PB[5][:, 128:256], c["c_ones"][0:1, :],
                      a["Gbuf"][:].rearrange("p h q -> p (h q)")[0:1, 640:768], start=True, stop=True)
        tensor.drain().then_inc(s_["st"], 1)
        tensor.wait_ge(s_["sv"], mk["vXN"])
        evac = {0: ("ss", "sIP8"), 1: ("sv", "vIP0"), 2: ("sv", "vIP1"),
                3: ("sv", "vIP2"), 4: ("sv", "vIP3"), 5: ("ss", "sIP4"),
                6: ("ss", "sIP5"), 7: ("ss", "sIP6"), 8: ("ss", "sIP7")}

        def t_ip(p):
            sl = pos2sl[p]
            ncol = 384 if sl == 8 else 512
            if p >= 2:
                ch_, nm = evac[p - 2]
                tensor.wait_ge(s_[ch_], mk[nm])
            for k in range(8):
                tensor.matmul(PB[p % 2][:, 0:ncol], a["xnb"][:, k, :],
                              a["winT"][:, k, sl * 512:sl * 512 + ncol],
                              start=(k == 0), stop=(k == 7))
            tensor.drain().then_inc(s_["st"], 1)

        for p in range(5):
            t_ip(p)
        tensor.wait_ge(s_["sv"], mk["vSM2"])
        tensor.matmul(PB[6][:, 0:32], c["c_L"][:], a["adt"][:], start=True,
                      stop=True)
        tensor.matmul(PB[6][:, 32:64], c["c_shift"][:], a["dt1ml"][:], start=True,
                      stop=True)
        tensor.matmul(PB[6][:, 64:96], c["c_ones"][:], a["adt"][:], start=True,
                      stop=True)
        tensor.matmul(PB[6][:, 96:128], c["c_e0"][:], a["dt1ml"][:], start=True,
                      stop=True)
        tensor.drain().then_inc(s_["st"], 1)
        for p in range(5, 9):
            t_ip(p)
        Gflat = a["Gbuf"][:].rearrange("p h q -> p (h q)")
        tensor.wait_ge(s_["sv"], mk["vG"])
        for j in range(2):
            tensor.matmul(PB[2 + j][:], c["c_ones"][0:1, :],
                          a["carry"][0:1, j * 512:(j + 1) * 512], start=True,
                          stop=False)
            tensor.matmul(PB[2 + j][:], c["c_L"][:], Gflat[:, j * 512:(j + 1) * 512],
                          start=False, stop=True)
        tensor.matmul(PB[7][0:1, :], c["c_cols"][:, 0:1], Gflat[:, 0:512],
                      start=True, stop=True)
        tensor.drain().then_inc(s_["st"], 1)
        tensor.wait_ge(s_["sv"], mk["vCAR0"])
        tensor.matmul(PB[7][0:1, :], c["c_cols"][:, 0:1], Gflat[:, 512:1024],
                      start=True, stop=True)
        tensor.drain().then_inc(s_["st"], 1)

        def t_tr(g):
            hg = g * 8
            if g == 0:
                tensor.wait_ge(s_["sv"], mk["vCB"])
            else:
                tensor.wait_ge(s_["ss"], mk[f"sE{g - 1}"])
            pa, pb_ = pair(g)
            pbh = PB[pa][:].bitcast(BF16)
            pch = PB[pb_][:].bitcast(BF16)
            for i in range(8):
                tensor.transpose(pbh[:, i * 128:(i + 1) * 128], bh_cur[:, hg + i, :],
                                 c["c_idbf"][:])
            for i in range(8):
                tensor.transpose(pch[:, i * 128:(i + 1) * 128],
                                 a["ch_all"][:, hg + i, :], c["c_idbf"][:])
            tensor.drain().then_inc(s_["st"], 1)

        def t_scs(g):
            tensor.wait_ge(s_["ss"], mk[f"sTRE{g}"])
            pa, pb_ = pair(g)
            for i in range(8):
                tensor.matmul(PB[pa + i // 4][:, (i % 4) * 128:(i % 4) * 128 + 128],
                              a["bh_sm"][:, g % 2, i, :], a["ch_sm"][:, g % 2, i, :],
                              start=True, stop=True)
            tensor.drain().then_inc(s_["st"], 1)

        def t_scb(g):
            tensor.wait_ge(s_["ss"], mk[f"sCSB{g}"])
            if g > 0:
                tensor.wait_ge(s_["sv"], mk[f"vPS{g - 1}"])
            pa, pb_ = opair(g)
            for i in range(8):
                tensor.transpose(PB[pa + i // 4][:, (i % 4) * 128:(i % 4) * 128 + 128],
                                 a["Ebuf"][:, i, :], c["c_idf32"][:])
            tensor.drain().then_inc(s_["st"], 1)

        def t_y(g):
            hg = g * 8
            tensor.wait_ge(s_["sv"], mk[f"vPS{g}"])
            if g >= 1:
                tensor.wait_ge(s_["ss"], mk[f"sYSM{g - 1}"])
            if ci > 0:
                for i in range(8):
                    tensor.matmul(PB[6][:, i * 64:(i + 1) * 64],
                                  a["bh_all"][:, prev, hg + i, :],
                                  a["xmaskE"][:, hg + i, :], start=True, stop=True)
                tensor.drain().then_inc(s_["st"], 1)
                tensor.wait_ge(s_["sv"], mk[f"vHB{g}"])
            for i in range(8):
                tensor.matmul(PB[6][:, i * 64:(i + 1) * 64], a["PSb"][:, i, :],
                              a["x_tm"][:, hg + i, :], start=True, stop=True)
            for i in range(8):
                tensor.matmul(PB[7][:, i * 64:(i + 1) * 64],
                              a["ch_sm"][:, g % 2, i, :],
                              a["Hst"][:, (hg + i) * HP:(hg + i + 1) * HP],
                              start=True, stop=True)
            tensor.drain().then_inc(s_["st"], 1)

        def t_ybq(g):
            hg = g * 8
            tensor.wait_ge(s_["ss"], mk[f"sYSC{g}"])
            for i in range(8):
                tensor.matmul(PB[7][:, i * 64:(i + 1) * 64], bh_cur[:, hg + i, :],
                              a["xqh"][:, hg + i, :], start=True, stop=True)
            tensor.drain().then_inc(s_["st"], 1)

        def t_yt(g):
            tensor.wait_ge(s_["sv"], mk[f"vYE1{g}"])
            pbf6 = PB[6][:].bitcast(BF16)
            for j in range(4):
                tensor.transpose(pbf6[:, j * 128:(j + 1) * 128],
                                 a["ybar_g"][:, j * 128:(j + 1) * 128], c["c_idbf"][:])
            tensor.drain().then_inc(s_["st"], 1)

        def t_op(g):
            tensor.wait_ge(s_["ss"], mk[f"sYSM{g}"])
            for j in range(4):
                kin = g * 4 + j
                for sx in range(2):
                    tensor.matmul(PB[sx][:, 0:512], a["ybar_sm"][:, j, :],
                                  a["woutT"][:, kin, sx * 512:(sx + 1) * 512],
                                  start=(kin == 0), stop=(kin == 15))

        def dmy(*_a, **_k):
            tensor.drain().then_inc(s_["st"], 1)
        if DBG_CUT == 1:
            t_tr = t_scs = t_scb = t_y = t_ybq = t_yt = dmy
            t_op = lambda g: None
        elif DBG_CUT == 2:
            t_y = t_ybq = t_yt = dmy
            t_op = lambda g: None
        elif DBG_CUT == 3:
            t_op = lambda g: None
        t_tr(0)
        t_scs(0)
        t_scb(0)
        t_tr(1)
        t_y(0)
        t_scs(1)
        t_ybq(0)
        t_scb(1)
        t_yt(0)
        t_tr(2)
        t_op(0)
        t_y(1)
        t_scs(2)
        t_ybq(1)
        t_scb(2)
        t_yt(1)
        t_tr(3)
        t_op(1)
        t_y(2)
        t_scs(3)
        t_ybq(2)
        t_scb(3)
        t_yt(2)
        t_y(3)
        t_ybq(3)
        t_op(2)
        t_yt(3)
        t_op(3)
        tensor.drain().then_inc(s_["st"], 1)
    b.cnt["st"] = mk["tOP"]

    @b.V
    def _(vector):
        s_ = b.sem
        vector.wait_ge(s_["sd"], mk["dU"])
        W(vector, "st", "p_tY3")
        W(vector, "ss", "p_sE3")
        vector.drain().then_inc(s_["sv"], 1)
        vector.wait_ge(s_["st"], mk["tLN"])
        vector.tensor_scalar(a["Gbuf"][:].rearrange("p h q -> p (h q)")[0:1, 512:640], PB[4][0:1, 0:128], 1.0 / DM,
                             None, OP.mult)
        vector.tensor_scalar(a["Gbuf"][:].rearrange("p h q -> p (h q)")[0:1, 640:768], PB[4][0:1, 128:256], 1.0 / DM,
                             None, OP.mult)
        vector.drain()
        vector.tensor_tensor(a["smt"][0:1, 0:128], a["Gbuf"][:].rearrange("p h q -> p (h q)")[0:1, 512:640],
                             a["Gbuf"][:].rearrange("p h q -> p (h q)")[0:1, 512:640], OP.mult)
        vector.drain()
        vector.tensor_tensor(a["Gbuf"][:].rearrange("p h q -> p (h q)")[0:1, 640:768], a["Gbuf"][:].rearrange("p h q -> p (h q)")[0:1, 640:768],
                             a["smt"][0:1, 0:128], OP.subtract)
        vector.drain().then_inc(s_["sv"], 1)
        vector.wait_ge(s_["ss"], mk["sBC"])
        srep = a["Gbuf"][:].rearrange("p h q -> p (h q)")[:, 768:1024]
        mrep = srep[:, 0:128].unsqueeze(1).broadcast_to([128, 8, 128])
        rrep = srep[:, 128:256].unsqueeze(1).broadcast_to([128, 8, 128])
        vector.tensor_tensor(a["ub32"][:], a["ub32"][:], mrep, OP.subtract)
        vector.drain()
        vector.tensor_tensor(a["xnb"][:], a["ub32"][:], rrep, OP.mult)
        vector.drain().then_inc(s_["sv"], 1)
        vector.wait_ge(s_["ss"], mk["sIP8"])
        vector.tensor_tensor(a["dt_"][:], a["sml"][:, 256:288], c["dtb"][:], OP.add)
        vector.drain()
        vector.scalar_tensor_tensor(a["smt"][:, 0:32], a["dt_"][:], -1.0, a["dt_"][:],
                                    OP.mult, OP.min)
        vector.scalar_tensor_tensor(a["smt"][:, 32:64], a["sml"][:, 288:320],
                                    -1.0, a["sml"][:, 288:320], OP.mult, OP.min)
        vector.drain().then_inc(s_["sv"], 1)
        vector.wait_ge(s_["st"], mk["tIP1"])
        vector.tensor_copy(a["zs"][:, 0:512], PB[1][:])
        vector.drain().then_inc(s_["sv"], 1)
        vector.wait_ge(s_["st"], mk["tIP2"])
        vector.tensor_copy(a["zs"][:, 512:1024], PB[0][:])
        vector.drain().then_inc(s_["sv"], 1)
        vector.wait_ge(s_["ss"], mk["sSM"])
        vector.tensor_scalar(a["smt"][:, 96:128], a["dt_"][:], 0.0, None, OP.max)
        vector.tensor_scalar(a["Gbuf"][:, 0, 0:32], a["sml"][:, 288:320], 0.0, None,
                             OP.max)
        vector.tensor_scalar(a["lam"][:], a["smt"][:, 64:96], 1.0, None, OP.add)
        vector.drain()
        vector.reciprocal(a["lam"][:], a["lam"][:])
        vector.drain()
        vector.tensor_tensor(a["dt_"][:], a["smt"][:, 96:128], a["smt"][:, 0:32],
                             OP.add)
        vector.tensor_tensor(a["adt"][:], a["Gbuf"][:, 0, 0:32], a["smt"][:, 32:64],
                             OP.add)
        vector.tensor_scalar(a["smt"][:, 96:128], a["lam"][:], -1.0, 1.0, OP.mult,
                             OP.add)
        vector.drain()
        vector.tensor_scalar(a["adt"][:], a["adt"][:], -1.0, -1e-4, OP.mult, OP.min)
        vector.tensor_tensor(a["dtlam"][:], a["dt_"][:], a["lam"][:], OP.mult)
        vector.tensor_tensor(a["dt1ml"][:], a["dt_"][:], a["smt"][:, 96:128],
                             OP.mult)
        vector.drain()
        vector.tensor_tensor(a["adt"][:], a["adt"][:], a["dt_"][:], OP.mult)
        vector.drain().then_inc(s_["sv"], 1)
        vector.wait_ge(s_["st"], mk["tCS"])
        vector.tensor_copy(a["cs_tm"][:], PB[6][:, 0:32])
        vector.tensor_tensor(a["wvec"][:], a["dtlam"][:], PB[6][:, 32:64], OP.add)
        vector.drain()
        vector.tensor_scalar(a["wvec"][:], a["wvec"][:], 1e-30, None, OP.max)
        vector.drain().then_inc(s_["sv"], 1)
        vector.wait_ge(s_["st"], mk["tIP3"])
        vector.tensor_copy(a["zs"][:, 1024:1536], PB[1][:])
        vector.drain().then_inc(s_["sv"], 1)
        vector.wait_ge(s_["st"], mk["tIP4"])
        vector.tensor_copy(a["zs"][:, 1536:2048], PB[0][:])
        vector.drain().then_inc(s_["sv"], 1)
        vector.wait_ge(s_["ss"], mk["sCS"])
        vector.tensor_tensor(a["qw"][:], a["qw"][:], a["cs_tm"][:], OP.subtract)
        vector.drain()
        vector.tensor_tensor(a["smt"][:, 0:32], a["qw"][:], PB[6][:, 64:96], OP.add)
        vector.drain().then_inc(s_["sv"], 1)
        dtr = a["dt_"][:].unsqueeze(2).broadcast_to([128, NH, NA])
        angr = a["sml"][:, 352:384].unsqueeze(1).broadcast_to([128, NH, NA])
        vector.tensor_tensor(a["Gbuf"][:], dtr, angr, OP.mult)
        vector.drain().then_inc(s_["sv"], 1)
        Gflat = a["Gbuf"][:].rearrange("p h q -> p (h q)")
        for j in range(2):
            gsl = Gflat[:, j * 512:(j + 1) * 512]
            vector.wait_ge(s_["st"], mk["tTH0" if j == 0 else "tTH1"])
            vector.tensor_tensor(a["carry"][0:1, j * 512:(j + 1) * 512],
                                 a["carry"][0:1, j * 512:(j + 1) * 512],
                                 PB[7][0:1, :], OP.add)
            vector.drain().then_inc(s_["sv"], 1)
            vector.tensor_scalar(gsl, PB[2 + j][:], 1.0 / (2 * PI), MAGIC, OP.mult,
                                 OP.add)
            vector.tensor_scalar(gsl, gsl, MAGIC, 2 * PI, OP.subtract, OP.mult)
            vector.tensor_tensor(gsl, PB[2 + j][:], gsl, OP.subtract)
            vector.drain().then_inc(s_["sv"], 1)
            vector.wait_ge(s_["ss"], mk[f"sS{j}"])
            vector.tensor_scalar(gsl, PB[2 + j][:], 1.0 / (2 * PI), 0.25, OP.mult,
                                 OP.add)
            vector.tensor_scalar(gsl, gsl, MAGIC, MAGIC, OP.add, OP.subtract)
            vector.tensor_scalar(gsl, gsl, 2 * PI, PI / 2, OP.mult, OP.subtract)
            vector.tensor_tensor(gsl, PB[2 + j][:], gsl, OP.subtract)
            vector.drain().then_inc(s_["sv"], 1)
        vector.wait_ge(s_["ss"], mk["sQH"])
        vector.wait_ge(s_["ss"], mk["sIP7"])
        qrep = a["qh_e"][:].unsqueeze(2).broadcast_to([128, NH, HP])
        vector.tensor_tensor(a["xqh"][:], a["x_tm"][:], qrep, OP.mult)
        if ci > 0:
            drep0 = a["d10bf"][:].unsqueeze(2).broadcast_to([128, NH, HP])
            vector.tensor_tensor(a["xmaskE"][:], a["xmaskE"][:], drep0, OP.mult)
        vector.drain().then_inc(s_["sv"], 1)
        bnbf = a["ybar_sm"][:, 0, :]
        for src0, nwkey, biaskey in [(0, "bnw", "bias_b"), (128, "cnw", "bias_c")]:
            vector.tensor_tensor(a["ub32"][:, 0, :], a["sml"][:, src0:src0 + 128],
                                 a["sml"][:, src0:src0 + 128], OP.mult)
            vector.drain()
            vector.tensor_reduce(a["acol"][:, 0:1], a["ub32"][:, 0, :], AX.X, OP.add)
            vector.drain().then_inc(s_["sv"], 1)
            vector.wait_ge(s_["ss"], mk["sRB" if src0 == 0 else "sRC"])
            if src0 == 0:
                W(vector, "st", "p_tOP")
                dst = a["bh_all"][:, cur]
            else:
                dst = a["ch_all"][:]
            vector.scalar_tensor_tensor(bnbf, a["sml"][:, src0:src0 + 128],
                                        a["acol"][:, 0:1], c[nwkey][:], OP.mult,
                                        OP.mult)
            vector.drain()
            nrep = bnbf.unsqueeze(1).broadcast_to([128, NH, NS])
            brep = a[biaskey][:].rearrange("p (h n) -> p h n", h=NH)
            vector.tensor_tensor(dst, nrep, brep, OP.add)
            vector.drain()
            v1 = dst[:, :, 0:NA]
            v2 = dst[:, :, NA:2 * NA]
            sinr = a["sin_t"][:]
            cosr = a["cos_t"][:]
            rt0 = a["dgbuf"][:].rearrange("p i t -> p (i t)").rearrange(
                "p (h q) -> p h q", h=NH)
            rt1 = a["Ebuf"][:].rearrange("p i t -> p (i t)").bitcast(BF16)[
                :, 0:1024].rearrange("p (h q) -> p h q", h=NH)
            vector.tensor_tensor(rt0, v2, sinr, OP.mult)
            vector.tensor_tensor(v2, v2, cosr, OP.mult)
            vector.tensor_tensor(rt1, v1, sinr, OP.mult)
            vector.tensor_tensor(v2, rt1, v2, OP.add)
            vector.tensor_tensor(v1, v1, cosr, OP.mult)
            vector.tensor_tensor(v1, v1, rt0, OP.subtract)
            vector.drain().then_inc(s_["sv"], 1)
        for g in range(4):
            hg = g * 8
            if DBG_CUT == 1:
                nmk = 3 if ci == 0 else 4
                for _ in range(nmk):
                    vector.drain().then_inc(s_["sv"], 1)
                continue
            vector.wait_ge(s_["ss"], mk[f"sE{g}"])
            pp = a["PSb"][:].rearrange("p i t -> p (i t)")
            dg = a["dgbuf"][:].rearrange("p i t -> p (i t)")
            vector.tensor_tensor(pp[:, 0:512], pp[:, 0:512], dg[:, 0:512], OP.add)
            vector.tensor_tensor(pp[:, 512:1024], pp[:, 512:1024], dg[:, 512:1024],
                                 OP.add)
            vector.drain()
            pa, pb_ = pair(g)
            vector.tensor_tensor(pp[:, 0:512], PB[pa][:], pp[:, 0:512], OP.mult)
            vector.tensor_tensor(pp[:, 512:1024], PB[pb_][:], pp[:, 512:1024],
                                 OP.mult)
            vector.drain().then_inc(s_["sv"], 1)
            if DBG_CUT == 2:
                for _ in range(2 if ci == 0 else 3):
                    vector.drain().then_inc(s_["sv"], 1)
                continue
            if ci > 0:
                vector.wait_ge(s_["st"], mk[f"tBD{g}"])
                hsl0 = a["Hst"][:, hg * HP:(hg + 8) * HP]
                vector.tensor_tensor(hsl0, hsl0, PB[6][:], OP.add)
                vector.drain().then_inc(s_["sv"], 1)
            vector.wait_ge(s_["st"], mk[f"tY{g}"])
            vector.wait_ge(s_["ss"], mk[f"sYSC{g}"])
            if g >= 1:
                vector.wait_ge(s_["st"], mk[f"tYT{g - 1}"])
            e13 = a["Gbuf"][:].rearrange("p h q -> p (h q)")[:, 0:512].rearrange(
                "p (i q) -> p i q", i=8)
            yb3 = a["ybar_g"][:].rearrange("p (i q) -> p i q", i=8)
            pb6_3 = PB[6][:].rearrange("p (i q) -> p i q", i=8)
            pb7_3 = PB[7][:].rearrange("p (i q) -> p i q", i=8)
            drp = c["drepbf"][:, hg:hg + 8].unsqueeze(2).broadcast_to([128, 8, HP])
            xsl = a["x_tm"][:, hg:hg + 8, :]
            eint3 = a["eint"][:]
            vector.tensor_tensor(yb3, xsl, drp, OP.mult)
            vector.tensor_tensor(e13, pb6_3, yb3, OP.add)
            vector.tensor_tensor(e13, e13, eint3, OP.add)
            zs3 = a["zs"][:, hg * HP:(hg + 8) * HP].rearrange("p (i q) -> p i q", i=8)
            vector.tensor_tensor(yb3, e13, zs3, OP.mult)
            vector.drain().then_inc(s_["sv"], 1)
            vector.wait_ge(s_["st"], mk[f"tYB{g}"])
            dcr = a["decay"][:, hg:hg + 8].unsqueeze(2).broadcast_to([128, 8, HP])
            hsl = a["Hst"][:, hg * HP:(hg + 8) * HP].rearrange("p (i q) -> p i q", i=8)
            vector.tensor_tensor(eint3, hsl, dcr, OP.mult)
            vector.drain()
            vector.tensor_tensor(hsl, eint3, pb7_3, OP.add)
            vector.drain().then_inc(s_["sv"], 1)
        vector.wait_ge(s_["st"], mk["tOP"])
        s_ev = a["ub32"][:].rearrange("p k t -> p (k t)")
        vector.tensor_copy(s_ev[:, 0:512], PB[0][:])
        vector.tensor_copy(s_ev[:, 512:1024], PB[1][:])
        vector.tensor_tensor(
            a["xmaskE"][:].rearrange("p h q -> p (h q)"),
            a["x_tm"][:].rearrange("p h q -> p (h q)"),
            e127bf.broadcast_to([128, NH * HP]), OP.mult)
        vector.drain().then_inc(s_["sv"], 1)
    b.cnt["sv"] = mk["vSEV"]

    @b.S
    def _(scalar):
        s_ = b.sem
        scalar.wait_ge(s_["sv"], mk["vLN1"])
        scalar.copy(a["PSb"][:].rearrange("p k t -> p (k t)"),
                    a["ub32"][:].rearrange("p k t -> p (k t)"))
        scalar.activation(a["xnb"][:].rearrange("p k t -> p (k t)"),
                          a["ub32"][:].rearrange("p k t -> p (k t)"), AF.Square)
        scalar.drain().then_inc(s_["ss"], 1)
        scalar.wait_ge(s_["sv"], mk["vLN2"])
        scalar.activation(a["smt"][0:1, 0:128], a["Gbuf"][:].rearrange("p h q -> p (h q)")[0:1, 640:768], AF.Ln,
                          bias=1e-5)
        scalar.activation(a["Gbuf"][:].rearrange("p h q -> p (h q)")[0:1, 640:768], a["smt"][0:1, 0:128], AF.Exp,
                          scale=-0.5)
        scalar.drain().then_inc(s_["ss"], 1)
        scalar.wait_ge(s_["st"], mk["tBC"])
        scalar.copy(a["Gbuf"][:].rearrange("p h q -> p (h q)")[:, 768:1024],
                    PB[5][:, 0:256])
        scalar.drain().then_inc(s_["ss"], 1)
        scalar.wait_ge(s_["st"], mk["tIP0"])
        scalar.copy(a["sml"][:], PB[0][:, 0:384])
        scalar.drain().then_inc(s_["ss"], 1)
        scalar.wait_ge(s_["sv"], mk["vSM1"])
        scalar.activation(a["smt"][:, 64:128], a["smt"][:, 0:64], AF.Exp)
        scalar.activation(a["smt"][:, 0:64], a["smt"][:, 64:128], AF.Ln, bias=1.0)
        scalar.activation(a["smt"][:, 64:96], a["sml"][:, 320:352], AF.Exp,
                          scale=-1.0)
        scalar.drain().then_inc(s_["ss"], 1)
        scalar.wait_ge(s_["sv"], mk["vCS1"])
        scalar.activation(a["qw"][:], a["wvec"][:], AF.Ln)
        scalar.activation(a["alpha"][:], a["cs_tm"][:], AF.Exp)
        scalar.activation(a["decay"][:], PB[6][:, 64:96], AF.Exp)
        scalar.drain().then_inc(s_["ss"], 1)
        scalar.wait_ge(s_["sv"], mk["vQW"])
        scalar.activation(a["qh_e"][:], a["smt"][:, 0:32], AF.Exp)
        scalar.copy(a["d10bf"][:], PB[6][:, 96:128])
        scalar.drain().then_inc(s_["ss"], 1)
        xf = a["x_tm"][:].rearrange("p h q -> p (h q)")
        scalar.wait_ge(s_["st"], mk["tIP5"])
        scalar.copy(xf[:, 0:512], PB[1][:])
        scalar.drain().then_inc(s_["ss"], 1)
        scalar.wait_ge(s_["st"], mk["tIP6"])
        scalar.copy(xf[:, 512:1024], PB[0][:])
        scalar.drain().then_inc(s_["ss"], 1)
        Gf = a["Gbuf"][:].rearrange("p h q -> p (h q)")
        sf = a["sin_t"][:].rearrange("p h q -> p (h q)")
        cf = a["cos_t"][:].rearrange("p h q -> p (h q)")
        scalar.wait_ge(s_["sv"], mk["vRS0"])
        scalar.activation(sf[:, 0:512], Gf[:, 0:512], AF.Sin)
        scalar.drain().then_inc(s_["ss"], 1)
        scalar.wait_ge(s_["sv"], mk["vRC0"])
        scalar.activation(cf[:, 0:512], Gf[:, 0:512], AF.Sin)
        scalar.drain().then_inc(s_["ss"], 1)
        scalar.wait_ge(s_["st"], mk["tIP7"])
        scalar.copy(xf[:, 1024:1536], PB[1][:])
        scalar.drain().then_inc(s_["ss"], 1)
        scalar.wait_ge(s_["st"], mk["tIP8"])
        scalar.copy(xf[:, 1536:2048], PB[0][:])
        scalar.drain().then_inc(s_["ss"], 1)
        scalar.wait_ge(s_["sv"], mk["vRS1"])
        scalar.activation(sf[:, 512:1024], Gf[:, 512:1024], AF.Sin)
        scalar.drain().then_inc(s_["ss"], 1)
        scalar.wait_ge(s_["sv"], mk["vRC1"])
        scalar.activation(cf[:, 512:1024], Gf[:, 512:1024], AF.Sin)
        scalar.drain().then_inc(s_["ss"], 1)
        scalar.wait_ge(s_["sv"], mk["vIP3"])
        for jz in range(4):
            scalar.activation(a["zs"][:, jz * 512:(jz + 1) * 512],
                              a["zs"][:, jz * 512:(jz + 1) * 512], AF.Silu)
        scalar.drain().then_inc(s_["ss"], 1)
        for nm, wm in [("sRB", "vRMB"), ("sRC", "vRMC")]:
            scalar.wait_ge(s_["sv"], mk[wm])
            scalar.activation(a["acol"][:, 1:2], a["acol"][:, 0:1], AF.Ln,
                              scale=1.0 / NS, bias=1e-5)
            scalar.activation(a["acol"][:, 0:1], a["acol"][:, 1:2], AF.Exp,
                              scale=-0.5)
            scalar.drain().then_inc(s_["ss"], 1)

        def s_csb(g):
            for i in range(8):
                scalar.activation(a["Ebuf"][:, i, :], c["c_maskT"][:], AF.Identity,
                                  scale=1.0,
                                  bias=a["cs_tm"][:, g * 8 + i:g * 8 + i + 1])
            scalar.drain().then_inc(s_["ss"], 1)

        def s_dg(g):
            for i in range(8):
                scalar.activation(a["dgbuf"][:, i, :], c["c_idf32"][:], AF.Copy,
                                  scale=a["dtlam"][:, g * 8 + i:g * 8 + i + 1])
            scalar.drain().then_inc(s_["ss"], 1)

        def s_tre(g):
            scalar.wait_ge(s_["st"], mk[f"tTR{g}"])
            if g >= 2:
                scalar.wait_ge(s_["st"], mk[f"tSCs{g - 2}"])
                scalar.wait_ge(s_["st"], mk[f"tY{g - 2}"])
            pa, pb_ = pair(g)
            scalar.copy(a["bh_sm"][:, g % 2].rearrange("p i t -> p (i t)"),
                        PB[pa][:].bitcast(BF16))
            scalar.copy(a["ch_sm"][:, g % 2].rearrange("p i t -> p (i t)"),
                        PB[pb_][:].bitcast(BF16))
            scalar.drain().then_inc(s_["ss"], 1)

        def s_dmy(n):
            for _ in range(n):
                scalar.drain().then_inc(s_["ss"], 1)
        if DBG_CUT == 1:
            s_csb = s_dg = s_tre = lambda g: s_dmy(1)
        scalar.wait_ge(s_["sv"], mk["vCB"])
        s_csb(0)
        s_dg(0)
        s_tre(0)
        for g in range(4):
            hg = g * 8
            if DBG_CUT == 1:
                s_dmy(5 if g == 0 else (6 if g < 3 else 2))
                continue
            scalar.wait_ge(s_["st"], mk[f"tSCb{g}"])
            pa, pb_ = opair(g)
            for i in range(8):
                scalar.activation(a["PSb"][:, i, :],
                                  PB[pa + i // 4][:, (i % 4) * 128:(i % 4) * 128 + 128],
                                  AF.Exp, bias=a["qw"][:, hg + i:hg + i + 1])
            scalar.drain().then_inc(s_["ss"], 1)
            if g >= 1:
                if DBG_CUT == 2:
                    s_dmy(1)
                else:
                    scalar.wait_ge(s_["st"], mk[f"tYT{g - 1}"])
                    scalar.copy(a["ybar_sm"][:].rearrange("p j t -> p (j t)"),
                                PB[6][:].bitcast(BF16)[:, 0:512])
                    scalar.drain().then_inc(s_["ss"], 1)
            if DBG_CUT == 2:
                s_dmy(1)
            else:
                scalar.wait_ge(s_["st"], mk[f"tY{g}"])
                for i in range(8):
                    scalar.activation(a["eint"][:, i, :],
                                      PB[7][:, i * 64:(i + 1) * 64],
                                      AF.Copy, scale=a["alpha"][:, hg + i:hg + i + 1])
                scalar.drain().then_inc(s_["ss"], 1)
            if g < 3:
                s_csb(g + 1)
                s_tre(g + 1)
                scalar.wait_ge(s_["sv"], mk[f"vPS{g}"])
                s_dg(g + 1)
        if DBG_CUT in (1, 2):
            s_dmy(1)
        else:
            scalar.wait_ge(s_["st"], mk["tYT3"])
            scalar.copy(a["ybar_sm"][:].rearrange("p j t -> p (j t)"),
                        PB[6][:].bitcast(BF16)[:, 0:512])
            scalar.drain().then_inc(s_["ss"], 1)
    b.cnt["ss"] = mk["sYSM3"]

    b.prev_a_marks = {"p_tY3": mk["tY3"], "p_sE3": mk["sE3"], "p_tOP": mk["tOP"]}

    b.tap("xnb", a["xnb"][:].rearrange("p k t -> p (k t)"), bi, ci)
    b.tap("xtm", a["x_tm"][:].rearrange("p h q -> p (h q)"), bi, ci)
    b.tap("zs", a["zs"][:], bi, ci)
    b.tap("sml", a["sml"][:], bi, ci)
    for nm in ("dt_", "adt", "lam", "wvec", "qw", "alpha", "qh_e", "decay",
               "cs_tm", "dtlam", "dt1ml"):
        b.tap(nm, a[nm][:], bi, ci)
    b.tap("sin", a["sin_t"][:].rearrange("p h q -> p (h q)"), bi, ci)
    b.tap("cos", a["cos_t"][:].rearrange("p h q -> p (h q)"), bi, ci)
    b.tap("bh", a["bh_all"][:, cur].rearrange("p h q -> p (h q)"), bi, ci)
    b.tap("ch", a["ch_all"][:].rearrange("p h q -> p (h q)"), bi, ci)
    b.tap("hst", a["Hst"][:], bi, ci)
    b.tap("sev", a["ub32"][:].rearrange("p k t -> p (k t)"), bi, ci)
    b.tap("psb", a["PSb"][:].rearrange("p i t -> p (i t)"), bi, ci)
    b.tap("ybar3", a["ybar_g"][:], bi, ci)
    b.tap("eint", a["eint"][:].rearrange("p i q -> p (i q)"), bi, ci)
    b.tap("carry", a["carry"][:], bi, ci)


def _chunk_b1(b, d, c, p, PB, bi, ci):
    mk = {}

    def plan(ch, names, step=1):
        v = b.cnt[ch]
        for n in names:
            v += step
            mk[n] = v

    plan("sv", ["vM8", "vSC", "vSR", "vXQ"] + [f"vGU{sl}" for sl in range(8)]
         + ["vGM"])
    plan("st", ["tM8", "tSR", "tSB"] + [x for sl in range(8)
                                        for x in (f"tG{sl}", f"tU{sl}")])
    plan("ss", [f"sG{sl}" for sl in range(8)])
    plan("sd", ["dXNF", "dGU"], step=16)

    prev_dGU = mk["dXNF"] - 16
    buf = (bi * NCH + ci) % 2
    prev_vXQ = getattr(b, "prev_b1_vXQ", 0)
    xnf = p["xnf"][:, buf]

    @b.D
    def _(sync):
        b.wait(sync, "sd", prev_dGU)
        sync.wait_ge(b.sem["sv"], prev_vXQ)
        sync.dma_start(xnf.rearrange("p k t -> p (k t)"),
                       d["sc_xn"][bi, ci]).then_inc(b.sem["sd"], 16)
        sync.wait_ge(b.sem["sv"], mk["vGU7"])
        sync.dma_start(d["sc_gu"][bi, ci], p["gu_b"][:]).then_inc(b.sem["sd"], 16)
    b.cnt["sd"] = mk["dGU"]
    b.prev_b1_vXQ = mk["vXQ"]

    @b.T
    def _(tensor):
        s_ = b.sem
        tensor.wait_ge(s_["sv"], mk["vM8"])
        tensor.transpose(PB[4][:, 0:128], p["m8"][:], c["c_idf32"][:])
        tensor.drain().then_inc(s_["st"], 1)
        tensor.wait_ge(s_["sv"], mk["vSC"])
        tensor.transpose(PB[4][0:1, 128:256], p["scol"][:, 1:2], c["c_idf32"][:])
        tensor.drain().then_inc(s_["st"], 1)
        tensor.wait_ge(s_["sv"], mk["vSR"])
        tensor.matmul(PB[5][:, 0:128], c["c_ones"][0:1, :], p["srow"][0:1, :],
                      start=True, stop=True)
        tensor.drain().then_inc(s_["st"], 1)
        tensor.wait_ge(s_["sv"], mk["vXQ"])
        for sl in range(8):
            if sl >= 2:
                tensor.wait_ge(s_["ss"], mk[f"sG{sl - 2}"])
            for k in range(8):
                tensor.matmul(PB[sl % 2][:], p["xq_fm"][:, k, :],
                              p["wgq"][:, k, sl * 512:(sl + 1) * 512],
                              start=(k == 0), stop=(k == 7))
            tensor.drain().then_inc(s_["st"], 1)
            if sl >= 2:
                tensor.wait_ge(s_["sv"], mk[f"vGU{sl - 2}"])
            for k in range(8):
                tensor.matmul(PB[2 + sl % 2][:], p["xq_fm"][:, k, :],
                              p["wuq"][:, k, sl * 512:(sl + 1) * 512],
                              start=(k == 0), stop=(k == 7))
            tensor.drain().then_inc(s_["st"], 1)
    b.cnt["st"] = mk["tU7"]
    prev_tU7 = getattr(b, "prev_b1_tU7", 0)
    b.prev_b1_tU7 = mk["tU7"]

    @b.V
    def _(vector):
        s_ = b.sem
        vector.wait_ge(s_["sd"], mk["dXNF"])
        vector.scalar_tensor_tensor(p["m8"][:], xnf[:, 0, :], -1.0,
                                    xnf[:, 0, :], OP.mult, OP.max)
        for k in range(1, 8):
            vector.scalar_tensor_tensor(p["mt"][:], xnf[:, k, :], -1.0,
                                        xnf[:, k, :], OP.mult, OP.max)
            vector.tensor_tensor(p["m8"][:], p["m8"][:], p["mt"][:], OP.max)
        vector.drain().then_inc(s_["sv"], 1)
        vector.wait_ge(s_["st"], mk["tM8"])
        vector.tensor_reduce(p["scol"][:, 0:1], PB[4][:, 0:128], AX.X, OP.max)
        vector.drain()
        vector.tensor_scalar(p["scol"][:, 0:1], p["scol"][:, 0:1], 1e-5, None,
                             OP.max)
        vector.drain()
        vector.reciprocal(p["scol"][:, 1:2], p["scol"][:, 0:1])
        vector.tensor_scalar(p["scol"][:, 2:3], p["scol"][:, 0:1],
                             c["wsc"][:, 3:4], 1.0 / 127.0, OP.mult, OP.mult)
        vector.tensor_scalar(p["scol"][:, 3:4], p["scol"][:, 0:1],
                             c["wsc"][:, 4:5], 1.0 / 127.0, OP.mult, OP.mult)
        vector.drain()
        vector.tensor_scalar(p["scol"][:, 1:2], p["scol"][:, 1:2], 127.0, None,
                             OP.mult)
        vector.drain().then_inc(s_["sv"], 1)
        vector.wait_ge(s_["st"], mk["tSR"])
        vector.tensor_copy(p["srow"][0:1, :], PB[4][0:1, 128:256])
        vector.drain().then_inc(s_["sv"], 1)
        vector.wait_ge(s_["st"], mk["tSB"])
        vector.wait_ge(s_["st"], prev_tU7)
        for k in range(8):
            vector.tensor_tensor(p["mt"][:], xnf[:, k, :], PB[5][:, 0:128],
                                 OP.mult)
            vector.tensor_scalar(p["xq_fm"][:, k, :], p["mt"][:], MAGIC, MAGIC,
                                 OP.add, OP.subtract)
        vector.drain().then_inc(s_["sv"], 1)
        for sl in range(8):
            vector.wait_ge(s_["st"], mk[f"tU{sl}"])
            vector.wait_ge(s_["ss"], mk[f"sG{sl}"])
            vector.scalar_tensor_tensor(
                p["gu_b"][:, sl * 512:(sl + 1) * 512], PB[2 + sl % 2][:],
                p["scol"][:, 3:4], p["gu_b"][:, sl * 512:(sl + 1) * 512],
                OP.mult, OP.mult)
            vector.drain().then_inc(s_["sv"], 1)
        vector.tensor_reduce(c["gumax"][:, bi * NCH + ci:bi * NCH + ci + 1],
                             p["gu_b"][:], AX.X, OP.max, apply_absolute_value=True)
        vector.drain().then_inc(s_["sv"], 1)
    b.cnt["sv"] = mk["vGM"]

    @b.S
    def _(scalar):
        s_ = b.sem
        scalar.wait_ge(s_["sd"], prev_dGU)
        for sl in range(8):
            scalar.wait_ge(s_["st"], mk[f"tG{sl}"])
            scalar.activation(p["gu_b"][:, sl * 512:(sl + 1) * 512],
                              PB[sl % 2][:], AF.Silu, scale=p["scol"][:, 2:3])
            scalar.drain().then_inc(s_["ss"], 1)
    b.cnt["ss"] = mk["sG7"]

    b.tap("xqfm", p["xq_fm"][:].rearrange("p k t -> p (k t)"), bi, ci)
    b.tap("gu", p["gu_b"][:], bi, ci)
    b.tap("b1scol", p["scol"][:], bi, ci)


def _chunk_b2(b, d, c, q, PB, bi, ci):
    mk = {}

    def plan(ch, names, step=1):
        v = b.cnt[ch]
        for n in names:
            v += step
            mk[n] = v

    plan("sv", ["vGQ"] + [f"vTQ{g4}" for g4 in range(4)] + ["vOB"])
    plan("st", [f"tTQ{g4}" for g4 in range(4)] + ["tD0", "tD1"])
    plan("ss", ["sD0", "sD1"])
    mk["dIN"] = b.cnt["sd"] + 48
    mk["dOUT"] = b.cnt["sd"] + 64

    buf = (bi * NCH + ci) % 2
    prev_vGQ = getattr(b, "prev_b2_vGQ", 0)
    prev_vOB = getattr(b, "prev_b2_vOB", 0)
    gu_tm = q["gu_tm"][:, buf]
    u_b = q["u_b"][:, buf]
    s_b = q["s_b"][:, buf]

    @b.D
    def _(sync):
        b.wait(sync, "sd", mk["dIN"] - 48)
        sync.wait_ge(b.sem["sv"], prev_vGQ)
        sync.dma_start(gu_tm, d["sc_gu"][bi, ci]).then_inc(b.sem["sd"], 16)
        sync.dma_start(u_b, d["u_tm"][bi, ci * 128:(ci + 1) * 128, :]
                       ).then_inc(b.sem["sd"], 16)
        sync.dma_start(s_b, d["sc_s"][bi, ci]).then_inc(b.sem["sd"], 16)
        sync.wait_ge(b.sem["sv"], mk["vOB"])
        sync.dma_start(d["out"][bi, ci * 128:(ci + 1) * 128, :], q["o_b"][:]
                       ).then_inc(b.sem["sd"], 16)
    b.cnt["sd"] = mk["dOUT"]
    b.prev_b2_vGQ = mk["vGQ"]
    b.prev_b2_vOB = mk["vOB"]

    @b.T
    def _(tensor):
        s_ = b.sem
        tensor.wait_ge(s_["sv"], mk["vGQ"])
        for g4 in range(4):
            if g4 >= 2:
                tensor.wait_ge(s_["sv"], mk[f"vTQ{g4 - 2}"])
            pbf = PB[4 + g4 % 2][:].bitcast(BF16)
            for i in range(8):
                tensor.transpose(
                    pbf[:, i * 128:(i + 1) * 128],
                    q["guq_tm"][:, (g4 * 8 + i) * 128:(g4 * 8 + i + 1) * 128],
                    c["c_idbf"][:])
            tensor.drain().then_inc(s_["st"], 1)
        tensor.wait_ge(s_["sv"], mk["vTQ3"])
        for sl in range(2):
            for k in range(32):
                tensor.matmul(PB[sl][:], q["guq_fm"][:, k, :],
                              q["wdq"][:, k, sl * 512:(sl + 1) * 512],
                              start=(k == 0), stop=(k == 31))
            tensor.drain().then_inc(s_["st"], 1)
    b.cnt["st"] = mk["tD1"]

    @b.V
    def _(vector):
        s_ = b.sem
        vector.wait_ge(s_["sd"], mk["dIN"])
        vector.tensor_scalar(q["scol2"][:, 0:1],
                             c["gumax"][:, bi * NCH + ci:bi * NCH + ci + 1],
                             1e-5, None, OP.max)
        vector.drain()
        vector.reciprocal(q["scol2"][:, 1:2], q["scol2"][:, 0:1])
        vector.tensor_scalar(q["scol2"][:, 2:3], q["scol2"][:, 0:1],
                             c["wsc"][:, 5:6], 1.0 / 127.0, OP.mult, OP.mult)
        vector.drain()
        vector.tensor_scalar(q["scol2"][:, 1:2], q["scol2"][:, 1:2], 127.0, None,
                             OP.mult)
        vector.drain()
        vector.tensor_tensor(q["qtmp"][:], gu_tm,
                             q["scol2"][:, 1:2].broadcast_to([128, FI]), OP.mult)
        vector.tensor_scalar(q["qtmp"][:], q["qtmp"][:], MAGIC, MAGIC, OP.add,
                             OP.subtract)
        vector.tensor_copy(q["guq_tm"][:], q["qtmp"][:])
        vector.drain().then_inc(s_["sv"], 1)
        for g4 in range(4):
            vector.wait_ge(s_["st"], mk[f"tTQ{g4}"])
            vector.tensor_copy(
                q["guq_fm"][:].rearrange("p k t -> p (k t)")[
                    :, g4 * 1024:(g4 + 1) * 1024], PB[4 + g4 % 2][:].bitcast(BF16))
            vector.drain().then_inc(s_["sv"], 1)
        vector.wait_ge(s_["ss"], mk["sD1"])
        b.wait(vector, "sd", mk["dOUT"] - 64)
        vector.tensor_tensor(q["o_b"][:], s_b, q["g_b"][:], OP.mult)
        vector.drain()
        vector.tensor_tensor(q["o_b"][:], q["o_b"][:], u_b, OP.add)
        vector.drain().then_inc(s_["sv"], 1)
    b.cnt["sv"] = mk["vOB"]

    @b.S
    def _(scalar):
        s_ = b.sem
        for sl in range(2):
            scalar.wait_ge(s_["st"], mk[f"tD{sl}"])
            scalar.activation(q["g_b"][:, sl * 512:(sl + 1) * 512], PB[sl][:],
                              AF.Silu, scale=q["scol2"][:, 2:3])
            scalar.drain().then_inc(s_["ss"], 1)
    b.cnt["ss"] = mk["sD1"]

    b.tap("guq", q["guq_tm"][:], bi, ci)
    b.tap("b2scol", q["scol2"][:], bi, ci)
    b.tap("gutm", gu_tm, bi, ci)
    b.tap("gb", q["g_b"][:], bi, ci)
    b.tap("ob", q["o_b"][:], bi, ci)


def _host_inputs(inputs, core):
    f = np.float32
    u = np.asarray(inputs["L0_attn_u"], f)[core * NB:(core + 1) * NB]
    ins = {
        "u_tm": np.ascontiguousarray(u),
        "u_fm": np.ascontiguousarray(u.transpose(0, 2, 1)),
        "w_inT": np.ascontiguousarray(np.asarray(inputs["L0_attn_W_in"], f).T),
        "w_outT": np.ascontiguousarray(np.asarray(inputs["L0_attn_W_out"], f).T),
        "w_gT": np.ascontiguousarray(np.asarray(inputs["L0_ffn_w_gate"], f).T),
        "w_uT": np.ascontiguousarray(np.asarray(inputs["L0_ffn_w_up"], f).T),
        "w_dT": np.ascontiguousarray(np.asarray(inputs["L0_ffn_w_down"], f).T),
        "dtb_rep": np.ascontiguousarray(
            np.broadcast_to(np.asarray(inputs["L0_attn_dt_bias"], f), (128, NH))),
        "bbias_row": np.asarray(inputs["L0_attn_B_bias"], f).reshape(1, NH * NS),
        "cbias_row": np.asarray(inputs["L0_attn_C_bias"], f).reshape(1, NH * NS),
        "bnw_rep": np.ascontiguousarray(
            np.broadcast_to(np.asarray(inputs["L0_attn_B_norm_w"], f), (128, NS))),
        "cnw_rep": np.ascontiguousarray(
            np.broadcast_to(np.asarray(inputs["L0_attn_C_norm_w"], f), (128, NS))),
        "d_rep": np.ascontiguousarray(
            np.broadcast_to(np.asarray(inputs["L0_attn_D_param"], f), (128, NH))),
    }
    return ins


def _host_consts():
    f = np.float32
    idx = np.arange(128)
    L = (idx[:, None] <= idx[None, :]).astype(f)
    shift = (idx[:, None] == idx[None, :] + 1).astype(f)
    mask = np.where(idx[None, :] > idx[:, None], 0.0, NEG).astype(f)
    ones = np.ones((128, 128), f)
    e0 = np.zeros((128, 128), f)
    e0[0, :] = 1.0
    ident = np.eye(128, dtype=f)
    cols = np.zeros((128, 4), f)
    cols[:, 0] = 1.0
    cols[127, 1] = 1.0
    cols[:, 2] = 1e-5
    colsbf = np.zeros((128, 2), ml_dtypes.bfloat16)
    colsbf[:, 0] = 1.0
    colsbf[127, 1] = 1.0
    return {
        "c_L": L, "c_shift": shift, "c_mask": mask,
        "c_maskT": np.ascontiguousarray(mask.T), "c_ones": ones, "c_e0": e0,
        "c_idf32": ident, "c_idbf": ident.astype(ml_dtypes.bfloat16),
        "c_cols": cols, "c_colsbf": colsbf,
    }


_CACHE = {}


def kernel(**inputs):
    if "nc" not in _CACHE:
        nc = bass.Bass("TRN2", target_bir_lowering=False, debug=False)
        build(nc)
        _CACHE["nc"] = nc
    nc = _CACHE["nc"]
    consts = _host_consts()
    in_maps = []
    for core in range(8):
        m = _host_inputs(inputs, core)
        m.update(consts)
        in_maps.append(m)
    res = run_bass_kernel_spmd(nc, in_maps, core_ids=list(range(8)))
    out = np.concatenate([np.asarray(res.results[cc]["out"]) for cc in range(8)],
                         axis=0)
    return out.astype(np.float32)


# version 12
# speedup vs baseline: 1.1945x; 1.0374x over previous
import numpy as np
import ml_dtypes
from contextlib import ExitStack

import concourse.bass as bass
import concourse.mybir as mybir
from concourse.bass_utils import run_bass_kernel_spmd

F32 = mybir.dt.float32
BF16 = mybir.dt.bfloat16
AX = mybir.AxisListType
OP = mybir.AluOpType
AF = mybir.ActivationFunctionType

TAPS = None
DBG_CUT = None

MAGIC = float(3 * 2.0 ** 22)
PI = float(np.pi)
NEG = -1e30

T = 2048
NB = 2
NCH = T // 128
DM = 1024
DI = 2048
NH = 32
HP = 64
NS = 128
NA = 32
DIN = 4480
FI = 4096


class B:

    def tap(self, name, apv, bi=None, ci=None):
        if TAPS is None:
            return
        if bi is not None and (bi, ci) != (TAPS.get("bi", 0), TAPS.get("ci", 0)):
            return
        sh = list(apv.shape)
        dt = apv.dtype
        t = self.nc.dram_tensor("tap_" + name, sh, dt, kind="ExternalOutput").ap()

        @self.D
        def _(sync):
            for nm in ("sv", "ss", "st", "sg"):
                self.wait(sync, nm)
            self.inc(sync.dma_start(t, apv), "sd", 16)

    def __init__(self, nc, stack):
        self.nc = nc
        self.sem = {}
        self.cnt = {}
        for name in ("sd", "st", "sv", "ss", "sg"):
            self.sem[name] = stack.enter_context(nc.semaphore(name))
            self.cnt[name] = 0
        self.blk = None

    def inc(self, instr, name, k=1):
        instr.then_inc(self.sem[name], k)
        self.cnt[name] += k
        return self.cnt[name]

    def wait(self, eng, name, val=None):
        v = self.cnt[name] if val is None else val
        if v > 0:
            eng.wait_ge(self.sem[name], v)

    def T(self, f):
        self.blk.tensor(f)

    def V(self, f):
        self.blk.vector(f)

    def S(self, f):
        self.blk.scalar(f)

    def G(self, f):
        self.blk.gpsimd(f)

    def D(self, f):
        self.blk.sync(f)


def build(nc):
    stack = ExitStack()
    b = B(nc, stack)
    ap = lambda t: t.ap()

    d = {}
    d["u_fm"] = ap(nc.dram_tensor("u_fm", [NB, DM, T], F32, kind="ExternalInput"))
    d["u_tm"] = ap(nc.dram_tensor("u_tm", [NB, T, DM], F32, kind="ExternalInput"))
    d["w_inT"] = ap(nc.dram_tensor("w_inT", [DM, DIN], F32, kind="ExternalInput"))
    d["w_outT"] = ap(nc.dram_tensor("w_outT", [DI, DM], F32, kind="ExternalInput"))
    d["w_gT"] = ap(nc.dram_tensor("w_gT", [DM, FI], F32, kind="ExternalInput"))
    d["w_uT"] = ap(nc.dram_tensor("w_uT", [DM, FI], F32, kind="ExternalInput"))
    d["w_dT"] = ap(nc.dram_tensor("w_dT", [FI, DM], F32, kind="ExternalInput"))
    d["dtb_rep"] = ap(nc.dram_tensor("dtb_rep", [128, NH], F32, kind="ExternalInput"))
    d["bbias_row"] = ap(nc.dram_tensor("bbias_row", [1, NH * NS], F32, kind="ExternalInput"))
    d["cbias_row"] = ap(nc.dram_tensor("cbias_row", [1, NH * NS], F32, kind="ExternalInput"))
    d["bnw_rep"] = ap(nc.dram_tensor("bnw_rep", [128, NS], F32, kind="ExternalInput"))
    d["cnw_rep"] = ap(nc.dram_tensor("cnw_rep", [128, NS], F32, kind="ExternalInput"))
    d["d_rep"] = ap(nc.dram_tensor("d_rep", [128, NH], F32, kind="ExternalInput"))
    for cn in ("c_L", "c_shift", "c_mask", "c_maskT", "c_ones", "c_e0", "c_idf32"):
        d[cn] = ap(nc.dram_tensor(cn, [128, 128], F32, kind="ExternalInput"))
    d["c_idbf"] = ap(nc.dram_tensor("c_idbf", [128, 128], BF16, kind="ExternalInput"))
    d["c_cols"] = ap(nc.dram_tensor("c_cols", [128, 4], F32, kind="ExternalInput"))
    d["c_colsbf"] = ap(nc.dram_tensor("c_colsbf", [128, 2], BF16, kind="ExternalInput"))
    d["out"] = ap(nc.dram_tensor("out", [NB, T, DM], F32, kind="ExternalOutput"))
    d["sc_xn"] = ap(nc.dram_tensor("sc_xn", [NB, NCH, 128, DM], BF16, kind="Internal"))
    d["sc_s"] = ap(nc.dram_tensor("sc_s", [NB, NCH, 128, DM], F32, kind="Internal"))
    d["sc_gu"] = ap(nc.dram_tensor("sc_gu", [NB, NCH, 128, FI], BF16, kind="Internal"))
    d["sc_wqg"] = ap(nc.dram_tensor("sc_wqg", [8, 128, FI], BF16, kind="Internal"))
    d["sc_wqu"] = ap(nc.dram_tensor("sc_wqu", [8, 128, FI], BF16, kind="Internal"))
    d["sc_wqd"] = ap(nc.dram_tensor("sc_wqd", [32, 128, DM], BF16, kind="Internal"))

    with stack:
        c = {}
        for cn in ("c_L", "c_shift", "c_mask", "c_maskT", "c_ones", "c_e0",
                   "c_idf32"):
            c[cn] = stack.enter_context(nc.sbuf_tensor("sb" + cn, [128, 128], F32))
        c["c_idbf"] = stack.enter_context(nc.sbuf_tensor("sbc_idbf", [128, 128], BF16))
        c["c_cols"] = stack.enter_context(nc.sbuf_tensor("sbc_cols", [128, 4], F32))
        c["c_colsbf"] = stack.enter_context(nc.sbuf_tensor("sbc_colsbf", [128, 2], BF16))
        c["dtb"] = stack.enter_context(nc.sbuf_tensor("dtb", [128, NH], F32))
        c["bnw"] = stack.enter_context(nc.sbuf_tensor("bnw", [128, NS], F32))
        c["cnw"] = stack.enter_context(nc.sbuf_tensor("cnw", [128, NS], F32))
        c["drep"] = stack.enter_context(nc.sbuf_tensor("drep", [128, NH], F32))
        c["drepbf"] = stack.enter_context(nc.sbuf_tensor("drepbf", [128, NH], BF16))
        c["wsc"] = stack.enter_context(nc.sbuf_tensor("wsc", [128, 8], F32))
        c["gumax"] = stack.enter_context(nc.sbuf_tensor("gumax", [128, NB * NCH], F32))
        PB = [stack.enter_context(nc.psum_tensor(f"pb{i}", [128, 512], F32))
              for i in range(8)]
        nc.const_aps.register(F32, 1.0, c["c_cols"][:, 0:1])
        nc.const_aps.register(F32, 1e-5, c["c_cols"][:, 2:3])
        nc.const_aps.register(F32, 0.0, c["c_cols"][:, 3:4])

        with nc.Block("init") as ib:
            @ib.sync
            def _(sync):
                for k in b.sem:
                    sync.sem_clear(b.sem[k])

        with ExitStack() as asc:
            a = {}

            def asb(n, sh, dt=F32):
                a[n] = asc.enter_context(nc.sbuf_tensor(n, sh, dt))
                return a[n]

            asb("winT", [128, 8, DIN], BF16)
            asb("woutT", [128, 16, DM], BF16)
            asb("bias_b", [128, NH * NS], BF16)
            asb("bias_c", [128, NH * NS], BF16)
            asb("Hst", [128, NH * HP], BF16)
            asb("carry", [1, NH * NA], F32)
            asb("acol", [128, 2], F32)

            with nc.Block("main_a") as blk:
                b.blk = blk
                with ExitStack() as pro:
                    a["wtmp"] = pro.enter_context(nc.sbuf_tensor("wtmp", [128, DIN], F32))
                    a["wqtmp"] = pro.enter_context(nc.sbuf_tensor("wqtmp", [128, FI], BF16))
                    a["row2"] = pro.enter_context(nc.sbuf_tensor("row2", [1, 2], F32))
                    _prologue(b, d, c, a, PB)
                del a["wtmp"], a["wqtmp"], a["row2"]
                asb("ub32", [128, 8, 128], F32)
                asb("xnb", [128, 8, 128], BF16)
                asb("x_tm", [128, NH, HP], BF16)
                asb("xqh", [128, NH, HP], BF16)
                asb("xmaskE", [128, NH, HP], BF16)
                asb("zs", [128, DI], BF16)
                asb("sml", [128, 384], F32)
                asb("smt", [128, 128], F32)
                for n in ("dt_", "adt", "lam", "dtlam", "dt1ml", "wvec", "qw",
                          "cs_tm", "alpha"):
                    asb(n, [128, NH], F32)
                for n in ("qh_e", "decay", "d10bf"):
                    asb(n, [128, NH], BF16)
                asb("Gbuf", [128, NH, NA], F32)
                asb("sin_t", [128, NH, NA], BF16)
                asb("cos_t", [128, NH, NA], BF16)
                asb("bh_all", [128, 2, NH, NS], BF16)
                asb("ch_all", [128, NH, NS], BF16)
                asb("bh_sm", [128, 2, 8, 128], BF16)
                asb("ch_sm", [128, 2, 8, 128], BF16)
                asb("Ebuf", [128, 8, 128], F32)
                asb("dgbuf", [128, 8, 128], BF16)
                asb("eint", [128, 8, HP], BF16)
                asb("PSb", [128, 8, 128], BF16)
                asb("ybar_g", [128, 512], BF16)
                asb("ybar_sm", [128, 4, 128], BF16)
                for bi in range(NB):
                    _batch_reset(b, a)
                    for ci in range(NCH):
                        _chunk_a(b, d, c, a, PB, bi, ci)

        with ExitStack() as bsc:
            p = {}

            def bsb(n, sh, dt=F32):
                p[n] = bsc.enter_context(nc.sbuf_tensor(n, sh, dt))
                return p[n]

            bsb("wgq", [128, 8, FI], BF16)
            bsb("wuq", [128, 8, FI], BF16)
            bsb("xnf", [128, 2, 8, 128], BF16)
            bsb("xq_fm", [128, 8, 128], BF16)
            bsb("m8", [128, 128], F32)
            bsb("mt", [128, 128], F32)
            bsb("srow", [1, 128], F32)
            bsb("scol", [128, 4], F32)
            bsb("gu_b", [128, FI], BF16)
            with nc.Block("main_b1") as blk:
                b.blk = blk

                @b.D
                def _(sync):
                    b.wait(sync, "sv")
                    b.wait(sync, "ss")
                    b.wait(sync, "sd")
                    for k in range(8):
                        b.inc(sync.dma_start(p["wgq"][:, k, :], d["sc_wqg"][k]), "sd", 16)
                        b.inc(sync.dma_start(p["wuq"][:, k, :], d["sc_wqu"][k]), "sd", 16)
                for bi in range(NB):
                    for ci in range(NCH):
                        _chunk_b1(b, d, c, p, PB, bi, ci)

        with ExitStack() as bsc2:
            q = {}

            def qsb(n, sh, dt=F32):
                q[n] = bsc2.enter_context(nc.sbuf_tensor(n, sh, dt))
                return q[n]

            qsb("wdq", [128, 32, DM], BF16)
            qsb("gu_tm", [128, 2, FI], BF16)
            qsb("guq_tm", [128, FI], BF16)
            qsb("qtmp", [128, FI], F32)
            qsb("guq_fm", [128, 32, 128], BF16)
            qsb("u_b", [128, 2, DM], F32)
            qsb("s_b", [128, 2, DM], F32)
            qsb("g_b", [128, DM], F32)
            qsb("o_b", [128, DM], F32)
            qsb("scol2", [128, 4], F32)
            with nc.Block("main_b2") as blk:
                b.blk = blk

                @b.D
                def _(sync):
                    b.wait(sync, "sv")
                    b.wait(sync, "ss")
                    b.wait(sync, "sd")
                    for k in range(32):
                        b.inc(sync.dma_start(q["wdq"][:, k, :], d["sc_wqd"][k]), "sd", 16)
                for bi in range(NB):
                    for ci in range(NCH):
                        _chunk_b2(b, d, c, q, PB, bi, ci)

            with nc.Block("fin") as blk:
                @blk.sync
                def _(sync):
                    b.wait(sync, "sd")
    return nc


def _prologue(b, d, c, a, PB):
    @b.D
    def _(sync):
        for dst, key in [(c["c_L"], "c_L"), (c["c_shift"], "c_shift"),
                         (c["c_mask"], "c_mask"), (c["c_maskT"], "c_maskT"),
                         (c["c_ones"], "c_ones"),
                         (c["c_e0"], "c_e0"), (c["c_idf32"], "c_idf32"),
                         (c["c_idbf"], "c_idbf"), (c["c_cols"], "c_cols"),
                         (c["c_colsbf"], "c_colsbf"), (c["dtb"], "dtb_rep"),
                         (c["bnw"], "bnw_rep"), (c["cnw"], "cnw_rep"),
                         (c["drep"], "d_rep")]:
            b.inc(sync.dma_start(dst[:], d[key]), "sd", 16)
        b.inc(sync.dma_start(a["wtmp"][0:1, 0:NH * NS], d["bbias_row"]), "sd", 16)

    @b.V
    def _(vector):
        b.wait(vector, "sd")
        vector.tensor_copy(c["drepbf"][:], c["drep"][:])
        b.inc(vector.drain(), "sv")

    for key, dstname in [("cbias_row", "bias_b"), (None, "bias_c")]:
        for j in range(8):
            @b.T
            def _(tensor, j=j):
                b.wait(tensor, "sd")
                b.wait(tensor, "sv")
                tensor.matmul(PB[j % 2][:], c["c_ones"][0:1, :],
                              a["wtmp"][0:1, j * 512:(j + 1) * 512],
                              start=True, stop=True)
                b.inc(tensor.drain(), "st")

            @b.V
            def _(vector, dstname=dstname, j=j):
                b.wait(vector, "st")
                vector.tensor_copy(a[dstname][:, j * 512:(j + 1) * 512], PB[j % 2][:])
                b.inc(vector.drain(), "sv")

        if key is not None:
            @b.D
            def _(sync, key=key):
                b.wait(sync, "st")
                b.inc(sync.dma_start(a["wtmp"][0:1, 0:NH * NS], d[key]), "sd", 16)

    for k in range(8):
        @b.D
        def _(sync, k=k):
            b.wait(sync, "sv")
            b.inc(sync.dma_start(a["wtmp"][:], d["w_inT"][k * 128:(k + 1) * 128, :]),
                  "sd", 16)

        @b.V
        def _(vector, k=k):
            b.wait(vector, "sd")
            vector.tensor_copy(a["winT"][:, k, :], a["wtmp"][:])
            b.inc(vector.drain(), "sv")
    for k in range(16):
        @b.D
        def _(sync, k=k):
            b.wait(sync, "sv")
            b.inc(sync.dma_start(a["wtmp"][:, 0:DM],
                                 d["w_outT"][k * 128:(k + 1) * 128, :]), "sd", 16)

        @b.V
        def _(vector, k=k):
            b.wait(vector, "sd")
            vector.tensor_copy(a["woutT"][:, k, :], a["wtmp"][:, 0:DM])
            b.inc(vector.drain(), "sv")

    for wi, (dkey, skey, nkw, wid) in enumerate(
            [("w_gT", "sc_wqg", 8, FI), ("w_uT", "sc_wqu", 8, FI),
             ("w_dT", "sc_wqd", 32, DM)]):
        @b.V
        def _(vector):
            vector.memset(a["acol"][:, 0:1], 0.0)
            b.inc(vector.drain(), "sv")
        for k in range(nkw):
            @b.D
            def _(sync, k=k, dkey=dkey, wid=wid):
                b.wait(sync, "sv")
                b.inc(sync.dma_start(a["wtmp"][:, 0:wid],
                                     d[dkey][k * 128:(k + 1) * 128, :]), "sd", 16)

            @b.V
            def _(vector, wid=wid):
                b.wait(vector, "sd")
                vector.tensor_reduce(a["acol"][:, 1:2], a["wtmp"][:, 0:wid], AX.X,
                                     OP.add, apply_absolute_value=True)
                vector.drain()
                vector.tensor_tensor(a["acol"][:, 0:1], a["acol"][:, 0:1],
                                     a["acol"][:, 1:2], OP.add)
                b.inc(vector.drain(), "sv")

        @b.T
        def _(tensor):
            b.wait(tensor, "sv")
            tensor.matmul(PB[2][0:1, 0:1], c["c_cols"][:, 0:1], a["acol"][:, 0:1],
                          start=True, stop=True)
            b.inc(tensor.drain(), "st")

        @b.V
        def _(vector, nkw=nkw, wid=wid):
            b.wait(vector, "st")
            n_el = float(nkw * 128 * wid)
            vector.tensor_scalar(a["row2"][0:1, 0:1], PB[2][0:1, 0:1], 1.0 / n_el,
                                 1e-5, OP.mult, OP.max)
            vector.drain()
            vector.reciprocal(a["row2"][0:1, 1:2], a["row2"][0:1, 0:1])
            b.inc(vector.drain(), "sv")

        @b.T
        def _(tensor):
            b.wait(tensor, "sv")
            tensor.matmul(PB[2][:, 0:2], c["c_ones"][0:1, :], a["row2"][0:1, 0:2],
                          start=True, stop=True)
            b.inc(tensor.drain(), "st")

        @b.V
        def _(vector, wi=wi):
            b.wait(vector, "st")
            vector.tensor_copy(c["wsc"][:, 3 + wi:4 + wi], PB[2][:, 0:1])
            vector.tensor_copy(c["wsc"][:, wi:wi + 1], PB[2][:, 1:2])
            b.inc(vector.drain(), "sv")

        for k in range(nkw):
            @b.D
            def _(sync, k=k, dkey=dkey, wid=wid):
                b.wait(sync, "sv")
                b.inc(sync.dma_start(a["wtmp"][:, 0:wid],
                                     d[dkey][k * 128:(k + 1) * 128, :]), "sd", 16)

            @b.V
            def _(vector, wi=wi, wid=wid):
                b.wait(vector, "sd")
                vector.tensor_scalar(a["wtmp"][:, 0:wid], a["wtmp"][:, 0:wid],
                                     c["wsc"][:, wi:wi + 1], MAGIC, OP.mult, OP.add)
                vector.tensor_scalar(a["wtmp"][:, 0:wid], a["wtmp"][:, 0:wid],
                                     MAGIC, None, OP.subtract)
                vector.tensor_scalar(a["wqtmp"][:, 0:wid], a["wtmp"][:, 0:wid],
                                     -1.0, 1.0, OP.max, OP.min)
                b.inc(vector.drain(), "sv")

            @b.D
            def _(sync, skey=skey, k=k, wid=wid):
                b.wait(sync, "sv")
                b.inc(sync.dma_start(d[skey][k, :, :], a["wqtmp"][:, 0:wid]), "sd", 16)


def _batch_reset(b, a):
    @b.V
    def _(vector):
        b.wait(vector, "st")
        vector.memset(a["Hst"][:], 0.0)
        vector.memset(a["carry"][:], 0.0)
        b.inc(vector.drain(), "sv")


def _chunk_a(b, d, c, a, PB, bi, ci):
    cur = ci % 2
    prev = 1 - cur
    onesbf = c["c_colsbf"][:, 0:1]
    e127bf = c["c_colsbf"][:, 1:2]
    bh_cur = a["bh_all"][:, cur]
    pos2sl = [8, 0, 1, 2, 3, 4, 5, 6, 7]

    def pair(g):
        return (2, 3) if g % 2 == 0 else (4, 5)

    def opair(g):
        return (4, 5) if g % 2 == 0 else (2, 3)

    mk = {}

    def plan(ch, names, step=1):
        v = b.cnt[ch]
        for n in names:
            v += step
            mk[n] = v

    vnames = ["vLN1", "vLN2", "vXN", "vSM1", "vIP0", "vIP1", "vSM2", "vCS1",
              "vIP2", "vIP3", "vQW", "vG", "vCAR0", "vRS0", "vRC0", "vCAR1",
              "vRS1", "vRC1", "vXM", "vRMB", "vBB", "vRMC", "vCB"]
    for g in range(4):
        vnames += [f"vPS{g}"]
        if ci > 0:
            vnames += [f"vHB{g}"]
        vnames += [f"vYE1{g}", f"vYEV{g}"]
    vnames += ["vSEV"]
    plan("sv", vnames)

    tnames = (["tLN", "tBC"] + [f"tIP{p}" for p in range(5)] +
              ["tCS", "tIP5", "tIP6", "tIP7", "tIP8", "tTH0", "tTH1"])
    tnames += ["tTR0", "tSCs0", "tSCb0", "tTR1"]
    for g in range(4):
        if ci > 0:
            tnames += [f"tBD{g}"]
        tnames += [f"tY{g}"]
        if g < 3:
            tnames += [f"tSCs{g + 1}"]
        tnames += [f"tYB{g}"]
        if g < 3:
            tnames += [f"tSCb{g + 1}"]
        tnames += [f"tYT{g}"]
        if g < 2:
            tnames += [f"tTR{g + 2}"]
    tnames += ["tOP"]
    plan("st", tnames)

    snames = ["sLN0", "sLN", "sBC", "sIP8", "sSM", "sCS", "sQH", "sIP4", "sIP5",
              "sS0", "sC0", "sIP6", "sIP7", "sS1", "sC1", "sZS", "sRB", "sRC",
              "sCSB0", "sTRE0"]
    for g in range(4):
        snames += [f"sE{g}"]
        if g < 3:
            snames += [f"sCSB{g + 1}"]
        if g >= 1:
            snames += [f"sYSM{g - 1}"]
        if g < 3:
            snames += [f"sTRE{g + 1}"]
    snames += ["sYSM3"]
    plan("ss", snames)
    plan("sd", ["dU", "dXN", "dS"], step=16)
    prev_mk = getattr(b, "prev_a_marks", {})

    def W(eng, ch, name):
        if name in mk:
            eng.wait_ge(b.sem[ch], mk[name])
        elif name in prev_mk:
            eng.wait_ge(b.sem[ch], prev_mk[name])

    @b.D
    def _(sync):
        b.wait(sync, "sd", mk["dU"] - 16)
        src = d["u_fm"][bi].rearrange("(k p) t -> p k t", p=128)[
            :, :, ci * 128:(ci + 1) * 128]
        sync.dma_start(a["ub32"][:], src).then_inc(b.sem["sd"], 16)
        sync.wait_ge(b.sem["sv"], mk["vXN"])
        sync.dma_start(d["sc_xn"][bi, ci],
                       a["xnb"][:].rearrange("p k t -> p (k t)")
                       ).then_inc(b.sem["sd"], 16)
        sync.wait_ge(b.sem["sv"], mk["vSEV"])
        sync.dma_start(d["sc_s"][bi, ci],
                       a["ub32"][:].rearrange("p k t -> p (k t)")
                       ).then_inc(b.sem["sd"], 16)
    b.cnt["sd"] = mk["dS"]

    @b.T
    def _(tensor):
        s_ = b.sem
        tensor.wait_ge(s_["sd"], mk["dU"])
        tensor.wait_ge(s_["ss"], mk["sLN0"])
        for k in range(8):
            tensor.matmul(PB[4][0:1, 0:128], onesbf, a["PSb"][:, k, :],
                          start=(k == 0), stop=(k == 7))
        for k in range(8):
            tensor.matmul(PB[4][0:1, 128:256], onesbf, a["xnb"][:, k, :],
                          start=(k == 0), stop=(k == 7))
        tensor.drain().then_inc(s_["st"], 1)
        tensor.wait_ge(s_["ss"], mk["sLN"])
        tensor.matmul(PB[5][:, 0:128], c["c_ones"][0:1, :], a["Gbuf"][:].rearrange("p h q -> p (h q)")[0:1, 512:640],
                      start=True, stop=True)
        tensor.matmul(PB[5][:, 128:256], c["c_ones"][0:1, :],
                      a["Gbuf"][:].rearrange("p h q -> p (h q)")[0:1, 640:768], start=True, stop=True)
        tensor.drain().then_inc(s_["st"], 1)
        tensor.wait_ge(s_["sv"], mk["vXN"])
        evac = {0: ("ss", "sIP8"), 1: ("sv", "vIP0"), 2: ("sv", "vIP1"),
                3: ("sv", "vIP2"), 4: ("sv", "vIP3"), 5: ("ss", "sIP4"),
                6: ("ss", "sIP5"), 7: ("ss", "sIP6"), 8: ("ss", "sIP7")}

        def t_ip(p):
            sl = pos2sl[p]
            ncol = 384 if sl == 8 else 512
            if p >= 2:
                ch_, nm = evac[p - 2]
                tensor.wait_ge(s_[ch_], mk[nm])
            for k in range(8):
                tensor.matmul(PB[p % 2][:, 0:ncol], a["xnb"][:, k, :],
                              a["winT"][:, k, sl * 512:sl * 512 + ncol],
                              start=(k == 0), stop=(k == 7))
            tensor.drain().then_inc(s_["st"], 1)

        for p in range(5):
            t_ip(p)
        tensor.wait_ge(s_["sv"], mk["vSM2"])
        tensor.matmul(PB[6][:, 0:32], c["c_L"][:], a["adt"][:], start=True,
                      stop=True)
        tensor.matmul(PB[6][:, 32:64], c["c_shift"][:], a["dt1ml"][:], start=True,
                      stop=True)
        tensor.matmul(PB[6][:, 64:96], c["c_ones"][:], a["adt"][:], start=True,
                      stop=True)
        tensor.matmul(PB[6][:, 96:128], c["c_e0"][:], a["dt1ml"][:], start=True,
                      stop=True)
        tensor.drain().then_inc(s_["st"], 1)
        for p in range(5, 9):
            t_ip(p)
        Gflat = a["Gbuf"][:].rearrange("p h q -> p (h q)")
        tensor.wait_ge(s_["sv"], mk["vG"])
        for j in range(2):
            tensor.matmul(PB[2 + j][:], c["c_ones"][0:1, :],
                          a["carry"][0:1, j * 512:(j + 1) * 512], start=True,
                          stop=False)
            tensor.matmul(PB[2 + j][:], c["c_L"][:], Gflat[:, j * 512:(j + 1) * 512],
                          start=False, stop=True)
        tensor.matmul(PB[7][0:1, :], c["c_cols"][:, 0:1], Gflat[:, 0:512],
                      start=True, stop=True)
        tensor.drain().then_inc(s_["st"], 1)
        tensor.wait_ge(s_["sv"], mk["vCAR0"])
        tensor.matmul(PB[7][0:1, :], c["c_cols"][:, 0:1], Gflat[:, 512:1024],
                      start=True, stop=True)
        tensor.drain().then_inc(s_["st"], 1)

        def t_tr(g):
            hg = g * 8
            if g == 0:
                tensor.wait_ge(s_["sv"], mk["vCB"])
            else:
                tensor.wait_ge(s_["ss"], mk[f"sE{g - 1}"])
            pa, pb_ = pair(g)
            pbh = PB[pa][:].bitcast(BF16)
            pch = PB[pb_][:].bitcast(BF16)
            for i in range(8):
                tensor.transpose(pbh[:, i * 128:(i + 1) * 128], bh_cur[:, hg + i, :],
                                 c["c_idbf"][:])
            for i in range(8):
                tensor.transpose(pch[:, i * 128:(i + 1) * 128],
                                 a["ch_all"][:, hg + i, :], c["c_idbf"][:])
            tensor.drain().then_inc(s_["st"], 1)

        def t_scs(g):
            tensor.wait_ge(s_["ss"], mk[f"sTRE{g}"])
            pa, pb_ = pair(g)
            for i in range(8):
                tensor.matmul(PB[pa + i // 4][:, (i % 4) * 128:(i % 4) * 128 + 128],
                              a["bh_sm"][:, g % 2, i, :], a["ch_sm"][:, g % 2, i, :],
                              start=True, stop=True)
            tensor.drain().then_inc(s_["st"], 1)

        def t_scb(g):
            tensor.wait_ge(s_["ss"], mk[f"sCSB{g}"])
            if g > 0:
                tensor.wait_ge(s_["sv"], mk[f"vPS{g - 1}"])
            pa, pb_ = opair(g)
            for i in range(8):
                tensor.transpose(PB[pa + i // 4][:, (i % 4) * 128:(i % 4) * 128 + 128],
                                 a["Ebuf"][:, i, :], c["c_idf32"][:])
            tensor.drain().then_inc(s_["st"], 1)

        def t_y(g):
            hg = g * 8
            tensor.wait_ge(s_["sv"], mk[f"vPS{g}"])
            if g >= 1:
                tensor.wait_ge(s_["ss"], mk[f"sYSM{g - 1}"])
            if ci > 0:
                for i in range(8):
                    tensor.matmul(PB[6][:, i * 64:(i + 1) * 64],
                                  a["bh_all"][:, prev, hg + i, :],
                                  a["xmaskE"][:, hg + i, :], start=True, stop=True)
                tensor.drain().then_inc(s_["st"], 1)
                tensor.wait_ge(s_["sv"], mk[f"vHB{g}"])
            for i in range(8):
                tensor.matmul(PB[6][:, i * 64:(i + 1) * 64], a["PSb"][:, i, :],
                              a["x_tm"][:, hg + i, :], start=True, stop=True)
            for i in range(8):
                tensor.matmul(PB[7][:, i * 64:(i + 1) * 64],
                              a["ch_sm"][:, g % 2, i, :],
                              a["Hst"][:, (hg + i) * HP:(hg + i + 1) * HP],
                              start=True, stop=True)
            tensor.drain().then_inc(s_["st"], 1)

        def t_ybq(g):
            hg = g * 8
            tensor.wait_ge(s_["sv"], mk[f"vYE1{g}"])
            for i in range(8):
                tensor.matmul(PB[7][:, i * 64:(i + 1) * 64], bh_cur[:, hg + i, :],
                              a["xqh"][:, hg + i, :], start=True, stop=True)
            tensor.drain().then_inc(s_["st"], 1)

        def t_yt(g):
            tensor.wait_ge(s_["sv"], mk[f"vYE1{g}"])
            pbf6 = PB[6][:].bitcast(BF16)
            for j in range(4):
                tensor.transpose(pbf6[:, j * 128:(j + 1) * 128],
                                 a["ybar_g"][:, j * 128:(j + 1) * 128], c["c_idbf"][:])
            tensor.drain().then_inc(s_["st"], 1)

        def t_op(g):
            tensor.wait_ge(s_["ss"], mk[f"sYSM{g}"])
            for j in range(4):
                kin = g * 4 + j
                for sx in range(2):
                    tensor.matmul(PB[sx][:, 0:512], a["ybar_sm"][:, j, :],
                                  a["woutT"][:, kin, sx * 512:(sx + 1) * 512],
                                  start=(kin == 0), stop=(kin == 15))

        def dmy(*_a, **_k):
            tensor.drain().then_inc(s_["st"], 1)
        if DBG_CUT == 1:
            t_tr = t_scs = t_scb = t_y = t_ybq = t_yt = dmy
            t_op = lambda g: None
        elif DBG_CUT == 2:
            t_y = t_ybq = t_yt = dmy
            t_op = lambda g: None
        elif DBG_CUT == 3:
            t_op = lambda g: None
        t_tr(0)
        t_scs(0)
        t_scb(0)
        t_tr(1)
        t_y(0)
        t_scs(1)
        t_ybq(0)
        t_scb(1)
        t_yt(0)
        t_tr(2)
        t_op(0)
        t_y(1)
        t_scs(2)
        t_ybq(1)
        t_scb(2)
        t_yt(1)
        t_tr(3)
        t_op(1)
        t_y(2)
        t_scs(3)
        t_ybq(2)
        t_scb(3)
        t_yt(2)
        t_y(3)
        t_ybq(3)
        t_op(2)
        t_yt(3)
        t_op(3)
        tensor.drain().then_inc(s_["st"], 1)
    b.cnt["st"] = mk["tOP"]

    @b.V
    def _(vector):
        s_ = b.sem
        vector.wait_ge(s_["sd"], mk["dU"])
        W(vector, "st", "p_tY3")
        W(vector, "ss", "p_sE3")
        vector.drain().then_inc(s_["sv"], 1)
        vector.wait_ge(s_["st"], mk["tLN"])
        vector.tensor_scalar(a["Gbuf"][:].rearrange("p h q -> p (h q)")[0:1, 512:640], PB[4][0:1, 0:128], 1.0 / DM,
                             None, OP.mult)
        vector.tensor_scalar(a["Gbuf"][:].rearrange("p h q -> p (h q)")[0:1, 640:768], PB[4][0:1, 128:256], 1.0 / DM,
                             None, OP.mult)
        vector.drain()
        vector.tensor_tensor(a["smt"][0:1, 0:128], a["Gbuf"][:].rearrange("p h q -> p (h q)")[0:1, 512:640],
                             a["Gbuf"][:].rearrange("p h q -> p (h q)")[0:1, 512:640], OP.mult)
        vector.drain()
        vector.tensor_tensor(a["Gbuf"][:].rearrange("p h q -> p (h q)")[0:1, 640:768], a["Gbuf"][:].rearrange("p h q -> p (h q)")[0:1, 640:768],
                             a["smt"][0:1, 0:128], OP.subtract)
        vector.drain().then_inc(s_["sv"], 1)
        vector.wait_ge(s_["ss"], mk["sBC"])
        srep = a["Gbuf"][:].rearrange("p h q -> p (h q)")[:, 768:1024]
        mrep = srep[:, 0:128].unsqueeze(1).broadcast_to([128, 8, 128])
        rrep = srep[:, 128:256].unsqueeze(1).broadcast_to([128, 8, 128])
        vector.tensor_tensor(a["ub32"][:], a["ub32"][:], mrep, OP.subtract)
        vector.drain()
        vector.tensor_tensor(a["xnb"][:], a["ub32"][:], rrep, OP.mult)
        vector.drain().then_inc(s_["sv"], 1)
        vector.wait_ge(s_["ss"], mk["sIP8"])
        vector.tensor_tensor(a["dt_"][:], a["sml"][:, 256:288], c["dtb"][:], OP.add)
        vector.drain()
        vector.scalar_tensor_tensor(a["smt"][:, 0:32], a["dt_"][:], -1.0, a["dt_"][:],
                                    OP.mult, OP.min)
        vector.scalar_tensor_tensor(a["smt"][:, 32:64], a["sml"][:, 288:320],
                                    -1.0, a["sml"][:, 288:320], OP.mult, OP.min)
        vector.drain().then_inc(s_["sv"], 1)
        vector.wait_ge(s_["st"], mk["tIP1"])
        vector.tensor_copy(a["zs"][:, 0:512], PB[1][:])
        vector.drain().then_inc(s_["sv"], 1)
        vector.wait_ge(s_["st"], mk["tIP2"])
        vector.tensor_copy(a["zs"][:, 512:1024], PB[0][:])
        vector.drain().then_inc(s_["sv"], 1)
        vector.wait_ge(s_["ss"], mk["sSM"])
        vector.tensor_scalar(a["smt"][:, 96:128], a["dt_"][:], 0.0, None, OP.max)
        vector.tensor_scalar(a["Gbuf"][:, 0, 0:32], a["sml"][:, 288:320], 0.0, None,
                             OP.max)
        vector.tensor_scalar(a["lam"][:], a["smt"][:, 64:96], 1.0, None, OP.add)
        vector.drain()
        vector.reciprocal(a["lam"][:], a["lam"][:])
        vector.drain()
        vector.tensor_tensor(a["dt_"][:], a["smt"][:, 96:128], a["smt"][:, 0:32],
                             OP.add)
        vector.tensor_tensor(a["adt"][:], a["Gbuf"][:, 0, 0:32], a["smt"][:, 32:64],
                             OP.add)
        vector.tensor_scalar(a["smt"][:, 96:128], a["lam"][:], -1.0, 1.0, OP.mult,
                             OP.add)
        vector.drain()
        vector.tensor_scalar(a["adt"][:], a["adt"][:], -1.0, -1e-4, OP.mult, OP.min)
        vector.tensor_tensor(a["dtlam"][:], a["dt_"][:], a["lam"][:], OP.mult)
        vector.tensor_tensor(a["dt1ml"][:], a["dt_"][:], a["smt"][:, 96:128],
                             OP.mult)
        vector.drain()
        vector.tensor_tensor(a["adt"][:], a["adt"][:], a["dt_"][:], OP.mult)
        vector.drain().then_inc(s_["sv"], 1)
        vector.wait_ge(s_["st"], mk["tCS"])
        vector.tensor_copy(a["cs_tm"][:], PB[6][:, 0:32])
        vector.tensor_tensor(a["wvec"][:], a["dtlam"][:], PB[6][:, 32:64], OP.add)
        vector.drain()
        vector.tensor_scalar(a["wvec"][:], a["wvec"][:], 1e-30, None, OP.max)
        vector.drain().then_inc(s_["sv"], 1)
        vector.wait_ge(s_["st"], mk["tIP3"])
        vector.tensor_copy(a["zs"][:, 1024:1536], PB[1][:])
        vector.drain().then_inc(s_["sv"], 1)
        vector.wait_ge(s_["st"], mk["tIP4"])
        vector.tensor_copy(a["zs"][:, 1536:2048], PB[0][:])
        vector.drain().then_inc(s_["sv"], 1)
        vector.wait_ge(s_["ss"], mk["sCS"])
        vector.tensor_tensor(a["qw"][:], a["qw"][:], a["cs_tm"][:], OP.subtract)
        vector.drain()
        vector.tensor_tensor(a["smt"][:, 0:32], a["qw"][:], PB[6][:, 64:96], OP.add)
        vector.drain().then_inc(s_["sv"], 1)
        dtr = a["dt_"][:].unsqueeze(2).broadcast_to([128, NH, NA])
        angr = a["sml"][:, 352:384].unsqueeze(1).broadcast_to([128, NH, NA])
        vector.tensor_tensor(a["Gbuf"][:], dtr, angr, OP.mult)
        vector.drain().then_inc(s_["sv"], 1)
        Gflat = a["Gbuf"][:].rearrange("p h q -> p (h q)")
        for j in range(2):
            gsl = Gflat[:, j * 512:(j + 1) * 512]
            vector.wait_ge(s_["st"], mk["tTH0" if j == 0 else "tTH1"])
            vector.tensor_tensor(a["carry"][0:1, j * 512:(j + 1) * 512],
                                 a["carry"][0:1, j * 512:(j + 1) * 512],
                                 PB[7][0:1, :], OP.add)
            vector.drain().then_inc(s_["sv"], 1)
            vector.tensor_scalar(gsl, PB[2 + j][:], 1.0 / (2 * PI), MAGIC, OP.mult,
                                 OP.add)
            vector.tensor_scalar(gsl, gsl, MAGIC, 2 * PI, OP.subtract, OP.mult)
            vector.tensor_tensor(gsl, PB[2 + j][:], gsl, OP.subtract)
            vector.drain().then_inc(s_["sv"], 1)
            vector.wait_ge(s_["ss"], mk[f"sS{j}"])
            vector.tensor_scalar(gsl, PB[2 + j][:], 1.0 / (2 * PI), 0.25, OP.mult,
                                 OP.add)
            vector.tensor_scalar(gsl, gsl, MAGIC, MAGIC, OP.add, OP.subtract)
            vector.tensor_scalar(gsl, gsl, 2 * PI, PI / 2, OP.mult, OP.subtract)
            vector.tensor_tensor(gsl, PB[2 + j][:], gsl, OP.subtract)
            vector.drain().then_inc(s_["sv"], 1)
        vector.wait_ge(s_["ss"], mk["sQH"])
        vector.wait_ge(s_["ss"], mk["sIP7"])
        qrep = a["qh_e"][:].unsqueeze(2).broadcast_to([128, NH, HP])
        vector.tensor_tensor(a["xqh"][:], a["x_tm"][:], qrep, OP.mult)
        if ci > 0:
            drep0 = a["d10bf"][:].unsqueeze(2).broadcast_to([128, NH, HP])
            vector.tensor_tensor(a["xmaskE"][:], a["xmaskE"][:], drep0, OP.mult)
        vector.drain().then_inc(s_["sv"], 1)
        bnbf = a["ybar_sm"][:, 0, :]
        for src0, nwkey, biaskey in [(0, "bnw", "bias_b"), (128, "cnw", "bias_c")]:
            vector.tensor_tensor(a["ub32"][:, 0, :], a["sml"][:, src0:src0 + 128],
                                 a["sml"][:, src0:src0 + 128], OP.mult)
            vector.drain()
            vector.tensor_reduce(a["acol"][:, 0:1], a["ub32"][:, 0, :], AX.X, OP.add)
            vector.drain().then_inc(s_["sv"], 1)
            vector.wait_ge(s_["ss"], mk["sRB" if src0 == 0 else "sRC"])
            if src0 == 0:
                W(vector, "st", "p_tOP")
                dst = a["bh_all"][:, cur]
            else:
                dst = a["ch_all"][:]
            vector.scalar_tensor_tensor(bnbf, a["sml"][:, src0:src0 + 128],
                                        a["acol"][:, 0:1], c[nwkey][:], OP.mult,
                                        OP.mult)
            vector.drain()
            nrep = bnbf.unsqueeze(1).broadcast_to([128, NH, NS])
            brep = a[biaskey][:].rearrange("p (h n) -> p h n", h=NH)
            vector.tensor_tensor(dst, nrep, brep, OP.add)
            vector.drain()
            v1 = dst[:, :, 0:NA]
            v2 = dst[:, :, NA:2 * NA]
            sinr = a["sin_t"][:]
            cosr = a["cos_t"][:]
            rt0 = a["dgbuf"][:].rearrange("p i t -> p (i t)").rearrange(
                "p (h q) -> p h q", h=NH)
            rt1 = a["Ebuf"][:].rearrange("p i t -> p (i t)").bitcast(BF16)[
                :, 0:1024].rearrange("p (h q) -> p h q", h=NH)
            vector.tensor_tensor(rt0, v2, sinr, OP.mult)
            vector.tensor_tensor(v2, v2, cosr, OP.mult)
            vector.tensor_tensor(rt1, v1, sinr, OP.mult)
            vector.tensor_tensor(v2, rt1, v2, OP.add)
            vector.tensor_tensor(v1, v1, cosr, OP.mult)
            vector.tensor_tensor(v1, v1, rt0, OP.subtract)
            vector.drain().then_inc(s_["sv"], 1)
        for g in range(4):
            hg = g * 8
            if DBG_CUT == 1:
                nmk = 3 if ci == 0 else 4
                for _ in range(nmk):
                    vector.drain().then_inc(s_["sv"], 1)
                continue
            irep = c["c_idbf"][:].unsqueeze(1).broadcast_to([128, 8, 128])
            drep8 = a["dtlam"][:, hg:hg + 8].unsqueeze(2).broadcast_to([128, 8, 128])
            vector.tensor_tensor(a["dgbuf"][:], irep, drep8, OP.mult)
            vector.drain()
            vector.wait_ge(s_["ss"], mk[f"sE{g}"])
            pp = a["PSb"][:].rearrange("p i t -> p (i t)")
            dg = a["dgbuf"][:].rearrange("p i t -> p (i t)")
            vector.tensor_tensor(pp[:, 0:512], pp[:, 0:512], dg[:, 0:512], OP.add)
            vector.tensor_tensor(pp[:, 512:1024], pp[:, 512:1024], dg[:, 512:1024],
                                 OP.add)
            vector.drain()
            pa, pb_ = pair(g)
            vector.tensor_tensor(pp[:, 0:512], PB[pa][:], pp[:, 0:512], OP.mult)
            vector.tensor_tensor(pp[:, 512:1024], PB[pb_][:], pp[:, 512:1024],
                                 OP.mult)
            vector.drain().then_inc(s_["sv"], 1)
            if DBG_CUT == 2:
                for _ in range(2 if ci == 0 else 3):
                    vector.drain().then_inc(s_["sv"], 1)
                continue
            if ci > 0:
                vector.wait_ge(s_["st"], mk[f"tBD{g}"])
                hsl0 = a["Hst"][:, hg * HP:(hg + 8) * HP]
                vector.tensor_tensor(hsl0, hsl0, PB[6][:], OP.add)
                vector.drain().then_inc(s_["sv"], 1)
            vector.wait_ge(s_["st"], mk[f"tY{g}"])
            if g >= 1:
                vector.wait_ge(s_["st"], mk[f"tYT{g - 1}"])
            e13 = a["Gbuf"][:].rearrange("p h q -> p (h q)")[:, 0:512].rearrange(
                "p (i q) -> p i q", i=8)
            yb3 = a["ybar_g"][:].rearrange("p (i q) -> p i q", i=8)
            pb6_3 = PB[6][:].rearrange("p (i q) -> p i q", i=8)
            pb7_3 = PB[7][:].rearrange("p (i q) -> p i q", i=8)
            drp = c["drepbf"][:, hg:hg + 8].unsqueeze(2).broadcast_to([128, 8, HP])
            xsl = a["x_tm"][:, hg:hg + 8, :]
            eint3 = a["eint"][:]
            arep = a["alpha"][:, hg:hg + 8].unsqueeze(2).broadcast_to([128, 8, HP])
            vector.tensor_tensor(eint3, pb7_3, arep, OP.mult)
            vector.tensor_tensor(yb3, xsl, drp, OP.mult)
            vector.drain()
            vector.tensor_tensor(e13, pb6_3, yb3, OP.add)
            vector.tensor_tensor(e13, e13, eint3, OP.add)
            zs3 = a["zs"][:, hg * HP:(hg + 8) * HP].rearrange("p (i q) -> p i q", i=8)
            vector.tensor_tensor(yb3, e13, zs3, OP.mult)
            vector.drain().then_inc(s_["sv"], 1)
            vector.wait_ge(s_["st"], mk[f"tYB{g}"])
            dcr = a["decay"][:, hg:hg + 8].unsqueeze(2).broadcast_to([128, 8, HP])
            hsl = a["Hst"][:, hg * HP:(hg + 8) * HP].rearrange("p (i q) -> p i q", i=8)
            vector.tensor_tensor(eint3, hsl, dcr, OP.mult)
            vector.drain()
            vector.tensor_tensor(hsl, eint3, pb7_3, OP.add)
            vector.drain().then_inc(s_["sv"], 1)
        vector.wait_ge(s_["st"], mk["tOP"])
        s_ev = a["ub32"][:].rearrange("p k t -> p (k t)")
        vector.tensor_copy(s_ev[:, 0:512], PB[0][:])
        vector.tensor_copy(s_ev[:, 512:1024], PB[1][:])
        vector.tensor_tensor(
            a["xmaskE"][:].rearrange("p h q -> p (h q)"),
            a["x_tm"][:].rearrange("p h q -> p (h q)"),
            e127bf.broadcast_to([128, NH * HP]), OP.mult)
        vector.drain().then_inc(s_["sv"], 1)
    b.cnt["sv"] = mk["vSEV"]

    @b.S
    def _(scalar):
        s_ = b.sem
        scalar.wait_ge(s_["sv"], mk["vLN1"])
        scalar.copy(a["PSb"][:].rearrange("p k t -> p (k t)"),
                    a["ub32"][:].rearrange("p k t -> p (k t)"))
        scalar.activation(a["xnb"][:].rearrange("p k t -> p (k t)"),
                          a["ub32"][:].rearrange("p k t -> p (k t)"), AF.Square)
        scalar.drain().then_inc(s_["ss"], 1)
        scalar.wait_ge(s_["sv"], mk["vLN2"])
        scalar.activation(a["smt"][0:1, 0:128], a["Gbuf"][:].rearrange("p h q -> p (h q)")[0:1, 640:768], AF.Ln,
                          bias=1e-5)
        scalar.activation(a["Gbuf"][:].rearrange("p h q -> p (h q)")[0:1, 640:768], a["smt"][0:1, 0:128], AF.Exp,
                          scale=-0.5)
        scalar.drain().then_inc(s_["ss"], 1)
        scalar.wait_ge(s_["st"], mk["tBC"])
        scalar.copy(a["Gbuf"][:].rearrange("p h q -> p (h q)")[:, 768:1024],
                    PB[5][:, 0:256])
        scalar.drain().then_inc(s_["ss"], 1)
        scalar.wait_ge(s_["st"], mk["tIP0"])
        scalar.copy(a["sml"][:], PB[0][:, 0:384])
        scalar.drain().then_inc(s_["ss"], 1)
        scalar.wait_ge(s_["sv"], mk["vSM1"])
        scalar.activation(a["smt"][:, 64:128], a["smt"][:, 0:64], AF.Exp)
        scalar.activation(a["smt"][:, 0:64], a["smt"][:, 64:128], AF.Ln, bias=1.0)
        scalar.activation(a["smt"][:, 64:96], a["sml"][:, 320:352], AF.Exp,
                          scale=-1.0)
        scalar.drain().then_inc(s_["ss"], 1)
        scalar.wait_ge(s_["sv"], mk["vCS1"])
        scalar.activation(a["qw"][:], a["wvec"][:], AF.Ln)
        scalar.activation(a["alpha"][:], a["cs_tm"][:], AF.Exp)
        scalar.activation(a["decay"][:], PB[6][:, 64:96], AF.Exp)
        scalar.drain().then_inc(s_["ss"], 1)
        scalar.wait_ge(s_["sv"], mk["vQW"])
        scalar.activation(a["qh_e"][:], a["smt"][:, 0:32], AF.Exp)
        scalar.copy(a["d10bf"][:], PB[6][:, 96:128])
        scalar.drain().then_inc(s_["ss"], 1)
        xf = a["x_tm"][:].rearrange("p h q -> p (h q)")
        scalar.wait_ge(s_["st"], mk["tIP5"])
        scalar.copy(xf[:, 0:512], PB[1][:])
        scalar.drain().then_inc(s_["ss"], 1)
        scalar.wait_ge(s_["st"], mk["tIP6"])
        scalar.copy(xf[:, 512:1024], PB[0][:])
        scalar.drain().then_inc(s_["ss"], 1)
        Gf = a["Gbuf"][:].rearrange("p h q -> p (h q)")
        sf = a["sin_t"][:].rearrange("p h q -> p (h q)")
        cf = a["cos_t"][:].rearrange("p h q -> p (h q)")
        scalar.wait_ge(s_["sv"], mk["vRS0"])
        scalar.activation(sf[:, 0:512], Gf[:, 0:512], AF.Sin)
        scalar.drain().then_inc(s_["ss"], 1)
        scalar.wait_ge(s_["sv"], mk["vRC0"])
        scalar.activation(cf[:, 0:512], Gf[:, 0:512], AF.Sin)
        scalar.drain().then_inc(s_["ss"], 1)
        scalar.wait_ge(s_["st"], mk["tIP7"])
        scalar.copy(xf[:, 1024:1536], PB[1][:])
        scalar.drain().then_inc(s_["ss"], 1)
        scalar.wait_ge(s_["st"], mk["tIP8"])
        scalar.copy(xf[:, 1536:2048], PB[0][:])
        scalar.drain().then_inc(s_["ss"], 1)
        scalar.wait_ge(s_["sv"], mk["vRS1"])
        scalar.activation(sf[:, 512:1024], Gf[:, 512:1024], AF.Sin)
        scalar.drain().then_inc(s_["ss"], 1)
        scalar.wait_ge(s_["sv"], mk["vRC1"])
        scalar.activation(cf[:, 512:1024], Gf[:, 512:1024], AF.Sin)
        scalar.drain().then_inc(s_["ss"], 1)
        scalar.wait_ge(s_["sv"], mk["vIP3"])
        for jz in range(4):
            scalar.activation(a["zs"][:, jz * 512:(jz + 1) * 512],
                              a["zs"][:, jz * 512:(jz + 1) * 512], AF.Silu)
        scalar.drain().then_inc(s_["ss"], 1)
        for nm, wm in [("sRB", "vRMB"), ("sRC", "vRMC")]:
            scalar.wait_ge(s_["sv"], mk[wm])
            scalar.activation(a["acol"][:, 1:2], a["acol"][:, 0:1], AF.Ln,
                              scale=1.0 / NS, bias=1e-5)
            scalar.activation(a["acol"][:, 0:1], a["acol"][:, 1:2], AF.Exp,
                              scale=-0.5)
            scalar.drain().then_inc(s_["ss"], 1)

        def s_csb(g):
            for i in range(8):
                scalar.activation(a["Ebuf"][:, i, :], c["c_maskT"][:], AF.Identity,
                                  scale=1.0,
                                  bias=a["cs_tm"][:, g * 8 + i:g * 8 + i + 1])
            scalar.drain().then_inc(s_["ss"], 1)

        def s_tre(g):
            scalar.wait_ge(s_["st"], mk[f"tTR{g}"])
            if g >= 2:
                scalar.wait_ge(s_["st"], mk[f"tSCs{g - 2}"])
                scalar.wait_ge(s_["st"], mk[f"tY{g - 2}"])
            pa, pb_ = pair(g)
            scalar.copy(a["bh_sm"][:, g % 2].rearrange("p i t -> p (i t)"),
                        PB[pa][:].bitcast(BF16))
            scalar.copy(a["ch_sm"][:, g % 2].rearrange("p i t -> p (i t)"),
                        PB[pb_][:].bitcast(BF16))
            scalar.drain().then_inc(s_["ss"], 1)

        def s_dmy(n):
            for _ in range(n):
                scalar.drain().then_inc(s_["ss"], 1)
        if DBG_CUT == 1:
            s_csb = s_tre = lambda g: s_dmy(1)
        scalar.wait_ge(s_["sv"], mk["vCB"])
        s_csb(0)
        s_tre(0)
        for g in range(4):
            hg = g * 8
            if DBG_CUT == 1:
                s_dmy(3 if g == 0 else (4 if g < 3 else 2))
                continue
            scalar.wait_ge(s_["st"], mk[f"tSCb{g}"])
            pa, pb_ = opair(g)
            for i in range(8):
                scalar.activation(a["PSb"][:, i, :],
                                  PB[pa + i // 4][:, (i % 4) * 128:(i % 4) * 128 + 128],
                                  AF.Exp, bias=a["qw"][:, hg + i:hg + i + 1])
            scalar.drain().then_inc(s_["ss"], 1)
            if g < 3:
                s_csb(g + 1)
            if g >= 1:
                if DBG_CUT == 2:
                    s_dmy(1)
                else:
                    scalar.wait_ge(s_["st"], mk[f"tYT{g - 1}"])
                    scalar.copy(a["ybar_sm"][:].rearrange("p j t -> p (j t)"),
                                PB[6][:].bitcast(BF16)[:, 0:512])
                    scalar.drain().then_inc(s_["ss"], 1)
            if g < 3:
                s_tre(g + 1)
        if DBG_CUT in (1, 2):
            s_dmy(1)
        else:
            scalar.wait_ge(s_["st"], mk["tYT3"])
            scalar.copy(a["ybar_sm"][:].rearrange("p j t -> p (j t)"),
                        PB[6][:].bitcast(BF16)[:, 0:512])
            scalar.drain().then_inc(s_["ss"], 1)
    b.cnt["ss"] = mk["sYSM3"]

    b.prev_a_marks = {"p_tY3": mk["tY3"], "p_sE3": mk["sE3"], "p_tOP": mk["tOP"]}

    b.tap("xnb", a["xnb"][:].rearrange("p k t -> p (k t)"), bi, ci)
    b.tap("xtm", a["x_tm"][:].rearrange("p h q -> p (h q)"), bi, ci)
    b.tap("zs", a["zs"][:], bi, ci)
    b.tap("sml", a["sml"][:], bi, ci)
    for nm in ("dt_", "adt", "lam", "wvec", "qw", "alpha", "qh_e", "decay",
               "cs_tm", "dtlam", "dt1ml"):
        b.tap(nm, a[nm][:], bi, ci)
    b.tap("sin", a["sin_t"][:].rearrange("p h q -> p (h q)"), bi, ci)
    b.tap("cos", a["cos_t"][:].rearrange("p h q -> p (h q)"), bi, ci)
    b.tap("bh", a["bh_all"][:, cur].rearrange("p h q -> p (h q)"), bi, ci)
    b.tap("ch", a["ch_all"][:].rearrange("p h q -> p (h q)"), bi, ci)
    b.tap("hst", a["Hst"][:], bi, ci)
    b.tap("sev", a["ub32"][:].rearrange("p k t -> p (k t)"), bi, ci)
    b.tap("psb", a["PSb"][:].rearrange("p i t -> p (i t)"), bi, ci)
    b.tap("ybar3", a["ybar_g"][:], bi, ci)
    b.tap("eint", a["eint"][:].rearrange("p i q -> p (i q)"), bi, ci)
    b.tap("carry", a["carry"][:], bi, ci)


def _chunk_b1(b, d, c, p, PB, bi, ci):
    mk = {}

    def plan(ch, names, step=1):
        v = b.cnt[ch]
        for n in names:
            v += step
            mk[n] = v

    plan("sv", ["vM8", "vSC", "vSR", "vXQ"] + [f"vGU{sl}" for sl in range(8)]
         + ["vGM"])
    plan("st", ["tM8", "tSR", "tSB"] + [x for sl in range(8)
                                        for x in (f"tG{sl}", f"tU{sl}")])
    plan("ss", [f"sG{sl}" for sl in range(8)])
    plan("sd", ["dXNF", "dGU"], step=16)

    prev_dGU = mk["dXNF"] - 16
    buf = (bi * NCH + ci) % 2
    prev_vXQ = getattr(b, "prev_b1_vXQ", 0)
    xnf = p["xnf"][:, buf]

    @b.D
    def _(sync):
        b.wait(sync, "sd", prev_dGU)
        sync.wait_ge(b.sem["sv"], prev_vXQ)
        sync.dma_start(xnf.rearrange("p k t -> p (k t)"),
                       d["sc_xn"][bi, ci]).then_inc(b.sem["sd"], 16)
        sync.wait_ge(b.sem["sv"], mk["vGU7"])
        sync.dma_start(d["sc_gu"][bi, ci], p["gu_b"][:]).then_inc(b.sem["sd"], 16)
    b.cnt["sd"] = mk["dGU"]
    b.prev_b1_vXQ = mk["vXQ"]

    @b.T
    def _(tensor):
        s_ = b.sem
        tensor.wait_ge(s_["sv"], mk["vM8"])
        tensor.transpose(PB[4][:, 0:128], p["m8"][:], c["c_idf32"][:])
        tensor.drain().then_inc(s_["st"], 1)
        tensor.wait_ge(s_["sv"], mk["vSC"])
        tensor.transpose(PB[4][0:1, 128:256], p["scol"][:, 1:2], c["c_idf32"][:])
        tensor.drain().then_inc(s_["st"], 1)
        tensor.wait_ge(s_["sv"], mk["vSR"])
        tensor.matmul(PB[5][:, 0:128], c["c_ones"][0:1, :], p["srow"][0:1, :],
                      start=True, stop=True)
        tensor.drain().then_inc(s_["st"], 1)
        tensor.wait_ge(s_["sv"], mk["vXQ"])
        for sl in range(8):
            if sl >= 2:
                tensor.wait_ge(s_["ss"], mk[f"sG{sl - 2}"])
            for k in range(8):
                tensor.matmul(PB[sl % 2][:], p["xq_fm"][:, k, :],
                              p["wgq"][:, k, sl * 512:(sl + 1) * 512],
                              start=(k == 0), stop=(k == 7))
            tensor.drain().then_inc(s_["st"], 1)
            if sl >= 2:
                tensor.wait_ge(s_["sv"], mk[f"vGU{sl - 2}"])
            for k in range(8):
                tensor.matmul(PB[2 + sl % 2][:], p["xq_fm"][:, k, :],
                              p["wuq"][:, k, sl * 512:(sl + 1) * 512],
                              start=(k == 0), stop=(k == 7))
            tensor.drain().then_inc(s_["st"], 1)
    b.cnt["st"] = mk["tU7"]
    prev_tU7 = getattr(b, "prev_b1_tU7", 0)
    b.prev_b1_tU7 = mk["tU7"]

    @b.V
    def _(vector):
        s_ = b.sem
        vector.wait_ge(s_["sd"], mk["dXNF"])
        vector.scalar_tensor_tensor(p["m8"][:], xnf[:, 0, :], -1.0,
                                    xnf[:, 0, :], OP.mult, OP.max)
        for k in range(1, 8):
            vector.scalar_tensor_tensor(p["mt"][:], xnf[:, k, :], -1.0,
                                        xnf[:, k, :], OP.mult, OP.max)
            vector.tensor_tensor(p["m8"][:], p["m8"][:], p["mt"][:], OP.max)
        vector.drain().then_inc(s_["sv"], 1)
        vector.wait_ge(s_["st"], mk["tM8"])
        vector.tensor_reduce(p["scol"][:, 0:1], PB[4][:, 0:128], AX.X, OP.max)
        vector.drain()
        vector.tensor_scalar(p["scol"][:, 0:1], p["scol"][:, 0:1], 1e-5, None,
                             OP.max)
        vector.drain()
        vector.reciprocal(p["scol"][:, 1:2], p["scol"][:, 0:1])
        vector.tensor_scalar(p["scol"][:, 2:3], p["scol"][:, 0:1],
                             c["wsc"][:, 3:4], 1.0 / 127.0, OP.mult, OP.mult)
        vector.tensor_scalar(p["scol"][:, 3:4], p["scol"][:, 0:1],
                             c["wsc"][:, 4:5], 1.0 / 127.0, OP.mult, OP.mult)
        vector.drain()
        vector.tensor_scalar(p["scol"][:, 1:2], p["scol"][:, 1:2], 127.0, None,
                             OP.mult)
        vector.drain().then_inc(s_["sv"], 1)
        vector.wait_ge(s_["st"], mk["tSR"])
        vector.tensor_copy(p["srow"][0:1, :], PB[4][0:1, 128:256])
        vector.drain().then_inc(s_["sv"], 1)
        vector.wait_ge(s_["st"], mk["tSB"])
        vector.wait_ge(s_["st"], prev_tU7)
        for k in range(8):
            vector.tensor_tensor(p["mt"][:], xnf[:, k, :], PB[5][:, 0:128],
                                 OP.mult)
            vector.tensor_scalar(p["xq_fm"][:, k, :], p["mt"][:], MAGIC, MAGIC,
                                 OP.add, OP.subtract)
        vector.drain().then_inc(s_["sv"], 1)
        for sl in range(8):
            vector.wait_ge(s_["st"], mk[f"tU{sl}"])
            vector.wait_ge(s_["ss"], mk[f"sG{sl}"])
            vector.scalar_tensor_tensor(
                p["gu_b"][:, sl * 512:(sl + 1) * 512], PB[2 + sl % 2][:],
                p["scol"][:, 3:4], p["gu_b"][:, sl * 512:(sl + 1) * 512],
                OP.mult, OP.mult)
            vector.drain().then_inc(s_["sv"], 1)
        vector.tensor_reduce(c["gumax"][:, bi * NCH + ci:bi * NCH + ci + 1],
                             p["gu_b"][:], AX.X, OP.max, apply_absolute_value=True)
        vector.drain().then_inc(s_["sv"], 1)
    b.cnt["sv"] = mk["vGM"]

    @b.S
    def _(scalar):
        s_ = b.sem
        scalar.wait_ge(s_["sd"], prev_dGU)
        for sl in range(8):
            scalar.wait_ge(s_["st"], mk[f"tG{sl}"])
            scalar.activation(p["gu_b"][:, sl * 512:(sl + 1) * 512],
                              PB[sl % 2][:], AF.Silu, scale=p["scol"][:, 2:3])
            scalar.drain().then_inc(s_["ss"], 1)
    b.cnt["ss"] = mk["sG7"]

    b.tap("xqfm", p["xq_fm"][:].rearrange("p k t -> p (k t)"), bi, ci)
    b.tap("gu", p["gu_b"][:], bi, ci)
    b.tap("b1scol", p["scol"][:], bi, ci)


def _chunk_b2(b, d, c, q, PB, bi, ci):
    mk = {}

    def plan(ch, names, step=1):
        v = b.cnt[ch]
        for n in names:
            v += step
            mk[n] = v

    plan("sv", ["vGQ"] + [f"vTQ{g4}" for g4 in range(4)] + ["vOB"])
    plan("st", [f"tTQ{g4}" for g4 in range(4)] + ["tD0", "tD1"])
    plan("ss", ["sD0", "sD1"])
    mk["dIN"] = b.cnt["sd"] + 48
    mk["dOUT"] = b.cnt["sd"] + 64

    buf = (bi * NCH + ci) % 2
    prev_vGQ = getattr(b, "prev_b2_vGQ", 0)
    prev_vOB = getattr(b, "prev_b2_vOB", 0)
    gu_tm = q["gu_tm"][:, buf]
    u_b = q["u_b"][:, buf]
    s_b = q["s_b"][:, buf]

    @b.D
    def _(sync):
        b.wait(sync, "sd", mk["dIN"] - 48)
        sync.wait_ge(b.sem["sv"], prev_vGQ)
        sync.dma_start(gu_tm, d["sc_gu"][bi, ci]).then_inc(b.sem["sd"], 16)
        sync.dma_start(u_b, d["u_tm"][bi, ci * 128:(ci + 1) * 128, :]
                       ).then_inc(b.sem["sd"], 16)
        sync.dma_start(s_b, d["sc_s"][bi, ci]).then_inc(b.sem["sd"], 16)
        sync.wait_ge(b.sem["sv"], mk["vOB"])
        sync.dma_start(d["out"][bi, ci * 128:(ci + 1) * 128, :], q["o_b"][:]
                       ).then_inc(b.sem["sd"], 16)
    b.cnt["sd"] = mk["dOUT"]
    b.prev_b2_vGQ = mk["vGQ"]
    b.prev_b2_vOB = mk["vOB"]

    @b.T
    def _(tensor):
        s_ = b.sem
        tensor.wait_ge(s_["sv"], mk["vGQ"])
        for g4 in range(4):
            if g4 >= 2:
                tensor.wait_ge(s_["sv"], mk[f"vTQ{g4 - 2}"])
            pbf = PB[4 + g4 % 2][:].bitcast(BF16)
            for i in range(8):
                tensor.transpose(
                    pbf[:, i * 128:(i + 1) * 128],
                    q["guq_tm"][:, (g4 * 8 + i) * 128:(g4 * 8 + i + 1) * 128],
                    c["c_idbf"][:])
            tensor.drain().then_inc(s_["st"], 1)
        tensor.wait_ge(s_["sv"], mk["vTQ3"])
        for sl in range(2):
            for k in range(32):
                tensor.matmul(PB[sl][:], q["guq_fm"][:, k, :],
                              q["wdq"][:, k, sl * 512:(sl + 1) * 512],
                              start=(k == 0), stop=(k == 31))
            tensor.drain().then_inc(s_["st"], 1)
    b.cnt["st"] = mk["tD1"]

    @b.V
    def _(vector):
        s_ = b.sem
        vector.wait_ge(s_["sd"], mk["dIN"])
        vector.tensor_scalar(q["scol2"][:, 0:1],
                             c["gumax"][:, bi * NCH + ci:bi * NCH + ci + 1],
                             1e-5, None, OP.max)
        vector.drain()
        vector.reciprocal(q["scol2"][:, 1:2], q["scol2"][:, 0:1])
        vector.tensor_scalar(q["scol2"][:, 2:3], q["scol2"][:, 0:1],
                             c["wsc"][:, 5:6], 1.0 / 127.0, OP.mult, OP.mult)
        vector.drain()
        vector.tensor_scalar(q["scol2"][:, 1:2], q["scol2"][:, 1:2], 127.0, None,
                             OP.mult)
        vector.drain()
        vector.tensor_tensor(q["qtmp"][:], gu_tm,
                             q["scol2"][:, 1:2].broadcast_to([128, FI]), OP.mult)
        vector.tensor_scalar(q["qtmp"][:], q["qtmp"][:], MAGIC, MAGIC, OP.add,
                             OP.subtract)
        vector.tensor_copy(q["guq_tm"][:], q["qtmp"][:])
        vector.drain().then_inc(s_["sv"], 1)
        for g4 in range(4):
            vector.wait_ge(s_["st"], mk[f"tTQ{g4}"])
            vector.tensor_copy(
                q["guq_fm"][:].rearrange("p k t -> p (k t)")[
                    :, g4 * 1024:(g4 + 1) * 1024], PB[4 + g4 % 2][:].bitcast(BF16))
            vector.drain().then_inc(s_["sv"], 1)
        vector.wait_ge(s_["ss"], mk["sD1"])
        b.wait(vector, "sd", mk["dOUT"] - 64)
        vector.tensor_tensor(q["o_b"][:], s_b, q["g_b"][:], OP.mult)
        vector.drain()
        vector.tensor_tensor(q["o_b"][:], q["o_b"][:], u_b, OP.add)
        vector.drain().then_inc(s_["sv"], 1)
    b.cnt["sv"] = mk["vOB"]

    @b.S
    def _(scalar):
        s_ = b.sem
        for sl in range(2):
            scalar.wait_ge(s_["st"], mk[f"tD{sl}"])
            scalar.activation(q["g_b"][:, sl * 512:(sl + 1) * 512], PB[sl][:],
                              AF.Silu, scale=q["scol2"][:, 2:3])
            scalar.drain().then_inc(s_["ss"], 1)
    b.cnt["ss"] = mk["sD1"]

    b.tap("guq", q["guq_tm"][:], bi, ci)
    b.tap("b2scol", q["scol2"][:], bi, ci)
    b.tap("gutm", gu_tm, bi, ci)
    b.tap("gb", q["g_b"][:], bi, ci)
    b.tap("ob", q["o_b"][:], bi, ci)


def _host_inputs(inputs, core):
    f = np.float32
    u = np.asarray(inputs["L0_attn_u"], f)[core * NB:(core + 1) * NB]
    ins = {
        "u_tm": np.ascontiguousarray(u),
        "u_fm": np.ascontiguousarray(u.transpose(0, 2, 1)),
        "w_inT": np.ascontiguousarray(np.asarray(inputs["L0_attn_W_in"], f).T),
        "w_outT": np.ascontiguousarray(np.asarray(inputs["L0_attn_W_out"], f).T),
        "w_gT": np.ascontiguousarray(np.asarray(inputs["L0_ffn_w_gate"], f).T),
        "w_uT": np.ascontiguousarray(np.asarray(inputs["L0_ffn_w_up"], f).T),
        "w_dT": np.ascontiguousarray(np.asarray(inputs["L0_ffn_w_down"], f).T),
        "dtb_rep": np.ascontiguousarray(
            np.broadcast_to(np.asarray(inputs["L0_attn_dt_bias"], f), (128, NH))),
        "bbias_row": np.asarray(inputs["L0_attn_B_bias"], f).reshape(1, NH * NS),
        "cbias_row": np.asarray(inputs["L0_attn_C_bias"], f).reshape(1, NH * NS),
        "bnw_rep": np.ascontiguousarray(
            np.broadcast_to(np.asarray(inputs["L0_attn_B_norm_w"], f), (128, NS))),
        "cnw_rep": np.ascontiguousarray(
            np.broadcast_to(np.asarray(inputs["L0_attn_C_norm_w"], f), (128, NS))),
        "d_rep": np.ascontiguousarray(
            np.broadcast_to(np.asarray(inputs["L0_attn_D_param"], f), (128, NH))),
    }
    return ins


def _host_consts():
    f = np.float32
    idx = np.arange(128)
    L = (idx[:, None] <= idx[None, :]).astype(f)
    shift = (idx[:, None] == idx[None, :] + 1).astype(f)
    mask = np.where(idx[None, :] > idx[:, None], 0.0, NEG).astype(f)
    ones = np.ones((128, 128), f)
    e0 = np.zeros((128, 128), f)
    e0[0, :] = 1.0
    ident = np.eye(128, dtype=f)
    cols = np.zeros((128, 4), f)
    cols[:, 0] = 1.0
    cols[127, 1] = 1.0
    cols[:, 2] = 1e-5
    colsbf = np.zeros((128, 2), ml_dtypes.bfloat16)
    colsbf[:, 0] = 1.0
    colsbf[127, 1] = 1.0
    return {
        "c_L": L, "c_shift": shift, "c_mask": mask,
        "c_maskT": np.ascontiguousarray(mask.T), "c_ones": ones, "c_e0": e0,
        "c_idf32": ident, "c_idbf": ident.astype(ml_dtypes.bfloat16),
        "c_cols": cols, "c_colsbf": colsbf,
    }


_CACHE = {}


def kernel(**inputs):
    if "nc" not in _CACHE:
        nc = bass.Bass("TRN2", target_bir_lowering=False, debug=False)
        build(nc)
        _CACHE["nc"] = nc
    nc = _CACHE["nc"]
    consts = _host_consts()
    in_maps = []
    for core in range(8):
        m = _host_inputs(inputs, core)
        m.update(consts)
        in_maps.append(m)
    res = run_bass_kernel_spmd(nc, in_maps, core_ids=list(range(8)))
    out = np.concatenate([np.asarray(res.results[cc]["out"]) for cc in range(8)],
                         axis=0)
    return out.astype(np.float32)


# version 13 (final)
# speedup vs baseline: 1.2163x; 1.0182x over previous
import numpy as np
import ml_dtypes
from contextlib import ExitStack

import concourse.bass as bass
import concourse.mybir as mybir
from concourse.bass_utils import run_bass_kernel_spmd

F32 = mybir.dt.float32
BF16 = mybir.dt.bfloat16
AX = mybir.AxisListType
OP = mybir.AluOpType
AF = mybir.ActivationFunctionType

TAPS = None
DBG_CUT = None

MAGIC = float(3 * 2.0 ** 22)
PI = float(np.pi)
NEG = -1e30

T = 2048
NB = 2
NCH = T // 128
DM = 1024
DI = 2048
NH = 32
HP = 64
NS = 128
NA = 32
DIN = 4480
FI = 4096


class B:

    def tap(self, name, apv, bi=None, ci=None):
        if TAPS is None:
            return
        if bi is not None and (bi, ci) != (TAPS.get("bi", 0), TAPS.get("ci", 0)):
            return
        sh = list(apv.shape)
        dt = apv.dtype
        t = self.nc.dram_tensor("tap_" + name, sh, dt, kind="ExternalOutput").ap()

        @self.D
        def _(sync):
            for nm in ("sv", "ss", "st", "sg"):
                self.wait(sync, nm)
            self.inc(sync.dma_start(t, apv), "sd", 16)

    def __init__(self, nc, stack):
        self.nc = nc
        self.sem = {}
        self.cnt = {}
        for name in ("sd", "st", "sv", "ss", "sg"):
            self.sem[name] = stack.enter_context(nc.semaphore(name))
            self.cnt[name] = 0
        self.blk = None

    def inc(self, instr, name, k=1):
        instr.then_inc(self.sem[name], k)
        self.cnt[name] += k
        return self.cnt[name]

    def wait(self, eng, name, val=None):
        v = self.cnt[name] if val is None else val
        if v > 0:
            eng.wait_ge(self.sem[name], v)

    def T(self, f):
        self.blk.tensor(f)

    def V(self, f):
        self.blk.vector(f)

    def S(self, f):
        self.blk.scalar(f)

    def G(self, f):
        self.blk.gpsimd(f)

    def D(self, f):
        self.blk.sync(f)


def build(nc):
    stack = ExitStack()
    b = B(nc, stack)
    ap = lambda t: t.ap()

    d = {}
    d["u_fm"] = ap(nc.dram_tensor("u_fm", [NB, DM, T], F32, kind="ExternalInput"))
    d["u_tm"] = ap(nc.dram_tensor("u_tm", [NB, T, DM], F32, kind="ExternalInput"))
    d["w_inT"] = ap(nc.dram_tensor("w_inT", [DM, DIN], F32, kind="ExternalInput"))
    d["w_outT"] = ap(nc.dram_tensor("w_outT", [DI, DM], F32, kind="ExternalInput"))
    d["w_gT"] = ap(nc.dram_tensor("w_gT", [DM, FI], F32, kind="ExternalInput"))
    d["w_uT"] = ap(nc.dram_tensor("w_uT", [DM, FI], F32, kind="ExternalInput"))
    d["w_dT"] = ap(nc.dram_tensor("w_dT", [FI, DM], F32, kind="ExternalInput"))
    d["dtb_rep"] = ap(nc.dram_tensor("dtb_rep", [128, NH], F32, kind="ExternalInput"))
    d["bbias_row"] = ap(nc.dram_tensor("bbias_row", [1, NH * NS], F32, kind="ExternalInput"))
    d["cbias_row"] = ap(nc.dram_tensor("cbias_row", [1, NH * NS], F32, kind="ExternalInput"))
    d["bnw_rep"] = ap(nc.dram_tensor("bnw_rep", [128, NS], F32, kind="ExternalInput"))
    d["cnw_rep"] = ap(nc.dram_tensor("cnw_rep", [128, NS], F32, kind="ExternalInput"))
    d["d_rep"] = ap(nc.dram_tensor("d_rep", [128, NH], F32, kind="ExternalInput"))
    for cn in ("c_L", "c_shift", "c_mask", "c_maskT", "c_ones", "c_e0", "c_idf32"):
        d[cn] = ap(nc.dram_tensor(cn, [128, 128], F32, kind="ExternalInput"))
    d["c_idbf"] = ap(nc.dram_tensor("c_idbf", [128, 128], BF16, kind="ExternalInput"))
    d["c_cols"] = ap(nc.dram_tensor("c_cols", [128, 4], F32, kind="ExternalInput"))
    d["c_colsbf"] = ap(nc.dram_tensor("c_colsbf", [128, 2], BF16, kind="ExternalInput"))
    d["out"] = ap(nc.dram_tensor("out", [NB, T, DM], F32, kind="ExternalOutput"))
    d["sc_xn"] = ap(nc.dram_tensor("sc_xn", [NB, NCH, 128, DM], BF16, kind="Internal"))
    d["sc_s"] = ap(nc.dram_tensor("sc_s", [NB, NCH, 128, DM], F32, kind="Internal"))
    d["sc_gu"] = ap(nc.dram_tensor("sc_gu", [NB, NCH, 128, FI], BF16, kind="Internal"))
    d["sc_wqg"] = ap(nc.dram_tensor("sc_wqg", [8, 128, FI], BF16, kind="Internal"))
    d["sc_wqu"] = ap(nc.dram_tensor("sc_wqu", [8, 128, FI], BF16, kind="Internal"))
    d["sc_wqd"] = ap(nc.dram_tensor("sc_wqd", [32, 128, DM], BF16, kind="Internal"))

    with stack:
        c = {}
        for cn in ("c_L", "c_shift", "c_mask", "c_maskT", "c_ones", "c_e0",
                   "c_idf32"):
            c[cn] = stack.enter_context(nc.sbuf_tensor("sb" + cn, [128, 128], F32))
        c["c_idbf"] = stack.enter_context(nc.sbuf_tensor("sbc_idbf", [128, 128], BF16))
        c["c_cols"] = stack.enter_context(nc.sbuf_tensor("sbc_cols", [128, 4], F32))
        c["c_colsbf"] = stack.enter_context(nc.sbuf_tensor("sbc_colsbf", [128, 2], BF16))
        c["dtb"] = stack.enter_context(nc.sbuf_tensor("dtb", [128, NH], F32))
        c["bnw"] = stack.enter_context(nc.sbuf_tensor("bnw", [128, NS], F32))
        c["cnw"] = stack.enter_context(nc.sbuf_tensor("cnw", [128, NS], F32))
        c["drep"] = stack.enter_context(nc.sbuf_tensor("drep", [128, NH], F32))
        c["drepbf"] = stack.enter_context(nc.sbuf_tensor("drepbf", [128, NH], BF16))
        c["wsc"] = stack.enter_context(nc.sbuf_tensor("wsc", [128, 8], F32))
        c["gumax"] = stack.enter_context(nc.sbuf_tensor("gumax", [128, NB * NCH], F32))
        PB = [stack.enter_context(nc.psum_tensor(f"pb{i}", [128, 512], F32))
              for i in range(8)]
        nc.const_aps.register(F32, 1.0, c["c_cols"][:, 0:1])
        nc.const_aps.register(F32, 1e-5, c["c_cols"][:, 2:3])
        nc.const_aps.register(F32, 0.0, c["c_cols"][:, 3:4])

        with nc.Block("init") as ib:
            @ib.sync
            def _(sync):
                for k in b.sem:
                    sync.sem_clear(b.sem[k])

        with ExitStack() as asc:
            a = {}

            def asb(n, sh, dt=F32):
                a[n] = asc.enter_context(nc.sbuf_tensor(n, sh, dt))
                return a[n]

            asb("winT", [128, 8, DIN], BF16)
            asb("woutT", [128, 16, DM], BF16)
            asb("bias_b", [128, NH * NS], BF16)
            asb("bias_c", [128, NH * NS], BF16)
            asb("Hst", [128, NH * HP], BF16)
            asb("carry", [1, NH * NA], F32)
            asb("acol", [128, 4], F32)

            with nc.Block("main_a") as blk:
                b.blk = blk
                with ExitStack() as pro:
                    a["wtmp"] = pro.enter_context(nc.sbuf_tensor("wtmp", [128, DIN], F32))
                    a["wqtmp"] = pro.enter_context(nc.sbuf_tensor("wqtmp", [128, FI], BF16))
                    a["row2"] = pro.enter_context(nc.sbuf_tensor("row2", [1, 2], F32))
                    _prologue(b, d, c, a, PB)
                del a["wtmp"], a["wqtmp"], a["row2"]
                asb("ub32", [128, 8, 128], F32)
                asb("xnb", [128, 8, 128], BF16)
                asb("x_tm", [128, NH, HP], BF16)
                asb("xqh", [128, NH, HP], BF16)
                asb("xmaskE", [128, NH, HP], BF16)
                asb("zs", [128, DI], BF16)
                asb("sml", [128, 384], F32)
                asb("smt", [128, 128], F32)
                for n in ("dt_", "adt", "lam", "dtlam", "dt1ml", "wvec", "qw",
                          "cs_tm", "alpha"):
                    asb(n, [128, NH], F32)
                for n in ("qh_e", "decay", "d10bf"):
                    asb(n, [128, NH], BF16)
                asb("Gbuf", [128, NH, NA], F32)
                asb("sin_t", [128, NH, NA], BF16)
                asb("cos_t", [128, NH, NA], BF16)
                asb("bh_all", [128, 2, NH, NS], BF16)
                asb("ch_all", [128, NH, NS], BF16)
                asb("bh_sm", [128, 2, 8, 128], BF16)
                asb("ch_sm", [128, 2, 8, 128], BF16)
                asb("Ebuf", [128, 8, 128], F32)
                asb("dgbuf", [128, 8, 128], BF16)
                asb("eint", [128, 8, HP], BF16)
                asb("PSb", [128, 8, 128], BF16)
                asb("ybar_g", [128, 512], BF16)
                asb("ybar_sm", [128, 4, 128], BF16)
                for bi in range(NB):
                    _batch_reset(b, a)
                    for ci in range(NCH):
                        _chunk_a(b, d, c, a, PB, bi, ci)

        with ExitStack() as bsc:
            p = {}

            def bsb(n, sh, dt=F32):
                p[n] = bsc.enter_context(nc.sbuf_tensor(n, sh, dt))
                return p[n]

            bsb("wgq", [128, 8, FI], BF16)
            bsb("wuq", [128, 8, FI], BF16)
            bsb("xnf", [128, 2, 8, 128], BF16)
            bsb("xq_fm", [128, 8, 128], BF16)
            bsb("m8", [128, 128], F32)
            bsb("mt", [128, 128], F32)
            bsb("srow", [1, 128], F32)
            bsb("scol", [128, 4], F32)
            bsb("gu_b", [128, FI], BF16)
            with nc.Block("main_b1") as blk:
                b.blk = blk

                @b.D
                def _(sync):
                    b.wait(sync, "sv")
                    b.wait(sync, "ss")
                    b.wait(sync, "sd")
                    for k in range(8):
                        b.inc(sync.dma_start(p["wgq"][:, k, :], d["sc_wqg"][k]), "sd", 16)
                        b.inc(sync.dma_start(p["wuq"][:, k, :], d["sc_wqu"][k]), "sd", 16)
                for bi in range(NB):
                    for ci in range(NCH):
                        _chunk_b1(b, d, c, p, PB, bi, ci)

        with ExitStack() as bsc2:
            q = {}

            def qsb(n, sh, dt=F32):
                q[n] = bsc2.enter_context(nc.sbuf_tensor(n, sh, dt))
                return q[n]

            qsb("wdq", [128, 32, DM], BF16)
            qsb("gu_tm", [128, 2, FI], BF16)
            qsb("guq_tm", [128, FI], BF16)
            qsb("qtmp", [128, FI], F32)
            qsb("guq_fm", [128, 32, 128], BF16)
            qsb("u_b", [128, 2, DM], F32)
            qsb("s_b", [128, 2, DM], F32)
            qsb("g_b", [128, DM], F32)
            qsb("o_b", [128, DM], F32)
            qsb("scol2", [128, 4], F32)
            with nc.Block("main_b2") as blk:
                b.blk = blk

                @b.D
                def _(sync):
                    b.wait(sync, "sv")
                    b.wait(sync, "ss")
                    b.wait(sync, "sd")
                    for k in range(32):
                        b.inc(sync.dma_start(q["wdq"][:, k, :], d["sc_wqd"][k]), "sd", 16)
                for bi in range(NB):
                    for ci in range(NCH):
                        _chunk_b2(b, d, c, q, PB, bi, ci)

            with nc.Block("fin") as blk:
                @blk.sync
                def _(sync):
                    b.wait(sync, "sd")
    return nc


def _prologue(b, d, c, a, PB):
    @b.D
    def _(sync):
        for dst, key in [(c["c_L"], "c_L"), (c["c_shift"], "c_shift"),
                         (c["c_mask"], "c_mask"), (c["c_maskT"], "c_maskT"),
                         (c["c_ones"], "c_ones"),
                         (c["c_e0"], "c_e0"), (c["c_idf32"], "c_idf32"),
                         (c["c_idbf"], "c_idbf"), (c["c_cols"], "c_cols"),
                         (c["c_colsbf"], "c_colsbf"), (c["dtb"], "dtb_rep"),
                         (c["bnw"], "bnw_rep"), (c["cnw"], "cnw_rep"),
                         (c["drep"], "d_rep")]:
            b.inc(sync.dma_start(dst[:], d[key]), "sd", 16)
        b.inc(sync.dma_start(a["wtmp"][0:1, 0:NH * NS], d["bbias_row"]), "sd", 16)

    @b.V
    def _(vector):
        b.wait(vector, "sd")
        vector.tensor_copy(c["drepbf"][:], c["drep"][:])
        b.inc(vector.drain(), "sv")

    for key, dstname in [("cbias_row", "bias_b"), (None, "bias_c")]:
        for j in range(8):
            @b.T
            def _(tensor, j=j):
                b.wait(tensor, "sd")
                b.wait(tensor, "sv")
                tensor.matmul(PB[j % 2][:], c["c_ones"][0:1, :],
                              a["wtmp"][0:1, j * 512:(j + 1) * 512],
                              start=True, stop=True)
                b.inc(tensor.drain(), "st")

            @b.V
            def _(vector, dstname=dstname, j=j):
                b.wait(vector, "st")
                vector.tensor_copy(a[dstname][:, j * 512:(j + 1) * 512], PB[j % 2][:])
                b.inc(vector.drain(), "sv")

        if key is not None:
            @b.D
            def _(sync, key=key):
                b.wait(sync, "st")
                b.inc(sync.dma_start(a["wtmp"][0:1, 0:NH * NS], d[key]), "sd", 16)

    for k in range(8):
        @b.D
        def _(sync, k=k):
            b.wait(sync, "sv")
            b.inc(sync.dma_start(a["wtmp"][:], d["w_inT"][k * 128:(k + 1) * 128, :]),
                  "sd", 16)

        @b.V
        def _(vector, k=k):
            b.wait(vector, "sd")
            vector.tensor_copy(a["winT"][:, k, :], a["wtmp"][:])
            b.inc(vector.drain(), "sv")
    for k in range(16):
        @b.D
        def _(sync, k=k):
            b.wait(sync, "sv")
            b.inc(sync.dma_start(a["wtmp"][:, 0:DM],
                                 d["w_outT"][k * 128:(k + 1) * 128, :]), "sd", 16)

        @b.V
        def _(vector, k=k):
            b.wait(vector, "sd")
            vector.tensor_copy(a["woutT"][:, k, :], a["wtmp"][:, 0:DM])
            b.inc(vector.drain(), "sv")

    for wi, (dkey, skey, nkw, wid) in enumerate(
            [("w_gT", "sc_wqg", 8, FI), ("w_uT", "sc_wqu", 8, FI),
             ("w_dT", "sc_wqd", 32, DM)]):
        @b.V
        def _(vector):
            vector.memset(a["acol"][:, 0:1], 0.0)
            b.inc(vector.drain(), "sv")
        for k in range(nkw):
            @b.D
            def _(sync, k=k, dkey=dkey, wid=wid):
                b.wait(sync, "sv")
                b.inc(sync.dma_start(a["wtmp"][:, 0:wid],
                                     d[dkey][k * 128:(k + 1) * 128, :]), "sd", 16)

            @b.V
            def _(vector, wid=wid):
                b.wait(vector, "sd")
                vector.tensor_reduce(a["acol"][:, 1:2], a["wtmp"][:, 0:wid], AX.X,
                                     OP.add, apply_absolute_value=True)
                vector.drain()
                vector.tensor_tensor(a["acol"][:, 0:1], a["acol"][:, 0:1],
                                     a["acol"][:, 1:2], OP.add)
                b.inc(vector.drain(), "sv")

        @b.T
        def _(tensor):
            b.wait(tensor, "sv")
            tensor.matmul(PB[2][0:1, 0:1], c["c_cols"][:, 0:1], a["acol"][:, 0:1],
                          start=True, stop=True)
            b.inc(tensor.drain(), "st")

        @b.V
        def _(vector, nkw=nkw, wid=wid):
            b.wait(vector, "st")
            n_el = float(nkw * 128 * wid)
            vector.tensor_scalar(a["row2"][0:1, 0:1], PB[2][0:1, 0:1], 1.0 / n_el,
                                 1e-5, OP.mult, OP.max)
            vector.drain()
            vector.reciprocal(a["row2"][0:1, 1:2], a["row2"][0:1, 0:1])
            b.inc(vector.drain(), "sv")

        @b.T
        def _(tensor):
            b.wait(tensor, "sv")
            tensor.matmul(PB[2][:, 0:2], c["c_ones"][0:1, :], a["row2"][0:1, 0:2],
                          start=True, stop=True)
            b.inc(tensor.drain(), "st")

        @b.V
        def _(vector, wi=wi):
            b.wait(vector, "st")
            vector.tensor_copy(c["wsc"][:, 3 + wi:4 + wi], PB[2][:, 0:1])
            vector.tensor_copy(c["wsc"][:, wi:wi + 1], PB[2][:, 1:2])
            b.inc(vector.drain(), "sv")

        for k in range(nkw):
            @b.D
            def _(sync, k=k, dkey=dkey, wid=wid):
                b.wait(sync, "sv")
                b.inc(sync.dma_start(a["wtmp"][:, 0:wid],
                                     d[dkey][k * 128:(k + 1) * 128, :]), "sd", 16)

            @b.V
            def _(vector, wi=wi, wid=wid):
                b.wait(vector, "sd")
                vector.tensor_scalar(a["wtmp"][:, 0:wid], a["wtmp"][:, 0:wid],
                                     c["wsc"][:, wi:wi + 1], MAGIC, OP.mult, OP.add)
                vector.tensor_scalar(a["wtmp"][:, 0:wid], a["wtmp"][:, 0:wid],
                                     MAGIC, None, OP.subtract)
                vector.tensor_scalar(a["wqtmp"][:, 0:wid], a["wtmp"][:, 0:wid],
                                     -1.0, 1.0, OP.max, OP.min)
                b.inc(vector.drain(), "sv")

            @b.D
            def _(sync, skey=skey, k=k, wid=wid):
                b.wait(sync, "sv")
                b.inc(sync.dma_start(d[skey][k, :, :], a["wqtmp"][:, 0:wid]), "sd", 16)


def _batch_reset(b, a):
    @b.V
    def _(vector):
        b.wait(vector, "st")
        vector.memset(a["Hst"][:], 0.0)
        vector.memset(a["carry"][:], 0.0)
        b.inc(vector.drain(), "sv")


def _chunk_a(b, d, c, a, PB, bi, ci):
    cur = ci % 2
    prev = 1 - cur
    onesbf = c["c_colsbf"][:, 0:1]
    e127bf = c["c_colsbf"][:, 1:2]
    bh_cur = a["bh_all"][:, cur]
    pos2sl = [8, 0, 1, 2, 3, 4, 5, 6, 7]

    def pair(g):
        return (2, 3) if g % 2 == 0 else (4, 5)

    def opair(g):
        return (4, 5) if g % 2 == 0 else (2, 3)

    mk = {}

    def plan(ch, names, step=1):
        v = b.cnt[ch]
        for n in names:
            v += step
            mk[n] = v

    vnames = ["vLN1", "vLN2", "vXN", "vSM1", "vIP0", "vIP1", "vSM2", "vCS1",
              "vIP2", "vIP3", "vQW", "vG", "vRMB", "vRMC", "vCAR0", "vRS0",
              "vEXB", "vRC0", "vEXC", "vCAR1", "vRS1", "vRC1", "vXM", "vBB",
              "vCB"]
    for g in range(4):
        vnames += [f"vPS{g}"]
        if ci > 0:
            vnames += [f"vHB{g}"]
        vnames += [f"vYE1{g}", f"vYEV{g}"]
    vnames += ["vSEV"]
    plan("sv", vnames)

    tnames = (["tLN", "tBC"] + [f"tIP{p}" for p in range(5)] +
              ["tCS", "tIP5", "tIP6", "tIP7", "tIP8", "tTH0", "tTH1"])
    tnames += ["tTR0", "tSCs0", "tSCb0", "tTR1"]
    for g in range(4):
        if ci > 0:
            tnames += [f"tBD{g}"]
        tnames += [f"tY{g}"]
        if g < 3:
            tnames += [f"tSCs{g + 1}"]
        tnames += [f"tYB{g}"]
        if g < 3:
            tnames += [f"tSCb{g + 1}"]
        tnames += [f"tYT{g}"]
        if g < 2:
            tnames += [f"tTR{g + 2}"]
    tnames += ["tOP"]
    plan("st", tnames)

    snames = ["sLN0", "sLN", "sBC", "sIP8", "sSM", "sCS", "sQH", "sRB", "sRC",
              "sIP4", "sIP5", "sS0", "sC0", "sIP6", "sIP7", "sS1", "sC1", "sZS",
              "sCSB0", "sTRE0"]
    for g in range(4):
        snames += [f"sE{g}"]
        if g < 3:
            snames += [f"sCSB{g + 1}"]
        if g >= 1:
            snames += [f"sYSM{g - 1}"]
        if g < 3:
            snames += [f"sTRE{g + 1}"]
    snames += ["sYSM3"]
    plan("ss", snames)
    plan("sd", ["dU", "dXN", "dS"], step=16)
    prev_mk = getattr(b, "prev_a_marks", {})

    def W(eng, ch, name):
        if name in mk:
            eng.wait_ge(b.sem[ch], mk[name])
        elif name in prev_mk:
            eng.wait_ge(b.sem[ch], prev_mk[name])

    @b.D
    def _(sync):
        b.wait(sync, "sd", mk["dU"] - 16)
        src = d["u_fm"][bi].rearrange("(k p) t -> p k t", p=128)[
            :, :, ci * 128:(ci + 1) * 128]
        sync.dma_start(a["ub32"][:], src).then_inc(b.sem["sd"], 16)
        sync.wait_ge(b.sem["sv"], mk["vXN"])
        sync.dma_start(d["sc_xn"][bi, ci],
                       a["xnb"][:].rearrange("p k t -> p (k t)")
                       ).then_inc(b.sem["sd"], 16)
        sync.wait_ge(b.sem["sv"], mk["vSEV"])
        sync.dma_start(d["sc_s"][bi, ci],
                       a["ub32"][:].rearrange("p k t -> p (k t)")
                       ).then_inc(b.sem["sd"], 16)
    b.cnt["sd"] = mk["dS"]

    @b.T
    def _(tensor):
        s_ = b.sem
        tensor.wait_ge(s_["sd"], mk["dU"])
        tensor.wait_ge(s_["ss"], mk["sLN0"])
        for k in range(8):
            tensor.matmul(PB[4][0:1, 0:128], onesbf, a["PSb"][:, k, :],
                          start=(k == 0), stop=(k == 7))
        for k in range(8):
            tensor.matmul(PB[4][0:1, 128:256], onesbf, a["xnb"][:, k, :],
                          start=(k == 0), stop=(k == 7))
        tensor.drain().then_inc(s_["st"], 1)
        tensor.wait_ge(s_["ss"], mk["sLN"])
        tensor.matmul(PB[5][:, 0:128], c["c_ones"][0:1, :], a["Gbuf"][:].rearrange("p h q -> p (h q)")[0:1, 512:640],
                      start=True, stop=True)
        tensor.matmul(PB[5][:, 128:256], c["c_ones"][0:1, :],
                      a["Gbuf"][:].rearrange("p h q -> p (h q)")[0:1, 640:768], start=True, stop=True)
        tensor.drain().then_inc(s_["st"], 1)
        tensor.wait_ge(s_["sv"], mk["vXN"])
        evac = {0: ("ss", "sIP8"), 1: ("sv", "vIP0"), 2: ("sv", "vIP1"),
                3: ("sv", "vIP2"), 4: ("sv", "vIP3"), 5: ("ss", "sIP4"),
                6: ("ss", "sIP5"), 7: ("ss", "sIP6"), 8: ("ss", "sIP7")}

        def t_ip(p):
            sl = pos2sl[p]
            ncol = 384 if sl == 8 else 512
            if p >= 2:
                ch_, nm = evac[p - 2]
                tensor.wait_ge(s_[ch_], mk[nm])
            for k in range(8):
                tensor.matmul(PB[p % 2][:, 0:ncol], a["xnb"][:, k, :],
                              a["winT"][:, k, sl * 512:sl * 512 + ncol],
                              start=(k == 0), stop=(k == 7))
            tensor.drain().then_inc(s_["st"], 1)

        for p in range(5):
            t_ip(p)
        tensor.wait_ge(s_["sv"], mk["vSM2"])
        tensor.matmul(PB[6][:, 0:32], c["c_L"][:], a["adt"][:], start=True,
                      stop=True)
        tensor.matmul(PB[6][:, 32:64], c["c_shift"][:], a["dt1ml"][:], start=True,
                      stop=True)
        tensor.matmul(PB[6][:, 64:96], c["c_ones"][:], a["adt"][:], start=True,
                      stop=True)
        tensor.matmul(PB[6][:, 96:128], c["c_e0"][:], a["dt1ml"][:], start=True,
                      stop=True)
        tensor.drain().then_inc(s_["st"], 1)
        for p in range(5, 9):
            t_ip(p)
        Gflat = a["Gbuf"][:].rearrange("p h q -> p (h q)")
        tensor.wait_ge(s_["sv"], mk["vG"])
        for j in range(2):
            tensor.matmul(PB[2 + j][:], c["c_ones"][0:1, :],
                          a["carry"][0:1, j * 512:(j + 1) * 512], start=True,
                          stop=False)
            tensor.matmul(PB[2 + j][:], c["c_L"][:], Gflat[:, j * 512:(j + 1) * 512],
                          start=False, stop=True)
        tensor.matmul(PB[7][0:1, :], c["c_cols"][:, 0:1], Gflat[:, 0:512],
                      start=True, stop=True)
        tensor.drain().then_inc(s_["st"], 1)
        tensor.wait_ge(s_["sv"], mk["vCAR0"])
        tensor.matmul(PB[7][0:1, :], c["c_cols"][:, 0:1], Gflat[:, 512:1024],
                      start=True, stop=True)
        tensor.drain().then_inc(s_["st"], 1)

        def t_tr(g):
            hg = g * 8
            if g == 0:
                tensor.wait_ge(s_["sv"], mk["vCB"])
            else:
                tensor.wait_ge(s_["ss"], mk[f"sE{g - 1}"])
            pa, pb_ = pair(g)
            pbh = PB[pa][:].bitcast(BF16)
            pch = PB[pb_][:].bitcast(BF16)
            for i in range(8):
                tensor.transpose(pbh[:, i * 128:(i + 1) * 128], bh_cur[:, hg + i, :],
                                 c["c_idbf"][:])
            for i in range(8):
                tensor.transpose(pch[:, i * 128:(i + 1) * 128],
                                 a["ch_all"][:, hg + i, :], c["c_idbf"][:])
            tensor.drain().then_inc(s_["st"], 1)

        def t_scs(g):
            tensor.wait_ge(s_["ss"], mk[f"sTRE{g}"])
            pa, pb_ = pair(g)
            for i in range(8):
                tensor.matmul(PB[pa + i // 4][:, (i % 4) * 128:(i % 4) * 128 + 128],
                              a["bh_sm"][:, g % 2, i, :], a["ch_sm"][:, g % 2, i, :],
                              start=True, stop=True)
            tensor.drain().then_inc(s_["st"], 1)

        def t_scb(g):
            tensor.wait_ge(s_["ss"], mk[f"sCSB{g}"])
            if g > 0:
                tensor.wait_ge(s_["sv"], mk[f"vPS{g - 1}"])
            pa, pb_ = opair(g)
            for i in range(8):
                tensor.transpose(PB[pa + i // 4][:, (i % 4) * 128:(i % 4) * 128 + 128],
                                 a["Ebuf"][:, i, :], c["c_idf32"][:])
            tensor.drain().then_inc(s_["st"], 1)

        def t_y(g):
            hg = g * 8
            tensor.wait_ge(s_["sv"], mk[f"vPS{g}"])
            if g >= 1:
                tensor.wait_ge(s_["ss"], mk[f"sYSM{g - 1}"])
            if ci > 0:
                for i in range(8):
                    tensor.matmul(PB[6][:, i * 64:(i + 1) * 64],
                                  a["bh_all"][:, prev, hg + i, :],
                                  a["xmaskE"][:, hg + i, :], start=True, stop=True)
                tensor.drain().then_inc(s_["st"], 1)
                tensor.wait_ge(s_["sv"], mk[f"vHB{g}"])
            for i in range(8):
                tensor.matmul(PB[6][:, i * 64:(i + 1) * 64], a["PSb"][:, i, :],
                              a["x_tm"][:, hg + i, :], start=True, stop=True)
            for i in range(8):
                tensor.matmul(PB[7][:, i * 64:(i + 1) * 64],
                              a["ch_sm"][:, g % 2, i, :],
                              a["Hst"][:, (hg + i) * HP:(hg + i + 1) * HP],
                              start=True, stop=True)
            tensor.drain().then_inc(s_["st"], 1)

        def t_ybq(g):
            hg = g * 8
            tensor.wait_ge(s_["sv"], mk[f"vYE1{g}"])
            for i in range(8):
                tensor.matmul(PB[7][:, i * 64:(i + 1) * 64], bh_cur[:, hg + i, :],
                              a["xqh"][:, hg + i, :], start=True, stop=True)
            tensor.drain().then_inc(s_["st"], 1)

        def t_yt(g):
            tensor.wait_ge(s_["sv"], mk[f"vYE1{g}"])
            pbf6 = PB[6][:].bitcast(BF16)
            for j in range(4):
                tensor.transpose(pbf6[:, j * 128:(j + 1) * 128],
                                 a["ybar_g"][:, j * 128:(j + 1) * 128], c["c_idbf"][:])
            tensor.drain().then_inc(s_["st"], 1)

        def t_op(g):
            tensor.wait_ge(s_["ss"], mk[f"sYSM{g}"])
            for j in range(4):
                kin = g * 4 + j
                for sx in range(2):
                    tensor.matmul(PB[sx][:, 0:512], a["ybar_sm"][:, j, :],
                                  a["woutT"][:, kin, sx * 512:(sx + 1) * 512],
                                  start=(kin == 0), stop=(kin == 15))

        def dmy(*_a, **_k):
            tensor.drain().then_inc(s_["st"], 1)
        if DBG_CUT == 1:
            t_tr = t_scs = t_scb = t_y = t_ybq = t_yt = dmy
            t_op = lambda g: None
        elif DBG_CUT == 2:
            t_y = t_ybq = t_yt = dmy
            t_op = lambda g: None
        elif DBG_CUT == 3:
            t_op = lambda g: None
        t_tr(0)
        t_scs(0)
        t_scb(0)
        t_tr(1)
        t_y(0)
        t_scs(1)
        t_ybq(0)
        t_scb(1)
        t_yt(0)
        t_tr(2)
        t_op(0)
        t_y(1)
        t_scs(2)
        t_ybq(1)
        t_scb(2)
        t_yt(1)
        t_tr(3)
        t_op(1)
        t_y(2)
        t_scs(3)
        t_ybq(2)
        t_scb(3)
        t_yt(2)
        t_y(3)
        t_ybq(3)
        t_op(2)
        t_yt(3)
        t_op(3)
        tensor.drain().then_inc(s_["st"], 1)
    b.cnt["st"] = mk["tOP"]

    @b.V
    def _(vector):
        s_ = b.sem
        vector.wait_ge(s_["sd"], mk["dU"])
        W(vector, "st", "p_tY3")
        W(vector, "ss", "p_sE3")
        vector.drain().then_inc(s_["sv"], 1)
        vector.wait_ge(s_["st"], mk["tLN"])
        vector.tensor_scalar(a["Gbuf"][:].rearrange("p h q -> p (h q)")[0:1, 512:640], PB[4][0:1, 0:128], 1.0 / DM,
                             None, OP.mult)
        vector.tensor_scalar(a["Gbuf"][:].rearrange("p h q -> p (h q)")[0:1, 640:768], PB[4][0:1, 128:256], 1.0 / DM,
                             None, OP.mult)
        vector.drain()
        vector.tensor_tensor(a["smt"][0:1, 0:128], a["Gbuf"][:].rearrange("p h q -> p (h q)")[0:1, 512:640],
                             a["Gbuf"][:].rearrange("p h q -> p (h q)")[0:1, 512:640], OP.mult)
        vector.drain()
        vector.tensor_tensor(a["Gbuf"][:].rearrange("p h q -> p (h q)")[0:1, 640:768], a["Gbuf"][:].rearrange("p h q -> p (h q)")[0:1, 640:768],
                             a["smt"][0:1, 0:128], OP.subtract)
        vector.drain().then_inc(s_["sv"], 1)
        vector.wait_ge(s_["ss"], mk["sBC"])
        srep = a["Gbuf"][:].rearrange("p h q -> p (h q)")[:, 768:1024]
        mrep = srep[:, 0:128].unsqueeze(1).broadcast_to([128, 8, 128])
        rrep = srep[:, 128:256].unsqueeze(1).broadcast_to([128, 8, 128])
        vector.tensor_tensor(a["ub32"][:], a["ub32"][:], mrep, OP.subtract)
        vector.drain()
        vector.tensor_tensor(a["xnb"][:], a["ub32"][:], rrep, OP.mult)
        vector.drain().then_inc(s_["sv"], 1)
        vector.wait_ge(s_["ss"], mk["sIP8"])
        vector.tensor_tensor(a["dt_"][:], a["sml"][:, 256:288], c["dtb"][:], OP.add)
        vector.drain()
        vector.scalar_tensor_tensor(a["smt"][:, 0:32], a["dt_"][:], -1.0, a["dt_"][:],
                                    OP.mult, OP.min)
        vector.scalar_tensor_tensor(a["smt"][:, 32:64], a["sml"][:, 288:320],
                                    -1.0, a["sml"][:, 288:320], OP.mult, OP.min)
        vector.drain().then_inc(s_["sv"], 1)
        vector.wait_ge(s_["st"], mk["tIP1"])
        vector.tensor_copy(a["zs"][:, 0:512], PB[1][:])
        vector.drain().then_inc(s_["sv"], 1)
        vector.wait_ge(s_["st"], mk["tIP2"])
        vector.tensor_copy(a["zs"][:, 512:1024], PB[0][:])
        vector.drain().then_inc(s_["sv"], 1)
        vector.wait_ge(s_["ss"], mk["sSM"])
        vector.tensor_scalar(a["smt"][:, 96:128], a["dt_"][:], 0.0, None, OP.max)
        vector.tensor_scalar(a["Gbuf"][:, 0, 0:32], a["sml"][:, 288:320], 0.0, None,
                             OP.max)
        vector.tensor_scalar(a["lam"][:], a["smt"][:, 64:96], 1.0, None, OP.add)
        vector.drain()
        vector.reciprocal(a["lam"][:], a["lam"][:])
        vector.drain()
        vector.tensor_tensor(a["dt_"][:], a["smt"][:, 96:128], a["smt"][:, 0:32],
                             OP.add)
        vector.tensor_tensor(a["adt"][:], a["Gbuf"][:, 0, 0:32], a["smt"][:, 32:64],
                             OP.add)
        vector.tensor_scalar(a["smt"][:, 96:128], a["lam"][:], -1.0, 1.0, OP.mult,
                             OP.add)
        vector.drain()
        vector.tensor_scalar(a["adt"][:], a["adt"][:], -1.0, -1e-4, OP.mult, OP.min)
        vector.tensor_tensor(a["dtlam"][:], a["dt_"][:], a["lam"][:], OP.mult)
        vector.tensor_tensor(a["dt1ml"][:], a["dt_"][:], a["smt"][:, 96:128],
                             OP.mult)
        vector.drain()
        vector.tensor_tensor(a["adt"][:], a["adt"][:], a["dt_"][:], OP.mult)
        vector.drain().then_inc(s_["sv"], 1)
        vector.wait_ge(s_["st"], mk["tCS"])
        vector.tensor_copy(a["cs_tm"][:], PB[6][:, 0:32])
        vector.tensor_tensor(a["wvec"][:], a["dtlam"][:], PB[6][:, 32:64], OP.add)
        vector.drain()
        vector.tensor_scalar(a["wvec"][:], a["wvec"][:], 1e-30, None, OP.max)
        vector.drain().then_inc(s_["sv"], 1)
        vector.wait_ge(s_["st"], mk["tIP3"])
        vector.tensor_copy(a["zs"][:, 1024:1536], PB[1][:])
        vector.drain().then_inc(s_["sv"], 1)
        vector.wait_ge(s_["st"], mk["tIP4"])
        vector.tensor_copy(a["zs"][:, 1536:2048], PB[0][:])
        vector.drain().then_inc(s_["sv"], 1)
        vector.wait_ge(s_["ss"], mk["sCS"])
        vector.tensor_tensor(a["qw"][:], a["qw"][:], a["cs_tm"][:], OP.subtract)
        vector.drain()
        vector.tensor_tensor(a["smt"][:, 0:32], a["qw"][:], PB[6][:, 64:96], OP.add)
        vector.drain().then_inc(s_["sv"], 1)
        dtr = a["dt_"][:].unsqueeze(2).broadcast_to([128, NH, NA])
        angr = a["sml"][:, 352:384].unsqueeze(1).broadcast_to([128, NH, NA])
        vector.tensor_tensor(a["Gbuf"][:], dtr, angr, OP.mult)
        vector.drain().then_inc(s_["sv"], 1)
        vector.tensor_tensor(a["ub32"][:, 0, :], a["sml"][:, 0:128],
                             a["sml"][:, 0:128], OP.mult)
        vector.tensor_tensor(a["ub32"][:, 1, :], a["sml"][:, 128:256],
                             a["sml"][:, 128:256], OP.mult)
        vector.drain()
        vector.tensor_reduce(a["acol"][:, 0:1], a["ub32"][:, 0, :], AX.X, OP.add)
        vector.drain().then_inc(s_["sv"], 1)
        vector.tensor_reduce(a["acol"][:, 2:3], a["ub32"][:, 1, :], AX.X, OP.add)
        vector.drain().then_inc(s_["sv"], 1)
        Gflat = a["Gbuf"][:].rearrange("p h q -> p (h q)")
        bnbf = a["ybar_sm"][:, 0, :]
        cnbf = a["ybar_sm"][:, 1, :]

        def rng_red(j):
            gsl = Gflat[:, j * 512:(j + 1) * 512]
            vector.wait_ge(s_["st"], mk["tTH0" if j == 0 else "tTH1"])
            vector.tensor_tensor(a["carry"][0:1, j * 512:(j + 1) * 512],
                                 a["carry"][0:1, j * 512:(j + 1) * 512],
                                 PB[7][0:1, :], OP.add)
            vector.drain().then_inc(s_["sv"], 1)
            vector.tensor_scalar(gsl, PB[2 + j][:], 1.0 / (2 * PI), MAGIC, OP.mult,
                                 OP.add)
            vector.tensor_scalar(gsl, gsl, MAGIC, 2 * PI, OP.subtract, OP.mult)
            vector.tensor_tensor(gsl, PB[2 + j][:], gsl, OP.subtract)
            vector.drain().then_inc(s_["sv"], 1)

        def cos_prep(j):
            gsl = Gflat[:, j * 512:(j + 1) * 512]
            vector.wait_ge(s_["ss"], mk[f"sS{j}"])
            vector.tensor_scalar(gsl, PB[2 + j][:], 1.0 / (2 * PI), 0.25, OP.mult,
                                 OP.add)
            vector.tensor_scalar(gsl, gsl, MAGIC, MAGIC, OP.add, OP.subtract)
            vector.tensor_scalar(gsl, gsl, 2 * PI, PI / 2, OP.mult, OP.subtract)
            vector.tensor_tensor(gsl, PB[2 + j][:], gsl, OP.subtract)
            vector.drain().then_inc(s_["sv"], 1)

        def bc_norm(src0, nwkey, biaskey, dstb, acl):
            vector.wait_ge(s_["ss"], mk["sRB" if src0 == 0 else "sRC"])
            if src0 == 0:
                W(vector, "st", "p_tOP")
                dst = a["bh_all"][:, cur]
            else:
                dst = a["ch_all"][:]
            vector.scalar_tensor_tensor(dstb, a["sml"][:, src0:src0 + 128],
                                        acl, c[nwkey][:], OP.mult, OP.mult)
            vector.drain()
            nrep = dstb.unsqueeze(1).broadcast_to([128, NH, NS])
            brep = a[biaskey][:].rearrange("p (h n) -> p h n", h=NH)
            vector.tensor_tensor(dst, nrep, brep, OP.add)
            vector.drain()
            return dst

        def bc_rope(dst, mark_inc):
            vector.wait_ge(s_["ss"], mk["sC1"])
            v1 = dst[:, :, 0:NA]
            v2 = dst[:, :, NA:2 * NA]
            sinr = a["sin_t"][:]
            cosr = a["cos_t"][:]
            rt0 = a["dgbuf"][:].rearrange("p i t -> p (i t)").rearrange(
                "p (h q) -> p h q", h=NH)
            rt1 = a["Ebuf"][:].rearrange("p i t -> p (i t)").bitcast(BF16)[
                :, 0:1024].rearrange("p (h q) -> p h q", h=NH)
            vector.tensor_tensor(rt0, v2, sinr, OP.mult)
            vector.tensor_tensor(v2, v2, cosr, OP.mult)
            vector.tensor_tensor(rt1, v1, sinr, OP.mult)
            vector.tensor_tensor(v2, rt1, v2, OP.add)
            vector.tensor_tensor(v1, v1, cosr, OP.mult)
            vector.tensor_tensor(v1, v1, rt0, OP.subtract)
            vector.drain().then_inc(s_["sv"], 1)

        rng_red(0)
        dstB = bc_norm(0, "bnw", "bias_b", bnbf, a["acol"][:, 0:1])
        vector.drain().then_inc(s_["sv"], 1)
        cos_prep(0)
        dstC = bc_norm(128, "cnw", "bias_c", cnbf, a["acol"][:, 2:3])
        vector.drain().then_inc(s_["sv"], 1)
        rng_red(1)
        cos_prep(1)
        vector.wait_ge(s_["ss"], mk["sQH"])
        vector.wait_ge(s_["ss"], mk["sIP7"])
        qrep = a["qh_e"][:].unsqueeze(2).broadcast_to([128, NH, HP])
        vector.tensor_tensor(a["xqh"][:], a["x_tm"][:], qrep, OP.mult)
        if ci > 0:
            drep0 = a["d10bf"][:].unsqueeze(2).broadcast_to([128, NH, HP])
            vector.tensor_tensor(a["xmaskE"][:], a["xmaskE"][:], drep0, OP.mult)
        vector.drain().then_inc(s_["sv"], 1)
        bc_rope(dstB, None)
        bc_rope(dstC, None)
        for g in range(4):
            hg = g * 8
            if DBG_CUT == 1:
                nmk = 3 if ci == 0 else 4
                for _ in range(nmk):
                    vector.drain().then_inc(s_["sv"], 1)
                continue
            irep = c["c_idbf"][:].unsqueeze(1).broadcast_to([128, 8, 128])
            drep8 = a["dtlam"][:, hg:hg + 8].unsqueeze(2).broadcast_to([128, 8, 128])
            vector.tensor_tensor(a["dgbuf"][:], irep, drep8, OP.mult)
            vector.drain()
            vector.wait_ge(s_["ss"], mk[f"sE{g}"])
            pp = a["PSb"][:].rearrange("p i t -> p (i t)")
            dg = a["dgbuf"][:].rearrange("p i t -> p (i t)")
            vector.tensor_tensor(pp[:, 0:512], pp[:, 0:512], dg[:, 0:512], OP.add)
            vector.tensor_tensor(pp[:, 512:1024], pp[:, 512:1024], dg[:, 512:1024],
                                 OP.add)
            vector.drain()
            pa, pb_ = pair(g)
            vector.tensor_tensor(pp[:, 0:512], PB[pa][:], pp[:, 0:512], OP.mult)
            vector.tensor_tensor(pp[:, 512:1024], PB[pb_][:], pp[:, 512:1024],
                                 OP.mult)
            vector.drain().then_inc(s_["sv"], 1)
            if DBG_CUT == 2:
                for _ in range(2 if ci == 0 else 3):
                    vector.drain().then_inc(s_["sv"], 1)
                continue
            if ci > 0:
                vector.wait_ge(s_["st"], mk[f"tBD{g}"])
                hsl0 = a["Hst"][:, hg * HP:(hg + 8) * HP]
                vector.tensor_tensor(hsl0, hsl0, PB[6][:], OP.add)
                vector.drain().then_inc(s_["sv"], 1)
            vector.wait_ge(s_["st"], mk[f"tY{g}"])
            if g >= 1:
                vector.wait_ge(s_["st"], mk[f"tYT{g - 1}"])
            e13 = a["Gbuf"][:].rearrange("p h q -> p (h q)")[:, 0:512].rearrange(
                "p (i q) -> p i q", i=8)
            yb3 = a["ybar_g"][:].rearrange("p (i q) -> p i q", i=8)
            pb6_3 = PB[6][:].rearrange("p (i q) -> p i q", i=8)
            pb7_3 = PB[7][:].rearrange("p (i q) -> p i q", i=8)
            drp = c["drepbf"][:, hg:hg + 8].unsqueeze(2).broadcast_to([128, 8, HP])
            xsl = a["x_tm"][:, hg:hg + 8, :]
            eint3 = a["eint"][:]
            arep = a["alpha"][:, hg:hg + 8].unsqueeze(2).broadcast_to([128, 8, HP])
            vector.tensor_tensor(eint3, pb7_3, arep, OP.mult)
            vector.tensor_tensor(yb3, xsl, drp, OP.mult)
            vector.drain()
            vector.tensor_tensor(e13, pb6_3, yb3, OP.add)
            vector.tensor_tensor(e13, e13, eint3, OP.add)
            zs3 = a["zs"][:, hg * HP:(hg + 8) * HP].rearrange("p (i q) -> p i q", i=8)
            vector.tensor_tensor(yb3, e13, zs3, OP.mult)
            vector.drain().then_inc(s_["sv"], 1)
            vector.wait_ge(s_["st"], mk[f"tYB{g}"])
            dcr = a["decay"][:, hg:hg + 8].unsqueeze(2).broadcast_to([128, 8, HP])
            hsl = a["Hst"][:, hg * HP:(hg + 8) * HP].rearrange("p (i q) -> p i q", i=8)
            vector.tensor_tensor(eint3, hsl, dcr, OP.mult)
            vector.drain()
            vector.tensor_tensor(hsl, eint3, pb7_3, OP.add)
            vector.drain().then_inc(s_["sv"], 1)
        vector.wait_ge(s_["st"], mk["tOP"])
        s_ev = a["ub32"][:].rearrange("p k t -> p (k t)")
        vector.tensor_copy(s_ev[:, 0:512], PB[0][:])
        vector.tensor_copy(s_ev[:, 512:1024], PB[1][:])
        vector.tensor_tensor(
            a["xmaskE"][:].rearrange("p h q -> p (h q)"),
            a["x_tm"][:].rearrange("p h q -> p (h q)"),
            e127bf.broadcast_to([128, NH * HP]), OP.mult)
        vector.drain().then_inc(s_["sv"], 1)
    b.cnt["sv"] = mk["vSEV"]

    @b.S
    def _(scalar):
        s_ = b.sem
        scalar.wait_ge(s_["sv"], mk["vLN1"])
        scalar.copy(a["PSb"][:].rearrange("p k t -> p (k t)"),
                    a["ub32"][:].rearrange("p k t -> p (k t)"))
        scalar.activation(a["xnb"][:].rearrange("p k t -> p (k t)"),
                          a["ub32"][:].rearrange("p k t -> p (k t)"), AF.Square)
        scalar.drain().then_inc(s_["ss"], 1)
        scalar.wait_ge(s_["sv"], mk["vLN2"])
        scalar.activation(a["smt"][0:1, 0:128], a["Gbuf"][:].rearrange("p h q -> p (h q)")[0:1, 640:768], AF.Ln,
                          bias=1e-5)
        scalar.activation(a["Gbuf"][:].rearrange("p h q -> p (h q)")[0:1, 640:768], a["smt"][0:1, 0:128], AF.Exp,
                          scale=-0.5)
        scalar.drain().then_inc(s_["ss"], 1)
        scalar.wait_ge(s_["st"], mk["tBC"])
        scalar.copy(a["Gbuf"][:].rearrange("p h q -> p (h q)")[:, 768:1024],
                    PB[5][:, 0:256])
        scalar.drain().then_inc(s_["ss"], 1)
        scalar.wait_ge(s_["st"], mk["tIP0"])
        scalar.copy(a["sml"][:], PB[0][:, 0:384])
        scalar.drain().then_inc(s_["ss"], 1)
        scalar.wait_ge(s_["sv"], mk["vSM1"])
        scalar.activation(a["smt"][:, 64:128], a["smt"][:, 0:64], AF.Exp)
        scalar.activation(a["smt"][:, 0:64], a["smt"][:, 64:128], AF.Ln, bias=1.0)
        scalar.activation(a["smt"][:, 64:96], a["sml"][:, 320:352], AF.Exp,
                          scale=-1.0)
        scalar.drain().then_inc(s_["ss"], 1)
        scalar.wait_ge(s_["sv"], mk["vCS1"])
        scalar.activation(a["qw"][:], a["wvec"][:], AF.Ln)
        scalar.activation(a["alpha"][:], a["cs_tm"][:], AF.Exp)
        scalar.activation(a["decay"][:], PB[6][:, 64:96], AF.Exp)
        scalar.drain().then_inc(s_["ss"], 1)
        scalar.wait_ge(s_["sv"], mk["vQW"])
        scalar.activation(a["qh_e"][:], a["smt"][:, 0:32], AF.Exp)
        scalar.copy(a["d10bf"][:], PB[6][:, 96:128])
        scalar.drain().then_inc(s_["ss"], 1)
        scalar.wait_ge(s_["sv"], mk["vRMB"])
        scalar.activation(a["acol"][:, 1:2], a["acol"][:, 0:1], AF.Ln,
                          scale=1.0 / NS, bias=1e-5)
        scalar.activation(a["acol"][:, 0:1], a["acol"][:, 1:2], AF.Exp,
                          scale=-0.5)
        scalar.drain().then_inc(s_["ss"], 1)
        scalar.wait_ge(s_["sv"], mk["vRMC"])
        scalar.activation(a["acol"][:, 3:4], a["acol"][:, 2:3], AF.Ln,
                          scale=1.0 / NS, bias=1e-5)
        scalar.activation(a["acol"][:, 2:3], a["acol"][:, 3:4], AF.Exp,
                          scale=-0.5)
        scalar.drain().then_inc(s_["ss"], 1)
        xf = a["x_tm"][:].rearrange("p h q -> p (h q)")
        scalar.wait_ge(s_["st"], mk["tIP5"])
        scalar.copy(xf[:, 0:512], PB[1][:])
        scalar.drain().then_inc(s_["ss"], 1)
        scalar.wait_ge(s_["st"], mk["tIP6"])
        scalar.copy(xf[:, 512:1024], PB[0][:])
        scalar.drain().then_inc(s_["ss"], 1)
        Gf = a["Gbuf"][:].rearrange("p h q -> p (h q)")
        sf = a["sin_t"][:].rearrange("p h q -> p (h q)")
        cf = a["cos_t"][:].rearrange("p h q -> p (h q)")
        scalar.wait_ge(s_["sv"], mk["vRS0"])
        scalar.activation(sf[:, 0:512], Gf[:, 0:512], AF.Sin)
        scalar.drain().then_inc(s_["ss"], 1)
        scalar.wait_ge(s_["sv"], mk["vRC0"])
        scalar.activation(cf[:, 0:512], Gf[:, 0:512], AF.Sin)
        scalar.drain().then_inc(s_["ss"], 1)
        scalar.wait_ge(s_["st"], mk["tIP7"])
        scalar.copy(xf[:, 1024:1536], PB[1][:])
        scalar.drain().then_inc(s_["ss"], 1)
        scalar.wait_ge(s_["st"], mk["tIP8"])
        scalar.copy(xf[:, 1536:2048], PB[0][:])
        scalar.drain().then_inc(s_["ss"], 1)
        scalar.wait_ge(s_["sv"], mk["vRS1"])
        scalar.activation(sf[:, 512:1024], Gf[:, 512:1024], AF.Sin)
        scalar.drain().then_inc(s_["ss"], 1)
        scalar.wait_ge(s_["sv"], mk["vRC1"])
        scalar.activation(cf[:, 512:1024], Gf[:, 512:1024], AF.Sin)
        scalar.drain().then_inc(s_["ss"], 1)
        scalar.wait_ge(s_["sv"], mk["vIP3"])
        for jz in range(4):
            scalar.activation(a["zs"][:, jz * 512:(jz + 1) * 512],
                              a["zs"][:, jz * 512:(jz + 1) * 512], AF.Silu)
        scalar.drain().then_inc(s_["ss"], 1)

        def s_csb(g):
            for i in range(8):
                scalar.activation(a["Ebuf"][:, i, :], c["c_maskT"][:], AF.Identity,
                                  scale=1.0,
                                  bias=a["cs_tm"][:, g * 8 + i:g * 8 + i + 1])
            scalar.drain().then_inc(s_["ss"], 1)

        def s_tre(g):
            scalar.wait_ge(s_["st"], mk[f"tTR{g}"])
            if g >= 2:
                scalar.wait_ge(s_["st"], mk[f"tSCs{g - 2}"])
                scalar.wait_ge(s_["st"], mk[f"tY{g - 2}"])
            pa, pb_ = pair(g)
            scalar.copy(a["bh_sm"][:, g % 2].rearrange("p i t -> p (i t)"),
                        PB[pa][:].bitcast(BF16))
            scalar.copy(a["ch_sm"][:, g % 2].rearrange("p i t -> p (i t)"),
                        PB[pb_][:].bitcast(BF16))
            scalar.drain().then_inc(s_["ss"], 1)

        def s_dmy(n):
            for _ in range(n):
                scalar.drain().then_inc(s_["ss"], 1)
        if DBG_CUT == 1:
            s_csb = s_tre = lambda g: s_dmy(1)
        scalar.wait_ge(s_["sv"], mk["vCB"])
        s_csb(0)
        s_tre(0)
        for g in range(4):
            hg = g * 8
            if DBG_CUT == 1:
                s_dmy(3 if g == 0 else (4 if g < 3 else 2))
                continue
            scalar.wait_ge(s_["st"], mk[f"tSCb{g}"])
            pa, pb_ = opair(g)
            for i in range(8):
                scalar.activation(a["PSb"][:, i, :],
                                  PB[pa + i // 4][:, (i % 4) * 128:(i % 4) * 128 + 128],
                                  AF.Exp, bias=a["qw"][:, hg + i:hg + i + 1])
            scalar.drain().then_inc(s_["ss"], 1)
            if g < 3:
                s_csb(g + 1)
            if g >= 1:
                if DBG_CUT == 2:
                    s_dmy(1)
                else:
                    scalar.wait_ge(s_["st"], mk[f"tYT{g - 1}"])
                    scalar.copy(a["ybar_sm"][:].rearrange("p j t -> p (j t)"),
                                PB[6][:].bitcast(BF16)[:, 0:512])
                    scalar.drain().then_inc(s_["ss"], 1)
            if g < 3:
                s_tre(g + 1)
        if DBG_CUT in (1, 2):
            s_dmy(1)
        else:
            scalar.wait_ge(s_["st"], mk["tYT3"])
            scalar.copy(a["ybar_sm"][:].rearrange("p j t -> p (j t)"),
                        PB[6][:].bitcast(BF16)[:, 0:512])
            scalar.drain().then_inc(s_["ss"], 1)
    b.cnt["ss"] = mk["sYSM3"]

    b.prev_a_marks = {"p_tY3": mk["tY3"], "p_sE3": mk["sE3"], "p_tOP": mk["tOP"]}

    b.tap("xnb", a["xnb"][:].rearrange("p k t -> p (k t)"), bi, ci)
    b.tap("xtm", a["x_tm"][:].rearrange("p h q -> p (h q)"), bi, ci)
    b.tap("zs", a["zs"][:], bi, ci)
    b.tap("sml", a["sml"][:], bi, ci)
    for nm in ("dt_", "adt", "lam", "wvec", "qw", "alpha", "qh_e", "decay",
               "cs_tm", "dtlam", "dt1ml"):
        b.tap(nm, a[nm][:], bi, ci)
    b.tap("sin", a["sin_t"][:].rearrange("p h q -> p (h q)"), bi, ci)
    b.tap("cos", a["cos_t"][:].rearrange("p h q -> p (h q)"), bi, ci)
    b.tap("bh", a["bh_all"][:, cur].rearrange("p h q -> p (h q)"), bi, ci)
    b.tap("ch", a["ch_all"][:].rearrange("p h q -> p (h q)"), bi, ci)
    b.tap("hst", a["Hst"][:], bi, ci)
    b.tap("sev", a["ub32"][:].rearrange("p k t -> p (k t)"), bi, ci)
    b.tap("psb", a["PSb"][:].rearrange("p i t -> p (i t)"), bi, ci)
    b.tap("ybar3", a["ybar_g"][:], bi, ci)
    b.tap("eint", a["eint"][:].rearrange("p i q -> p (i q)"), bi, ci)
    b.tap("carry", a["carry"][:], bi, ci)


def _chunk_b1(b, d, c, p, PB, bi, ci):
    mk = {}

    def plan(ch, names, step=1):
        v = b.cnt[ch]
        for n in names:
            v += step
            mk[n] = v

    plan("sv", ["vM8", "vSC", "vSR", "vXQ"] + [f"vGU{sl}" for sl in range(8)]
         + ["vGM"])
    plan("st", ["tM8", "tSR", "tSB"] + [x for sl in range(8)
                                        for x in (f"tG{sl}", f"tU{sl}")])
    plan("ss", [f"sG{sl}" for sl in range(8)])
    plan("sd", ["dXNF", "dGU"], step=16)

    prev_dGU = mk["dXNF"] - 16
    buf = (bi * NCH + ci) % 2
    prev_vXQ = getattr(b, "prev_b1_vXQ", 0)
    xnf = p["xnf"][:, buf]

    @b.D
    def _(sync):
        b.wait(sync, "sd", prev_dGU)
        sync.wait_ge(b.sem["sv"], prev_vXQ)
        sync.dma_start(xnf.rearrange("p k t -> p (k t)"),
                       d["sc_xn"][bi, ci]).then_inc(b.sem["sd"], 16)
        sync.wait_ge(b.sem["sv"], mk["vGU7"])
        sync.dma_start(d["sc_gu"][bi, ci], p["gu_b"][:]).then_inc(b.sem["sd"], 16)
    b.cnt["sd"] = mk["dGU"]
    b.prev_b1_vXQ = mk["vXQ"]

    @b.T
    def _(tensor):
        s_ = b.sem
        tensor.wait_ge(s_["sv"], mk["vM8"])
        tensor.transpose(PB[4][:, 0:128], p["m8"][:], c["c_idf32"][:])
        tensor.drain().then_inc(s_["st"], 1)
        tensor.wait_ge(s_["sv"], mk["vSC"])
        tensor.transpose(PB[4][0:1, 128:256], p["scol"][:, 1:2], c["c_idf32"][:])
        tensor.drain().then_inc(s_["st"], 1)
        tensor.wait_ge(s_["sv"], mk["vSR"])
        tensor.matmul(PB[5][:, 0:128], c["c_ones"][0:1, :], p["srow"][0:1, :],
                      start=True, stop=True)
        tensor.drain().then_inc(s_["st"], 1)
        tensor.wait_ge(s_["sv"], mk["vXQ"])
        for sl in range(8):
            if sl >= 2:
                tensor.wait_ge(s_["ss"], mk[f"sG{sl - 2}"])
            for k in range(8):
                tensor.matmul(PB[sl % 2][:], p["xq_fm"][:, k, :],
                              p["wgq"][:, k, sl * 512:(sl + 1) * 512],
                              start=(k == 0), stop=(k == 7))
            tensor.drain().then_inc(s_["st"], 1)
            if sl >= 2:
                tensor.wait_ge(s_["sv"], mk[f"vGU{sl - 2}"])
            for k in range(8):
                tensor.matmul(PB[2 + sl % 2][:], p["xq_fm"][:, k, :],
                              p["wuq"][:, k, sl * 512:(sl + 1) * 512],
                              start=(k == 0), stop=(k == 7))
            tensor.drain().then_inc(s_["st"], 1)
    b.cnt["st"] = mk["tU7"]
    prev_tU7 = getattr(b, "prev_b1_tU7", 0)
    b.prev_b1_tU7 = mk["tU7"]

    @b.V
    def _(vector):
        s_ = b.sem
        vector.wait_ge(s_["sd"], mk["dXNF"])
        vector.scalar_tensor_tensor(p["m8"][:], xnf[:, 0, :], -1.0,
                                    xnf[:, 0, :], OP.mult, OP.max)
        for k in range(1, 8):
            vector.scalar_tensor_tensor(p["mt"][:], xnf[:, k, :], -1.0,
                                        xnf[:, k, :], OP.mult, OP.max)
            vector.tensor_tensor(p["m8"][:], p["m8"][:], p["mt"][:], OP.max)
        vector.drain().then_inc(s_["sv"], 1)
        vector.wait_ge(s_["st"], mk["tM8"])
        vector.tensor_reduce(p["scol"][:, 0:1], PB[4][:, 0:128], AX.X, OP.max)
        vector.drain()
        vector.tensor_scalar(p["scol"][:, 0:1], p["scol"][:, 0:1], 1e-5, None,
                             OP.max)
        vector.drain()
        vector.reciprocal(p["scol"][:, 1:2], p["scol"][:, 0:1])
        vector.tensor_scalar(p["scol"][:, 2:3], p["scol"][:, 0:1],
                             c["wsc"][:, 3:4], 1.0 / 127.0, OP.mult, OP.mult)
        vector.tensor_scalar(p["scol"][:, 3:4], p["scol"][:, 0:1],
                             c["wsc"][:, 4:5], 1.0 / 127.0, OP.mult, OP.mult)
        vector.drain()
        vector.tensor_scalar(p["scol"][:, 1:2], p["scol"][:, 1:2], 127.0, None,
                             OP.mult)
        vector.drain().then_inc(s_["sv"], 1)
        vector.wait_ge(s_["st"], mk["tSR"])
        vector.tensor_copy(p["srow"][0:1, :], PB[4][0:1, 128:256])
        vector.drain().then_inc(s_["sv"], 1)
        vector.wait_ge(s_["st"], mk["tSB"])
        vector.wait_ge(s_["st"], prev_tU7)
        for k in range(8):
            vector.tensor_tensor(p["mt"][:], xnf[:, k, :], PB[5][:, 0:128],
                                 OP.mult)
            vector.tensor_scalar(p["xq_fm"][:, k, :], p["mt"][:], MAGIC, MAGIC,
                                 OP.add, OP.subtract)
        vector.drain().then_inc(s_["sv"], 1)
        for sl in range(8):
            vector.wait_ge(s_["st"], mk[f"tU{sl}"])
            vector.wait_ge(s_["ss"], mk[f"sG{sl}"])
            vector.scalar_tensor_tensor(
                p["gu_b"][:, sl * 512:(sl + 1) * 512], PB[2 + sl % 2][:],
                p["scol"][:, 3:4], p["gu_b"][:, sl * 512:(sl + 1) * 512],
                OP.mult, OP.mult)
            vector.drain().then_inc(s_["sv"], 1)
        vector.tensor_reduce(c["gumax"][:, bi * NCH + ci:bi * NCH + ci + 1],
                             p["gu_b"][:], AX.X, OP.max, apply_absolute_value=True)
        vector.drain().then_inc(s_["sv"], 1)
    b.cnt["sv"] = mk["vGM"]

    @b.S
    def _(scalar):
        s_ = b.sem
        scalar.wait_ge(s_["sd"], prev_dGU)
        for sl in range(8):
            scalar.wait_ge(s_["st"], mk[f"tG{sl}"])
            scalar.activation(p["gu_b"][:, sl * 512:(sl + 1) * 512],
                              PB[sl % 2][:], AF.Silu, scale=p["scol"][:, 2:3])
            scalar.drain().then_inc(s_["ss"], 1)
    b.cnt["ss"] = mk["sG7"]

    b.tap("xqfm", p["xq_fm"][:].rearrange("p k t -> p (k t)"), bi, ci)
    b.tap("gu", p["gu_b"][:], bi, ci)
    b.tap("b1scol", p["scol"][:], bi, ci)


def _chunk_b2(b, d, c, q, PB, bi, ci):
    mk = {}

    def plan(ch, names, step=1):
        v = b.cnt[ch]
        for n in names:
            v += step
            mk[n] = v

    plan("sv", ["vGQ"] + [f"vTQ{g4}" for g4 in range(4)] + ["vOB"])
    plan("st", [f"tTQ{g4}" for g4 in range(4)] + ["tD0", "tD1"])
    plan("ss", ["sD0", "sD1"])
    mk["dIN"] = b.cnt["sd"] + 48
    mk["dOUT"] = b.cnt["sd"] + 64

    buf = (bi * NCH + ci) % 2
    prev_vGQ = getattr(b, "prev_b2_vGQ", 0)
    prev_vOB = getattr(b, "prev_b2_vOB", 0)
    gu_tm = q["gu_tm"][:, buf]
    u_b = q["u_b"][:, buf]
    s_b = q["s_b"][:, buf]

    @b.D
    def _(sync):
        b.wait(sync, "sd", mk["dIN"] - 48)
        sync.wait_ge(b.sem["sv"], prev_vGQ)
        sync.dma_start(gu_tm, d["sc_gu"][bi, ci]).then_inc(b.sem["sd"], 16)
        sync.dma_start(u_b, d["u_tm"][bi, ci * 128:(ci + 1) * 128, :]
                       ).then_inc(b.sem["sd"], 16)
        sync.dma_start(s_b, d["sc_s"][bi, ci]).then_inc(b.sem["sd"], 16)
        sync.wait_ge(b.sem["sv"], mk["vOB"])
        sync.dma_start(d["out"][bi, ci * 128:(ci + 1) * 128, :], q["o_b"][:]
                       ).then_inc(b.sem["sd"], 16)
    b.cnt["sd"] = mk["dOUT"]
    b.prev_b2_vGQ = mk["vGQ"]
    b.prev_b2_vOB = mk["vOB"]

    @b.T
    def _(tensor):
        s_ = b.sem
        tensor.wait_ge(s_["sv"], mk["vGQ"])
        for g4 in range(4):
            if g4 >= 2:
                tensor.wait_ge(s_["sv"], mk[f"vTQ{g4 - 2}"])
            pbf = PB[4 + g4 % 2][:].bitcast(BF16)
            for i in range(8):
                tensor.transpose(
                    pbf[:, i * 128:(i + 1) * 128],
                    q["guq_tm"][:, (g4 * 8 + i) * 128:(g4 * 8 + i + 1) * 128],
                    c["c_idbf"][:])
            tensor.drain().then_inc(s_["st"], 1)
        tensor.wait_ge(s_["sv"], mk["vTQ3"])
        for sl in range(2):
            for k in range(32):
                tensor.matmul(PB[sl][:], q["guq_fm"][:, k, :],
                              q["wdq"][:, k, sl * 512:(sl + 1) * 512],
                              start=(k == 0), stop=(k == 31))
            tensor.drain().then_inc(s_["st"], 1)
    b.cnt["st"] = mk["tD1"]

    @b.V
    def _(vector):
        s_ = b.sem
        vector.wait_ge(s_["sd"], mk["dIN"])
        vector.tensor_scalar(q["scol2"][:, 0:1],
                             c["gumax"][:, bi * NCH + ci:bi * NCH + ci + 1],
                             1e-5, None, OP.max)
        vector.drain()
        vector.reciprocal(q["scol2"][:, 1:2], q["scol2"][:, 0:1])
        vector.tensor_scalar(q["scol2"][:, 2:3], q["scol2"][:, 0:1],
                             c["wsc"][:, 5:6], 1.0 / 127.0, OP.mult, OP.mult)
        vector.drain()
        vector.tensor_scalar(q["scol2"][:, 1:2], q["scol2"][:, 1:2], 127.0, None,
                             OP.mult)
        vector.drain()
        vector.tensor_tensor(q["qtmp"][:], gu_tm,
                             q["scol2"][:, 1:2].broadcast_to([128, FI]), OP.mult)
        vector.tensor_scalar(q["qtmp"][:], q["qtmp"][:], MAGIC, MAGIC, OP.add,
                             OP.subtract)
        vector.tensor_copy(q["guq_tm"][:], q["qtmp"][:])
        vector.drain().then_inc(s_["sv"], 1)
        for g4 in range(4):
            vector.wait_ge(s_["st"], mk[f"tTQ{g4}"])
            vector.tensor_copy(
                q["guq_fm"][:].rearrange("p k t -> p (k t)")[
                    :, g4 * 1024:(g4 + 1) * 1024], PB[4 + g4 % 2][:].bitcast(BF16))
            vector.drain().then_inc(s_["sv"], 1)
        vector.wait_ge(s_["ss"], mk["sD1"])
        b.wait(vector, "sd", mk["dOUT"] - 64)
        vector.tensor_tensor(q["o_b"][:], s_b, q["g_b"][:], OP.mult)
        vector.drain()
        vector.tensor_tensor(q["o_b"][:], q["o_b"][:], u_b, OP.add)
        vector.drain().then_inc(s_["sv"], 1)
    b.cnt["sv"] = mk["vOB"]

    @b.S
    def _(scalar):
        s_ = b.sem
        for sl in range(2):
            scalar.wait_ge(s_["st"], mk[f"tD{sl}"])
            scalar.activation(q["g_b"][:, sl * 512:(sl + 1) * 512], PB[sl][:],
                              AF.Silu, scale=q["scol2"][:, 2:3])
            scalar.drain().then_inc(s_["ss"], 1)
    b.cnt["ss"] = mk["sD1"]

    b.tap("guq", q["guq_tm"][:], bi, ci)
    b.tap("b2scol", q["scol2"][:], bi, ci)
    b.tap("gutm", gu_tm, bi, ci)
    b.tap("gb", q["g_b"][:], bi, ci)
    b.tap("ob", q["o_b"][:], bi, ci)


def _host_inputs(inputs, core):
    f = np.float32
    u = np.asarray(inputs["L0_attn_u"], f)[core * NB:(core + 1) * NB]
    ins = {
        "u_tm": np.ascontiguousarray(u),
        "u_fm": np.ascontiguousarray(u.transpose(0, 2, 1)),
        "w_inT": np.ascontiguousarray(np.asarray(inputs["L0_attn_W_in"], f).T),
        "w_outT": np.ascontiguousarray(np.asarray(inputs["L0_attn_W_out"], f).T),
        "w_gT": np.ascontiguousarray(np.asarray(inputs["L0_ffn_w_gate"], f).T),
        "w_uT": np.ascontiguousarray(np.asarray(inputs["L0_ffn_w_up"], f).T),
        "w_dT": np.ascontiguousarray(np.asarray(inputs["L0_ffn_w_down"], f).T),
        "dtb_rep": np.ascontiguousarray(
            np.broadcast_to(np.asarray(inputs["L0_attn_dt_bias"], f), (128, NH))),
        "bbias_row": np.asarray(inputs["L0_attn_B_bias"], f).reshape(1, NH * NS),
        "cbias_row": np.asarray(inputs["L0_attn_C_bias"], f).reshape(1, NH * NS),
        "bnw_rep": np.ascontiguousarray(
            np.broadcast_to(np.asarray(inputs["L0_attn_B_norm_w"], f), (128, NS))),
        "cnw_rep": np.ascontiguousarray(
            np.broadcast_to(np.asarray(inputs["L0_attn_C_norm_w"], f), (128, NS))),
        "d_rep": np.ascontiguousarray(
            np.broadcast_to(np.asarray(inputs["L0_attn_D_param"], f), (128, NH))),
    }
    return ins


def _host_consts():
    f = np.float32
    idx = np.arange(128)
    L = (idx[:, None] <= idx[None, :]).astype(f)
    shift = (idx[:, None] == idx[None, :] + 1).astype(f)
    mask = np.where(idx[None, :] > idx[:, None], 0.0, NEG).astype(f)
    ones = np.ones((128, 128), f)
    e0 = np.zeros((128, 128), f)
    e0[0, :] = 1.0
    ident = np.eye(128, dtype=f)
    cols = np.zeros((128, 4), f)
    cols[:, 0] = 1.0
    cols[127, 1] = 1.0
    cols[:, 2] = 1e-5
    colsbf = np.zeros((128, 2), ml_dtypes.bfloat16)
    colsbf[:, 0] = 1.0
    colsbf[127, 1] = 1.0
    return {
        "c_L": L, "c_shift": shift, "c_mask": mask,
        "c_maskT": np.ascontiguousarray(mask.T), "c_ones": ones, "c_e0": e0,
        "c_idf32": ident, "c_idbf": ident.astype(ml_dtypes.bfloat16),
        "c_cols": cols, "c_colsbf": colsbf,
    }


_CACHE = {}


def kernel(**inputs):
    if "nc" not in _CACHE:
        nc = bass.Bass("TRN2", target_bir_lowering=False, debug=False)
        build(nc)
        _CACHE["nc"] = nc
    nc = _CACHE["nc"]
    consts = _host_consts()
    in_maps = []
    for core in range(8):
        m = _host_inputs(inputs, core)
        m.update(consts)
        in_maps.append(m)
    res = run_bass_kernel_spmd(nc, in_maps, core_ids=list(range(8)))
    out = np.concatenate([np.asarray(res.results[cc]["out"]) for cc in range(8)],
                         axis=0)
    return out.astype(np.float32)
